# version 18
# baseline (speedup 1.0000x reference)
import numpy as np
import ml_dtypes
import concourse.bass as bass
import concourse.mybir as mybir
from concourse.bass_utils import run_bass_kernel_spmd

F32 = mybir.dt.float32
BF16 = mybir.dt.bfloat16
AF = mybir.ActivationFunctionType
ALU = mybir.AluOpType
AX = mybir.AxisListType

T = 4096
D = 1024
GH, GD, CH = 4, 256, 256
NCH = T // CH
NP, QT, NQT = 8, 512, 8
EXPB = -12.0
NE = 8
KB = 1024


NDMAS = 16


class Sch:
    def __init__(self, nc):
        self.nc = nc
        self.queues = ["sp", "gp", "pe", "act", "dve"]
        semnames = ["pe", "act", "dve"] + \
            ["spd%d" % i for i in range(NDMAS)] + ["gpd%d" % i for i in range(NDMAS)]
        self.sem = {s: nc.alloc_semaphore("sem_" + s) for s in semnames}
        self.cnt = {s: 0 for s in semnames}
        self.ops = {q: [] for q in self.queues}
        self.res = {}
        self.clock = {q: {s: 0 for s in semnames} for q in self.queues}
        self.dmaidx = {"sp": 0, "gp": 0}
        self.cur_fence = None
        self.nfence = 0

    def op(self, e, fn, r=(), w=(), dma=False):
        need = {}
        if self.cur_fence is not None:
            r = list(r) + [self.cur_fence]
        for name in r:
            rec = self.res.get(name)
            if rec and rec["w"] is not None:
                we, wc = rec["w"]
                if need.get(we, 0) < wc:
                    need[we] = wc
        for name in w:
            rec = self.res.get(name)
            if rec:
                if rec["w"] is not None:
                    we, wc = rec["w"]
                    if need.get(we, 0) < wc:
                        need[we] = wc
                for re_, rc in rec["r"].items():
                    if need.get(re_, 0) < rc:
                        need[re_] = rc
        if dma:
            mysem = "%sd%d" % (e, self.dmaidx[e] % NDMAS)
            self.dmaidx[e] += 1
            inc = 16
            if self.cnt[mysem] > 0 and need.get(mysem, 0) < self.cnt[mysem]:
                need[mysem] = self.cnt[mysem]
        else:
            mysem = e
            inc = 1
        waits = []
        for s2, v in need.items():
            if e == "pe" and s2 == "pe":
                continue
            if self.clock[e][s2] < v:
                waits.append((s2, v))
                self.clock[e][s2] = v
        self.cnt[mysem] += inc
        myc = self.cnt[mysem]
        self.ops[e].append((waits, fn, inc, mysem))
        for name in w:
            self.res[name] = {"w": (mysem, myc), "r": {}}
        for name in r:
            rec = self.res.setdefault(name, {"w": None, "r": {}})
            rec["r"][mysem] = myc

    def fence(self, scratch_ap):
        self.nfence += 1
        fname = "fence%d" % self.nfence
        kills = list(self.res.keys())
        self.cur_fence = None
        self.op("dve", lambda e, a=scratch_ap: e.memset(a, 0.0), r=(), w=kills + [fname])
        self.cur_fence = fname

    def replay(self):
        nc = self.nc
        sems = self.sem
        with nc.allow_low_precision(reason="bf16 elementwise blessed by tolerance"), \
             nc.Block() as blk:
            for ename, meth in (("sp", blk.sync), ("gp", blk.gpsimd),
                                ("pe", blk.tensor), ("act", blk.scalar),
                                ("dve", blk.vector)):
                ops = self.ops[ename]

                def body(eng, ops=ops, ename=ename):
                    for waits, fn, inc, mysem in ops:
                        for s2, v in waits:
                            eng.wait_ge(sems[s2], v)
                        fn(eng).then_inc(sems[mysem], inc)
                    if ename in ("sp", "gp"):
                        for i in range(NDMAS):
                            s = "%sd%d" % (ename, i)
                            if self.cnt[s] > 0:
                                eng.wait_ge(sems[s], self.cnt[s])

                meth(body)


def pq(tname, lo, hi):
    if tname.startswith("pb"):
        return [tname]
    q0, q1 = lo // 512, (hi - 1) // 512
    return [tname + "q%d" % i for i in range(q0, q1 + 1)]


def build_nc(debug=False):
    nc = bass.Bass("TRN2", target_bir_lowering=False, dynamic_dma_scratch_size=4096)

    def din(name, shape, dtype=F32):
        return nc.dram_tensor(name, shape, dtype, kind="ExternalInput").ap()

    xt = din("xt", [D, T])
    g_wq = din("g_wq", [D, D]); g_wk = din("g_wk", [D, D])
    g_wv = din("g_wv", [D, D]); g_wf = din("g_wf", [D, D]); g_wo = din("g_wo", [D, D])
    f_wq = din("f_wq", [D, D]); f_wk = din("f_wk", [D, D]); f_wv = din("f_wv", [D, D])
    f_wo = din("f_wo", [D, D])
    exp_w = din("exp_w", [NE, D, D])
    biasm = din("biasm", [128, 16, 8])
    f_bv = din("f_bv", [1, D])
    gate_wt = din("gate_wt", [D, 8])
    gate_b = din("gate_b", [1, 8])
    exp_b = din("exp_b", [8, D])
    identf = din("identf", [128, 128])
    onecol_b = din("onecol_b", [128, 1], BF16)
    onerow_f = din("onerow_f", [1, 128])
    onerow_b = din("onerow_b", [1, 128], BF16)
    tri_b = din("tri_b", [128, 128], BF16)
    ones_b = din("ones_b", [128, 128], BF16)
    ul_b = din("ul_b", [128, 2, 256], BF16)
    ul_f = din("ul_f", [128, 2, 256])
    cosi = din("cosi", [128, T], BF16)
    sini = din("sini", [128, T], BF16)
    pswap = din("pswap", [128, 128], BF16)
    cvals = din("cvals", [128, 2])

    outT = nc.dram_tensor("outT", [D, T], F32, kind="ExternalOutput").ap()

    def scr(name, shape, dtype):
        kind = "ExternalOutput" if debug else "Internal"
        return nc.dram_tensor(name, shape, dtype, kind=kind).ap()

    o_s = scr("o_s", [D, T], BF16)
    s_s = scr("s_s", [D, T], F32)
    q_s = scr("q_s", [D, T], BF16)
    k_s = scr("k_s", [D, T], BF16)
    ao_s = scr("ao_s", [D, T], BF16)
    g_sd = scr("g_sd", [D, T], BF16)
    moew = scr("moew", [NE, D, D], BF16)
    wfo_sd = scr("wfo_sd", [D, D], BF16)

    SBASE = ((nc.sbuf_base + 63) // 64) * 64
    SB = nc.alloc_sbuf_tensor_at
    xn = SB("xn", [128, 8, T], BF16, offset=SBASE + (0))
    x1 = SB("x1", [128, 8, T], BF16, offset=SBASE + (0))
    R2 = 64 * KB
    qfm = SB("qfm", [128, 2, T], BF16, offset=SBASE + (R2))
    kfm = SB("kfm", [128, 2, T], BF16, offset=SBASE + (R2 + 16 * KB))
    ffm = SB("ffm", [128, 2, T], BF16, offset=SBASE + (R2 + 32 * KB))
    vtm = SB("vtm", [128, 32, 256], BF16, offset=SBASE + (R2 + 48 * KB))
    vhat = SB("vhat", [128, 32, 8, 130], BF16, offset=SBASE + (R2))
    webuf = SB("webuf", [128, 2, 8, KB], BF16, offset=SBASE + (R2))
    mfm3 = SB("mfm3", [128, 8, 512], BF16, offset=SBASE + (R2 + 32 * KB))
    y2b = SB("y2b", [128, 8, 512], BF16, offset=SBASE + (R2 + 40 * KB))
    y2sq = SB("y2sq", [128, 512], BF16, offset=SBASE + (R2 + 48 * KB))
    gbo = SB("gbo", [128, 8, 512], BF16, offset=SBASE + (R2 + 49 * KB))
    gld = SB("gld", [128, 8, 512], BF16, offset=SBASE + (R2))
    xf5 = SB("xf5", [128, 8, 512], F32, offset=SBASE + (R2 + 8 * KB))
    sf5 = SB("sf5", [128, 8, 512], F32, offset=SBASE + (R2 + 24 * KB))
    ob5 = SB("ob5", [128, 8, 512], F32, offset=SBASE + (R2 + 40 * KB))
    R3 = 129 * KB
    xst = SB("xst", [128, 2, 8, 512], F32, offset=SBASE + (R3))
    xbfst = SB("xbfst", [128, 512], BF16, offset=SBASE + (R3 + 32 * KB))
    xsqst = SB("xsqst", [128, 512], BF16, offset=SBASE + (R3 + 33 * KB))
    cst = SB("cst", [128, KB], F32, offset=SBASE + (R3 + 34 * KB))
    ktm = SB("ktm", [128, 32, 256], BF16, offset=SBASE + (R3))
    wst = SB("wst", [128, 512], F32, offset=SBASE + (R3 + 16 * KB))
    wgb = SB("wgb", [128, 4, 8, 256], BF16, offset=SBASE + (R3 + 18 * KB))
    mwst = SB("mwst", [128, KB], F32, offset=SBASE + (R3 + 34 * KB))
    mwbf = SB("mwbf", [128, KB], BF16, offset=SBASE + (R3 + 38 * KB))
    ovbb = SB("ovbb", [128, 2, 256], BF16, offset=SBASE + (R3 + 40 * KB))
    sgst = SB("sgst", [128, 512], F32, offset=SBASE + (R3 + 41 * KB))
    C0 = R3 + 43 * KB
    ftm_s = SB("ftm_s", [128, 2, 256], BF16, offset=SBASE + (C0))
    enl_s = SB("enl_s", [128, 512], BF16, offset=SBASE + (C0 + 1 * KB))
    gks_s = SB("gks_s", [128, 512], BF16, offset=SBASE + (C0 + 2 * KB))
    ssm_s = SB("ssm_s", [128, 512], BF16, offset=SBASE + (C0 + 3 * KB))
    sfm_s = SB("sfm_s", [128, 512], BF16, offset=SBASE + (C0 + 4 * KB))
    stm_s = SB("stm_s", [128, 2, 256], BF16, offset=SBASE + (C0 + 5 * KB))
    gtm_s = SB("gtm_s", [128, 2, 256], BF16, offset=SBASE + (C0 + 6 * KB))
    gcr_s = SB("gcr_s", [1, 256], BF16, offset=SBASE + (C0 + 7 * KB))
    gcc_s = SB("gcc_s", [128, 2], F32, offset=SBASE + (C0 + 7 * KB + 512))
    a1_s = SB("a1_s", [128, 2, 256], BF16, offset=SBASE + (C0 + 7 * KB + 576))
    oks_s = SB("oks_s", [128, 256], F32, offset=SBASE + (C0 + 8 * KB + 576))
    eok_s = SB("eok_s", [128, 256], F32, offset=SBASE + (C0 + 9 * KB + 576))
    qvt_s = SB("qvt_s", [128, 2, 256], BF16, offset=SBASE + (C0 + 10 * KB + 576))
    qvf_s = SB("qvf_s", [128, 2, 256], BF16, offset=SBASE + (C0 + 11 * KB + 576))
    a2_s = SB("a2_s", [128, 2, 256], BF16, offset=SBASE + (C0 + 12 * KB + 576))
    s1_s = SB("s1_s", [128, 2, 256], BF16, offset=SBASE + (C0 + 13 * KB + 576))
    s2_s = SB("s2_s", [128, 2, 256], BF16, offset=SBASE + (C0 + 14 * KB + 576))
    col_s = SB("col_s", [128, 8], F32, offset=SBASE + (C0 + 15 * KB + 576))
    stm2 = SB("stm2", [128, 2, 2, 256], BF16, offset=SBASE + (C0 + 16 * KB))
    gcc2 = SB("gcc2", [128, 4], F32, offset=SBASE + (C0 + 18 * KB + 64))
    gcbs = SB("gcbs", [128, 2, 256], BF16, offset=SBASE + (C0 + 18 * KB + 128))
    olod = SB("olod", [128, 2, 8, 512], BF16, offset=SBASE + (R3))
    osil = SB("osil", [128, 8, 512], F32, offset=SBASE + (R3 + 16 * KB))
    osq = SB("osq", [128, 512], BF16, offset=SBASE + (R3 + 32 * KB))
    obuf = SB("obuf", [128, 8, 512], BF16, offset=SBASE + (R3 + 33 * KB))
    wot = SB("wot", [128, 8, KB], BF16, offset=SBASE + (R3 + 41 * KB))
    sev = SB("sev", [128, 512], F32, offset=SBASE + (R3 + 57 * KB))
    wstg = SB("wstg", [128, KB], F32, offset=SBASE + (R3 + 59 * KB))
    wqkb = SB("wqkb", [128, 2, 8, KB], BF16, offset=SBASE + (R3))
    rqb = SB("rqb", [128, 8, 512], BF16, offset=SBASE + (R3 + 32 * KB))
    cosblk = SB("cosblk", [128, 512], BF16, offset=SBASE + (R3 + 40 * KB))
    sinblk = SB("sinblk", [128, 512], BF16, offset=SBASE + (R3 + 41 * KB))
    swpb = SB("swpb", [128, 512], BF16, offset=SBASE + (R3 + 42 * KB))
    m1b = SB("m1b", [128, 512], BF16, offset=SBASE + (R3 + 43 * KB))
    m2b = SB("m2b", [128, 512], BF16, offset=SBASE + (R3 + 44 * KB))
    qkbo = SB("qkbo", [128, 512], BF16, offset=SBASE + (R3 + 45 * KB))
    p2st = SB("p2st", [128, 2 * KB], F32, offset=SBASE + (R3 + 46 * KB))
    p2bf = SB("p2bf", [128, 2 * KB], BF16, offset=SBASE + (R3 + 54 * KB))
    qkbo2 = SB("qkbo2", [128, 512], BF16, offset=SBASE + (R3 + 62 * KB))
    sev2 = SB("sev2", [128, 512], F32, offset=SBASE + (R3 + 63 * KB))
    kbuf = SB("kbuf", [128, 2, T], BF16, offset=SBASE + (R3))
    qbuf = SB("qbuf", [128, 8, 512], BF16, offset=SBASE + (R3 + 16 * KB))
    est = SB("est", [128, 2, KB], BF16, offset=SBASE + (R3 + 24 * KB))
    ctxf = SB("ctxf", [128, 8, 512], BF16, offset=SBASE + (R3 + 28 * KB))
    wfob = SB("wfob", [128, 8, KB], BF16, offset=SBASE + (R3 + 36 * KB))
    aob = SB("aob", [128, 512], BF16, offset=SBASE + (R3 + 52 * KB))
    cxb = SB("cxb", [64, 512], BF16, offset=SBASE + (R3 + 53 * KB))
    xti = SB("xti", [128, 8, 512], F32, offset=SBASE + (R3))
    aol = SB("aol", [128, 8, 512], BF16, offset=SBASE + (R3 + 16 * KB))
    ybf = SB("ybf", [128, 8, 512], BF16, offset=SBASE + (R3 + 24 * KB))
    ysq = SB("ysq", [128, 512], BF16, offset=SBASE + (R3 + 32 * KB))
    macc = SB("macc", [128, 32, KB], BF16, offset=SBASE + (R3))
    R4 = 194 * KB
    identf_s = SB("identf_s", [128, 128], F32, offset=SBASE + (R4))
    identb_s = SB("identb_s", [128, 128], BF16, offset=SBASE + (R4 + 512))
    onecol_s = SB("onecol_s", [128, 1], BF16, offset=SBASE + (R4 + 768))
    onerow_s = SB("onerow_s", [1, 128], F32, offset=SBASE + (R4 + 832))
    onerowb_s = SB("onerowb_s", [1, 128], BF16, offset=SBASE + (R4 + 1344))
    tri_s = SB("tri_s", [128, 128], BF16, offset=SBASE + (R4 + 1600))
    onesb_s = SB("onesb_s", [128, 128], BF16, offset=SBASE + (R4 + 1856))
    ul_s = SB("ul_s", [128, 2, 256], BF16, offset=SBASE + (R4 + 2112))
    ulf_s = SB("ulf_s", [128, 2, 256], F32, offset=SBASE + (R4 + 3136))
    pswap_s = SB("pswap_s", [128, 128], BF16, offset=SBASE + (R4 + 5184))
    B0 = R4 + 5 * KB + 448
    bias_s = SB("bias_s", [128, 16, 8], F32, offset=SBASE + (B0))
    bvrow_s = SB("bvrow_s", [1, D], BF16, offset=SBASE + (B0 + 512))
    gbrow_s = SB("gbrow_s", [1, 8], BF16, offset=SBASE + (B0 + 2560))
    gwt_s = SB("gwt_s", [128, 8, 8], BF16, offset=SBASE + (B0 + 2624))
    expb_s = SB("expb_s", [8, D], BF16, offset=SBASE + (B0 + 2752))
    wbuf_s = SB("wbuf_s", [128, 32, 8], F32, offset=SBASE + (B0 + 4800))
    g8_s = SB("g8_s", [128, 5, 8], F32, offset=SBASE + (B0 + 5824))
    zr_s = SB("zr_s", [1, 512], BF16, offset=SBASE + (B0 + 6016))
    zb_s = SB("zb_s", [64, 512], BF16, offset=SBASE + (B0 + 7040))
    wts_s = SB("wts_s", [8, 128], BF16, offset=SBASE + (B0 + 8064))
    rows = SB("rows", [1, 6, 512], F32, offset=SBASE + (B0 + 8384))
    cvals_s = SB("cvals_s", [128, 2], F32, offset=SBASE + (B0 + 8384 + 12 * KB))

    pf0 = nc.alloc_psum_tensor("pf0", [128, KB], F32)
    pf1 = nc.alloc_psum_tensor("pf1", [128, KB], F32)
    pf2 = nc.alloc_psum_tensor("pf2", [128, KB], F32)
    pb0 = nc.alloc_psum_tensor("pb0", [128, KB], BF16)
    pb1 = nc.alloc_psum_tensor("pb1", [128, KB], BF16)

    S = Sch(nc)

    def dma(dst, src, rr, ww, q="sp"):
        q = "sp"
        S.op(q, lambda e, d=dst, s=src: e.dma_start(out=d, in_=s), r=rr, w=ww, dma=True)

    def mm(out, lhsT, rhs, rr, ww, start=True, stop=True, tp=None):
        S.op("pe", lambda e, o=out, l=lhsT, rh=rhs, st=start, sp=stop, t=tp:
             e.matmul(out=o, lhsT=l, rhs=rh, start=st, stop=sp, tile_position=t,
                      skip_group_check=True), r=rr, w=ww)

    def tr(out, in_, rr, ww, ident):
        S.op("pe", lambda e, o=out, i=in_, idn=ident:
             e.transpose(out=o, in_=i, identity=idn), r=rr, w=ww)

    def act(out, in_, func, rr, ww, bias=0.0, scale=1.0):
        S.op("act", lambda e, o=out, i=in_, f=func, b=bias, s=scale:
             e.activation(out=o, in_=i, func=f, bias=b, scale=s), r=rr, w=ww)

    def dve(fn, rr, ww):
        S.op("dve", fn, r=rr, w=ww)

    def tt(out, a, b, op, rr, ww):
        dve(lambda e, o=out, x=a, y=b, p=op: e.tensor_tensor(out=o, in0=x, in1=y, op=p), rr, ww)

    def ts(out, a, s1, op, rr, ww):
        dve(lambda e, o=out, x=a, s_=s1, p=op:
            e.tensor_single_scalar(out=o, in_=x, scalar=s_, op=p), rr, ww)

    def stt(out, in0, scal, in1, op0, op1, rr, ww):
        dve(lambda e, o=out, x=in0, s_=scal, y=in1, p0=op0, p1=op1:
            e.scalar_tensor_tensor(out=o, in0=x, scalar=s_, in1=y, op0=p0, op1=p1), rr, ww)

    def ts2(out, a, s1, s2, op0, op1, rr, ww):
        dve(lambda e, o=out, x=a, u=s1, v=s2, p0=op0, p1=op1:
            e.tensor_scalar(out=o, in0=x, scalar1=u, scalar2=v, op0=p0, op1=p1), rr, ww)

    def cp(out, in_, rr, ww):
        dve(lambda e, o=out, i=in_: e.tensor_copy(out=o, in_=i), rr, ww)

    def rmax(out, in_, rr, ww, negate=False):
        dve(lambda e, o=out, i=in_, n=negate:
            e.reduce_max(out=o, in_=i, axis=AX.X, negate=n), rr, ww)

    def rsum(out, in_, rr, ww):
        dve(lambda e, o=out, i=in_: e.reduce_sum(out=o, in_=i, axis=AX.X), rr, ww)

    def recip(out, in_, rr, ww):
        dve(lambda e, o=out, i=in_: e.reciprocal(out=o, in_=i), rr, ww)

    def mset(ap, val, rr, ww):
        dve(lambda e, a=ap, v=val: e.memset(a, v), rr, ww)

    nc.const_aps.register(F32, 1e-5, cvals_s[:, 0:1])
    nc.const_aps.register(F32, EXPB, cvals_s[:, 1:2])
    dma(cvals_s[:], cvals, [], ["cvals"])
    dma(identf_s[:], identf, [], ["identf"])
    cp(identb_s[:], identf_s[:], ["identf"], ["identb"])
    dma(onecol_s[:], onecol_b, [], ["onecol"])
    dma(onerow_s[:], onerow_f, [], ["onerow"])
    dma(onerowb_s[:], onerow_b, [], ["onerowb"])
    dma(tri_s[:], tri_b, [], ["tri"])
    dma(onesb_s[:], ones_b, [], ["onesb"])
    dma(ul_s[:], ul_b, [], ["ul"])
    dma(ulf_s[:], ul_f, [], ["ulf"])
    dma(pswap_s[:], pswap, [], ["pswap"])
    dma(bias_s[:], biasm, [], ["bias"])
    ts(bias_s[:, 8, :], bias_s[:, 0, :], 0.125, ALU.mult, ["bias"], ["bias"])
    dma(cst[0:1, :], f_bv, ["cstr"], ["cst"])
    cp(bvrow_s[:], cst[0:1, :], ["cst"], ["bvrow", "cstr"])
    dma(cst[0:1, 0:8], gate_b, ["cstr"], ["cst"])
    cp(gbrow_s[:], cst[0:1, 0:8], ["cst"], ["gbrow", "cstr"])
    dma(cst[:, 0:64].rearrange("p (c e) -> p c e", c=8),
        gate_wt.rearrange("(c p) e -> p c e", p=128), ["cstr"], ["cst"])
    cp(gwt_s[:].rearrange("p c e -> p (c e)"), cst[:, 0:64], ["cst"], ["gwt", "cstr"])
    dma(cst[0:8, :], exp_b, ["cstr"], ["cst"])
    cp(expb_s[:], cst[0:8, :], ["cst"], ["expb", "cstr"])

    inv_d = 1.0 / D
    for tb in range(8):
        bf = tb % 2
        c0 = tb * 512
        for c in range(8):
            dma(xst[:, bf, c, :], xt[128 * c:128 * (c + 1), c0:c0 + 512],
                [], ["xst%d_%d" % (bf, c)])
        for c in range(8):
            xc = "xst%d_%d" % (bf, c)
            cp(xbfst[:], xst[:, bf, c, :], [xc], ["xbf"])
            tt(xsqst[:], xbfst[:], xbfst[:], ALU.mult, ["xbf"], ["xsq"])
            mm(pf0[0:1, 0:512], onecol_s[:], xbfst[:], ["onecol", "xbf"], pq("pf0", 0, 512),
               start=(c == 0), stop=(c == 7))
            mm(pf0[0:1, 512:1024], onecol_s[:], xsqst[:], ["onecol", "xsq"],
               pq("pf0", 512, 1024), start=(c == 0), stop=(c == 7))
        ts(rows[0:1, 0, :], pf0[0:1, 0:512], inv_d, ALU.mult, pq("pf0", 0, 512), ["r0"])
        ts(rows[0:1, 1, :], pf0[0:1, 512:1024], inv_d, ALU.mult, pq("pf0", 512, 1024), ["r1"])
        tt(rows[0:1, 2, :], rows[0:1, 0, :], rows[0:1, 0, :], ALU.mult, ["r0"], ["r2"])
        tt(rows[0:1, 3, :], rows[0:1, 1, :], rows[0:1, 2, :], ALU.subtract, ["r1", "r2"], ["r3"])
        act(rows[0:1, 4, :], rows[0:1, 3, :], AF.Ln, ["r3"], ["r4"], bias=1e-5)
        act(rows[0:1, 5, :], rows[0:1, 4, :], AF.Exp, ["r4"], ["r5"], scale=-0.5)
        mm(pf1[:, 0:512], onerow_s[:], rows[0:1, 0, :], ["onerow", "r0"], pq("pf1", 0, 512))
        mm(pf1[:, 512:1024], onerow_s[:], rows[0:1, 5, :], ["onerow", "r5"],
           pq("pf1", 512, 1024))
        for c in range(8):
            xc = "xst%d_%d" % (bf, c)
            tt(xst[:, bf, c, :], xst[:, bf, c, :], pf1[:, 0:512], ALU.subtract,
               [xc] + pq("pf1", 0, 512), [xc])
            tt(xn[:, c, c0:c0 + 512], xst[:, bf, c, :], pf1[:, 512:1024], ALU.mult,
               [xc] + pq("pf1", 512, 1024), ["xn"])

    S.fence(col_s[:, 7:8])
    if debug:
        dxn = nc.dram_tensor("dxn", [D, T], BF16, kind="ExternalOutput").ap()
        dma(dxn.rearrange("(c p) t -> p c t", p=128), xn[:], ["xn"], ["dxn"])
    for ei in range(NE):
        for qtr in range(8):
            dma(mwst[:].rearrange("p (a o) -> p a o", a=1),
                exp_w[ei].rearrange("(a p) o -> p a o", p=128)[:, qtr:qtr + 1, :],
                ["mwst_r"], ["mwst"], q="gp")
            cp(mwbf[:], mwst[:], ["mwst", "mwbf_r"], ["mwbf", "mwst_r"])
            dma(moew[ei].rearrange("(a p) o -> p a o", p=128)[:, qtr:qtr + 1, :],
                mwbf[:].rearrange("p (a o) -> p a o", a=1), ["mwbf"], ["mwbf_r"], q="gp")

    wsrc = [g_wq, g_wk, g_wv, g_wf]
    for h in range(GH):
        hc = GD * h
        for wi in range(4):
            for sl in range(4):
                dma(wst[:].rearrange("p (a o) -> p a o", a=2),
                    wsrc[wi].rearrange("(a p) o -> p a o", p=128)
                    [:, 2 * sl:2 * sl + 2, hc:hc + GD], ["wstr"], ["wstw"], q="gp")
                cp(wgb[:, wi, 2 * sl:2 * sl + 2, :].rearrange("p a o -> p (a o)"),
                   wst[:], ["wstw"], ["wgb%d" % wi, "wstr"])
        for (wi, dst, dname, actf) in ((0, qfm, "qfm", AF.Silu), (1, kfm, "kfm", AF.Silu)):
            for ofc in range(2):
                for tb in range(8):
                    t0 = tb * 512
                    pt_t = ("pf0" if tb % 2 == 0 else "pf1")
                    dstp = (pf0 if tb % 2 == 0 else pf1)[:, 512 * ofc:512 * (ofc + 1)]
                    toks = pq(pt_t, 512 * ofc, 512 * (ofc + 1))
                    for ifc in range(8):
                        mm(dstp, wgb[:, wi, ifc, 128 * ofc:128 * (ofc + 1)],
                           xn[:, ifc, t0:t0 + 512], ["wgb%d" % wi, "xn"], toks,
                           start=(ifc == 0), stop=(ifc == 7))
                    act(dst[:, ofc, t0:t0 + 512], dstp, actf, toks, [dname])
        for ofc in range(2):
            for tb in range(8):
                t0 = tb * 512
                pt_t = ("pf0" if tb % 2 == 0 else "pf1")
                dstp = (pf0 if tb % 2 == 0 else pf1)[:, 512 * ofc:512 * (ofc + 1)]
                toks = pq(pt_t, 512 * ofc, 512 * (ofc + 1))
                for ifc in range(8):
                    mm(dstp, wgb[:, 3, ifc, 128 * ofc:128 * (ofc + 1)],
                       xn[:, ifc, t0:t0 + 512], ["wgb3", "xn"], toks,
                       start=(ifc == 0), stop=(ifc == 7))
                act(sgst[:], dstp, AF.Exp, toks, ["sgst"], scale=-1.0)
                act(ffm[:, ofc, t0:t0 + 512], sgst[:], AF.Ln, ["sgst"], ["ffm"], bias=1.0)
        for (wi, dst, dname) in ((1, ktm, "ktm"), (2, vtm, "vtm")):
            for tt_ in range(32):
                t0 = tt_ * 128
                pt_t = ("pf0" if tt_ % 2 == 0 else "pf1")
                dstp = (pf0 if tt_ % 2 == 0 else pf1)[:, 0:256]
                toks = pq(pt_t, 0, 256)
                for ifc in range(8):
                    mm(dstp, xn[:, ifc, t0:t0 + 128], wgb[:, wi, ifc, :],
                       ["wgb%d" % wi, "xn"], toks, start=(ifc == 0), stop=(ifc == 7))
                act(dst[:, tt_, :], dstp, AF.Silu, toks, [dname])

        mset(s1_s[:].rearrange("p a b -> p (a b)"), 0.0, [], ["s1"])
        mset(s2_s[:].rearrange("p a b -> p (a b)"), 0.0, [], ["s2"])

        def st_ftm(cc):
            t0 = cc * CH
            for th in range(2):
                for mc in range(2):
                    tr(pb0[:, 128 * (2 * th + mc):128 * (2 * th + mc + 1)],
                       ffm[:, mc, t0 + 128 * th:t0 + 128 * (th + 1)],
                       ["ffm"] + pq("pb0", 0, 1), pq("pb0", 0, 1), identb_s[:])
                cp(ftm_s[:, th, :], pb0[:, 256 * th:256 * (th + 1)],
                   pq("pb0", 0, 1), ["ftm"])

        def st_front(cc):
            t0 = cc * CH
            for mc in range(2):
                for th in range(2):
                    mm(pf0[:, 256 * mc:256 * (mc + 1)], ftm_s[:, th, 128 * mc:128 * (mc + 1)],
                       ul_s[:, th, :], ["ftm", "ul"], pq("pf0", 256 * mc, 256 * (mc + 1)),
                       start=(th == 0), stop=(th == 1))
            for tc in range(2):
                for dkc in range(2):
                    mm(pf1[:, 512 + 256 * tc:512 + 256 * (tc + 1)],
                       kfm[:, dkc, t0 + 128 * tc:t0 + 128 * (tc + 1)],
                       qfm[:, dkc, t0:t0 + 256], ["kfm", "qfm"],
                       pq("pf1", 512 + 256 * tc, 768 + 256 * tc),
                       start=(dkc == 0), stop=(dkc == 1))
                tt(a1_s[:, tc, :], pf1[:, 512 + 256 * tc:512 + 256 * (tc + 1)],
                   ulf_s[:, tc, :], ALU.mult,
                   pq("pf1", 512 + 256 * tc, 768 + 256 * tc) + ["ulf"], ["a1"])
            act(enl_s[:], pf0[:, 0:512], AF.Exp, pq("pf0", 0, 512), ["enl"], scale=0.125)
            act(gcc2[:, 2 * (cc % 2):2 * (cc % 2) + 2], pf0[:, 255:512:256], AF.Exp,
                pq("pf0", 0, 512), ["gcc%d" % (cc % 2)], scale=-0.125)
            for mc in range(2):
                act(gks_s[:, 256 * mc:256 * (mc + 1)], ffm[:, mc, t0:t0 + 256],
                    AF.Exp, ["ffm"], ["gks"], scale=-0.125)
            tt(ssm_s[:], gks_s[:], enl_s[:], ALU.mult, ["gks", "enl"], ["ssm"])
            tt(sfm_s[:], enl_s[:], ssm_s[:], ALU.subtract, ["enl", "ssm"], ["sfm"])
            mm(pf0[:, 512:768], tri_s[:], ftm_s[:, 0, :], ["tri", "ftm"], pq("pf0", 512, 768))
            mm(pf0[:, 768:1024], onesb_s[:], ftm_s[:, 0, :], ["onesb", "ftm"],
               pq("pf0", 768, 1024), start=True, stop=False)
            mm(pf0[:, 768:1024], tri_s[:], ftm_s[:, 1, :], ["tri", "ftm"],
               pq("pf0", 768, 1024), start=False, stop=True)
            act(gtm_s[:].rearrange("p a b -> p (a b)"), pf0[:, 512:1024], AF.Exp,
                pq("pf0", 512, 1024), ["gtm"], scale=-0.125)
            for th in range(2):
                mm(pf1[0:1, 0:256], onecol_s[:], ftm_s[:, th, :], ["onecol", "ftm"],
                   pq("pf1", 0, 256), start=(th == 0), stop=(th == 1))
            act(gcr_s[:], pf1[0:1, 0:256], AF.Exp, pq("pf1", 0, 256), ["gcr"], scale=-0.125)
            mm(pf1[:, 256:512], onerowb_s[:], gcr_s[:], ["onerowb", "gcr"], pq("pf1", 256, 512))
            cp(gcbs[:, cc % 2, :], pf1[:, 256:512], pq("pf1", 256, 512), ["gcb%d" % (cc % 2)])
            for th in range(2):
                for mc in range(2):
                    tr(pb0[:, 512 + 128 * (2 * th + mc):512 + 128 * (2 * th + mc + 1)],
                       sfm_s[:, 256 * mc + 128 * th:256 * mc + 128 * (th + 1)],
                       ["sfm"] + pq("pb0", 0, 1), pq("pb0", 0, 1), identb_s[:])
                cp(stm2[:, cc % 2, th, :], pb0[:, 512 + 256 * th:512 + 256 * (th + 1)],
                   pq("pb0", 0, 1), ["stm%d" % (cc % 2)])

        def st_ok(cc):
            t0 = cc * CH
            for th in range(2):
                pok = pf2[:, 256 * th:256 * (th + 1)]
                ptoks = pq("pf2", 256 * th, 256 * (th + 1))
                for dkc in range(2):
                    mm(pok, qfm[:, dkc, t0 + 128 * th:t0 + 128 * (th + 1)],
                       s1_s[:, dkc, :], ["qfm", "s1"], ptoks,
                       start=(dkc == 0), stop=False)
                for tc in range(2):
                    mm(pok, a1_s[:, tc, 128 * th:128 * (th + 1)], stm2[:, cc % 2, tc, :],
                       ["a1", "stm%d" % (cc % 2)], ptoks, start=False, stop=(tc == 1))
                tt(oks_s[:], pok, gtm_s[:, th, :], ALU.mult, ptoks + ["gtm"], ["oks"])
                rmax(col_s[:, 0:1], oks_s[:], ["oks"], ["nmx"], negate=True)
                act(eok_s[:], oks_s[:], AF.Exp, ["oks", "nmx"], ["eok"], bias=col_s[:, 0:1])
                rsum(col_s[:, 1:2], eok_s[:], ["eok"], ["smx"])
                recip(col_s[:, 2:3], col_s[:, 1:2], ["smx"], ["rsx"])
                ts(eok_s[:], eok_s[:], col_s[:, 2:3], ALU.mult, ["eok", "rsx"], ["eok"])
                tt(qvt_s[:, th, :], eok_s[:], gtm_s[:, th, :], ALU.mult, ["eok", "gtm"], ["qvt"])
            for mc in range(2):
                for th in range(2):
                    tr(pb1[:, 128 * (2 * mc + th):128 * (2 * mc + th + 1)],
                       qvt_s[:, th, 128 * mc:128 * (mc + 1)],
                       ["qvt"] + pq("pb1", 0, 1), pq("pb1", 0, 1), identb_s[:])
                cp(qvf_s[:, mc, :], pb1[:, 256 * mc:256 * (mc + 1)],
                   pq("pb1", 0, 1), ["qvf"])
            for tc in range(2):
                for mc in range(2):
                    mm(pf2[:, 512 + 256 * tc:512 + 256 * (tc + 1)],
                       sfm_s[:, 256 * mc + 128 * tc:256 * mc + 128 * (tc + 1)],
                       qvf_s[:, mc, :], ["sfm", "qvf"],
                       pq("pf2", 512 + 256 * tc, 768 + 256 * tc),
                       start=(mc == 0), stop=(mc == 1))
                tt(a2_s[:, tc, :], pf2[:, 512 + 256 * tc:512 + 256 * (tc + 1)],
                   ulf_s[:, tc, :], ALU.mult,
                   pq("pf2", 512 + 256 * tc, 768 + 256 * tc) + ["ulf"], ["a2"])

        def st_back(cc):
            t0 = cc * CH
            for vc in range(2):
                pov = pf0[:, 256 * vc:256 * (vc + 1)]
                ptoks = pq("pf0", 256 * vc, 256 * (vc + 1))
                for mc in range(2):
                    mm(pov, s2_s[:, mc, 128 * vc:128 * (vc + 1)], qvf_s[:, mc, :],
                       ["s2", "qvf"], ptoks, start=(mc == 0), stop=False)
                for tc in range(2):
                    mm(pov, vtm[:, 2 * cc + tc, 128 * vc:128 * (vc + 1)], a2_s[:, tc, :],
                       ["vtm", "a2"], ptoks, start=False, stop=(tc == 1))
                cp(ovbb[:, vc, :], pov, ptoks, ["ovb%d" % vc])
                dma(o_s[hc + 128 * vc:hc + 128 * (vc + 1), t0:t0 + CH], ovbb[:, vc, :],
                    ["ovb%d" % vc], ["o_sd"], q="gp")
            for dkc in range(2):
                pc1 = pf0[:, 512 + 256 * dkc:512 + 256 * (dkc + 1)]
                ptoks = pq("pf0", 512 + 256 * dkc, 768 + 256 * dkc)
                for tc in range(2):
                    mm(pc1, ktm[:, 2 * cc + tc, 128 * dkc:128 * (dkc + 1)],
                       stm2[:, cc % 2, tc, :], ["ktm", "stm%d" % (cc % 2)], ptoks,
                       start=(tc == 0), stop=(tc == 1))
            for mc in range(2):
                pc2 = pf1[:, 512 + 256 * mc:512 + 256 * (mc + 1)]
                ptoks = pq("pf1", 512 + 256 * mc, 768 + 256 * mc)
                for tc in range(2):
                    mm(pc2, stm2[:, cc % 2, tc, 128 * mc:128 * (mc + 1)],
                       vtm[:, 2 * cc + tc, :], ["stm%d" % (cc % 2), "vtm"], ptoks,
                       start=(tc == 0), stop=(tc == 1))
            for dkc in range(2):
                tt(s1_s[:, dkc, :], s1_s[:, dkc, :], pf0[:, 512 + 256 * dkc:512 + 256 * (dkc + 1)],
                   ALU.add, ["s1"] + pq("pf0", 512 + 256 * dkc, 768 + 256 * dkc), ["s1"])
                tt(s1_s[:, dkc, :], s1_s[:, dkc, :], gcbs[:, cc % 2, :], ALU.mult,
                   ["s1", "gcb%d" % (cc % 2)], ["s1"])
            for mc in range(2):
                tt(s2_s[:, mc, :], s2_s[:, mc, :], pf1[:, 512 + 256 * mc:512 + 256 * (mc + 1)],
                   ALU.add, ["s2"] + pq("pf1", 512 + 256 * mc, 768 + 256 * mc), ["s2"])
                ts(s2_s[:, mc, :], s2_s[:, mc, :], gcc2[:, 2 * (cc % 2) + mc:2 * (cc % 2) + mc + 1],
                   ALU.mult, ["s2", "gcc%d" % (cc % 2)], ["s2"])

        st_ftm(0)
        st_front(0)
        for cc in range(NCH):
            st_ok(cc)
            if cc + 1 < NCH:
                st_ftm(cc + 1)
                st_front(cc + 1)
            st_back(cc)
    for sl in range(8):
        dma(wstg[:].rearrange("p (a o) -> p a o", a=1),
            g_wo.rearrange("(a p) o -> p a o", p=128)[:, sl:sl + 1, :],
            ["wstg_r"], ["wstg"], q="gp")
        cp(wot[:, sl, :], wstg[:], ["wstg"], ["wot", "wstg_r"])
    def p1t_load(tb):
        t0 = tb * 512
        bf = tb % 2
        for c in range(8):
            dma(olod[:, bf, c, :], o_s[128 * c:128 * (c + 1), t0:t0 + 512],
                ["o_sd", "ol%d_%d" % (bf, c), "ktm", "wgb1", "wgb2"], ["ol%d_%d" % (bf, c)])
    p1t_load(0)
    for tb in range(8):
        t0 = tb * 512
        bf = tb % 2
        if tb + 1 < 8:
            p1t_load(tb + 1)
        for c in range(8):
            act(osil[:, c, :], olod[:, bf, c, :], AF.Silu, ["ol%d_%d" % (bf, c)], ["osil"])
            tt(osq[:], osil[:, c, :], osil[:, c, :], ALU.mult, ["osil"], ["osq"])
            mm(pf2[0:1, 0:512], onecol_s[:], osq[:], ["onecol", "osq"], pq("pf2", 0, 512),
               start=(c == 0), stop=(c == 7))
        ts(rows[0:1, 0, :], pf2[0:1, 0:512], inv_d, ALU.mult, pq("pf2", 0, 512), ["r0"])
        act(rows[0:1, 1, :], rows[0:1, 0, :], AF.Ln, ["r0"], ["r1"], bias=1e-5)
        act(rows[0:1, 2, :], rows[0:1, 1, :], AF.Exp, ["r1"], ["r2"], scale=-0.5)
        mm(pf1[:, 0:512], onerow_s[:], rows[0:1, 2, :], ["onerow", "r2"], pq("pf1", 0, 512))
        for c in range(8):
            tt(osil[:, c, :], osil[:, c, :], pf1[:, 0:512], ALU.mult,
               ["osil"] + pq("pf1", 0, 512), ["osil"])
            ts(obuf[:, c, :], osil[:, c, :], bias_s[:, 7, c:c + 1], ALU.mult,
               ["osil", "bias"], ["obuf"])
        for ofc in range(8):
            pt_t = "pf0" if ofc % 2 == 0 else "pf2"
            po = (pf0 if ofc % 2 == 0 else pf2)[:, 0:512]
            toks = pq(pt_t, 0, 512)
            for ifc in range(8):
                mm(po, wot[:, ifc, 128 * ofc:128 * (ofc + 1)], obuf[:, ifc, :],
                   ["wot", "obuf"], toks, start=(ifc == 0), stop=(ifc == 7))
            sv_ = sev if ofc % 2 == 0 else sev2
            svtok = "sev%d" % (ofc % 2)
            cp(sv_[:], po, toks + ["sdma%d" % (ofc % 2)], [svtok])
            dma(s_s[128 * ofc:128 * (ofc + 1), t0:t0 + 512], sv_[:], [svtok],
                ["sdma%d" % (ofc % 2), "s_sd"], q="gp")

    S.fence(col_s[:, 7:8])
    for wi, wt_ in ((0, f_wq), (1, f_wk)):
        for sl in range(4):
            dma(p2st[:].rearrange("p (a o) -> p a o", a=2),
                wt_.rearrange("(a p) o -> p a o", p=128)[:, 2 * sl:2 * sl + 2, :],
                ["p2str"], ["p2stw"], q="gp")
            cp(wqkb[:, wi, 2 * sl:2 * sl + 2, :].rearrange("p a b -> p (a b)"),
               p2st[:], ["p2stw", "wot", "sev", "ol0"], ["wqk%d" % wi, "p2str"])
    for sl in range(4):
        dma(p2st[:].rearrange("p (a o) -> p a o", a=2),
            f_wo.rearrange("(a p) o -> p a o", p=128)[:, 2 * sl:2 * sl + 2, :],
            ["p2str"], ["p2stw"], q="gp")
        cp(p2bf[:], p2st[:], ["p2stw", "p2bfr"], ["p2bf", "p2str"])
        dma(wfo_sd.rearrange("(a p) o -> p a o", p=128)[:, 2 * sl:2 * sl + 2, :],
            p2bf[:].rearrange("p (a o) -> p a o", a=2), ["p2bf"], ["p2bfr"], q="gp")
    for tb in range(8):
        t0 = tb * 512
        dma(cosblk[:], cosi[:, t0:t0 + 512], ["m1"], ["cosb"])
        dma(sinblk[:], sini[:, t0:t0 + 512], ["m2"], ["sinb"])
        for ifc in range(8):
            par = ifc % 2
            pp = pf0[:, 512 * par:512 * (par + 1)]
            toks = pq("pf0", 512 * par, 512 * (par + 1))
            mm(pp, pswap_s[:], xn[:, ifc, t0:t0 + 512], ["pswap", "xn"], toks)
            cp(swpb[:], pp, toks, ["swp"])
            tt(m1b[:], xn[:, ifc, t0:t0 + 512], cosblk[:], ALU.mult, ["xn", "cosb"], ["m1"])
            tt(m2b[:], swpb[:], sinblk[:], ALU.mult, ["swp", "sinb"], ["m2"])
            tt(rqb[:, ifc, :], m1b[:], m2b[:], ALU.add, ["m1", "m2"], ["rqb"])
        for wi, dsts, bslot, scl in ((0, q_s, 0, 0.125), (1, k_s, 1, 1.0)):
            for ofc in range(8):
                par = ofc % 2
                pp = pf1[:, 512 * par:512 * (par + 1)]
                toks = pq("pf1", 512 * par, 512 * (par + 1))
                for ifc in range(8):
                    mm(pp, wqkb[:, wi, ifc, 128 * ofc:128 * (ofc + 1)], rqb[:, ifc, :],
                       ["wqk%d" % wi, "rqb"], toks, start=(ifc == 0), stop=(ifc == 7))
                qb_ = qkbo if ofc % 2 == 0 else qkbo2
                qtok = "qkbo%d" % (ofc % 2)
                ts2(qb_[:], pp, bias_s[:, bslot, ofc:ofc + 1], scl, ALU.add, ALU.mult,
                    toks + ["bias", "qkdma%d" % (ofc % 2)], [qtok])
                dma(dsts[128 * ofc:128 * (ofc + 1), t0:t0 + 512], qb_[:], [qtok],
                    ["qkdma%d" % (ofc % 2), "qkdma"])

    for sl in range(4):
        dma(p2st[:].rearrange("p (a o) -> p a o", a=2),
            f_wv.rearrange("(a p) o -> p a o", p=128)[:, 2 * sl:2 * sl + 2, :],
            ["p2str"], ["p2stw"], q="gp")
        cp(wqkb[:, 0, 2 * sl:2 * sl + 2, :].rearrange("p a b -> p (a b)"),
           p2st[:], ["p2stw", "wqk0"], ["wqk0", "p2str"])
    mset(vhat[:, :, :, 64:65], 1.0, ["qfm", "kfm", "ffm", "vtm"], ["vhat"])
    mset(vhat[:, :, :, 129:130], 1.0, ["vhat"], ["vhat"])
    for tg in range(32):
        tt0 = tg * 128
        for hf in range(2):
            ppv = pf2[:, 512 * hf:512 * (hf + 1)]
            toks = pq("pf2", 512 * hf, 512 * (hf + 1))
            for ifc in range(8):
                mm(ppv, xn[:, ifc, tt0:tt0 + 128], wqkb[:, 0, ifc, 512 * hf:512 * (hf + 1)],
                   ["wqk0", "xn"], toks, start=(ifc == 0), stop=False)
            mm(ppv, onerowb_s[0:1, :], bvrow_s[0:1, 512 * hf:512 * (hf + 1)],
               ["onerowb", "bvrow"], toks, start=False, stop=True)
        for p_ in range(8):
            base = 128 * p_
            hf = 1 if base >= 512 else 0
            col = base - 512 * hf
            ppv = pf2[:, 512 * hf:512 * (hf + 1)]
            toks = pq("pf2", 512 * hf, 512 * (hf + 1))
            cp(vhat[:, tg, p_, 0:64], ppv[:, col:col + 64], toks + ["vhat"], ["vhat"])
            cp(vhat[:, tg, p_, 65:129], ppv[:, col + 64:col + 128], toks + ["vhat"], ["vhat"])

    S.fence(col_s[:, 7:8])
    for sl in range(2):
        dma(wfob[:].rearrange("p a b -> p a b")[:, 4 * sl:4 * sl + 4, :],
            wfo_sd.rearrange("(a p) o -> p a o", p=128)[:, 4 * sl:4 * sl + 4, :],
            ["p2bfr", "wqk0", "wqk1", "rqb", "swp", "m1", "m2", "qkbo"], ["wfob"])
    for qt in range(NQT):
        q0 = qt * 512
        nkt = 4 * (qt + 1)
        dma(qbuf[:], q_s.rearrange("(pp p) t -> p pp t", p=128)[:, :, q0:q0 + 512],
            ["qkdma", "qbr"], ["qbw"])
        for pp_ in range(NP):
            kpar = pp_ % 2
            dma(kbuf[:, kpar, 0:512 * (qt + 1)], k_s[128 * pp_:128 * (pp_ + 1), 0:512 * (qt + 1)],
                ["kb%d" % kpar, "qkdma"], ["kbw%d" % kpar], q="gp")
            def emit_ctx(kt, nkt=nkt, pp_=pp_):
                par = kt % 2
                mm(pf2[0:65, 0:512], vhat[:, kt, pp_, 0:65], est[:, par, 0:512],
                   ["vhat", "est%d" % par], pq("pf2", 0, 512),
                   start=(kt == 0), stop=(kt == nkt - 1))
                mm(pf2[0:65, 512:1024], vhat[:, kt, pp_, 65:130], est[:, par, 512:1024],
                   ["vhat", "est%d" % par], pq("pf2", 512, 1024),
                   start=(kt == 0), stop=(kt == nkt - 1))
            for kt in range(nkt):
                par = kt % 2
                psc = pf0 if par == 0 else pf1
                ptn = "pf0" if par == 0 else "pf1"
                toksA = pq(ptn, 0, 512)
                toksB = pq(ptn, 512, 1024)
                mm(psc[:, 0:512], kbuf[0:64, kpar, 128 * kt:128 * (kt + 1)],
                   qbuf[0:64, pp_, :], ["kbw%d" % kpar, "qbw"], toksA, tp=(0, 0))
                mm(psc[:, 512:1024], kbuf[64:128, kpar, 128 * kt:128 * (kt + 1)],
                   qbuf[64:128, pp_, :], ["kbw%d" % kpar, "qbw"], toksB, tp=(64, 0))
                act(est[:, par, :], psc[:, 0:1024], AF.Exp,
                    toksA + toksB + ["est%d" % par], ["est%d" % par], bias=EXPB)
                di = kt - 4 * qt
                if di >= 0:
                    for hh in range(2):
                        base = 512 * hh
                        if di > 0:
                            mset(est[:, par, base:base + 128 * di], 0.0,
                                 ["est%d" % par], ["est%d" % par])
                        tt(est[:, par, base + 128 * di:base + 128 * (di + 1)],
                           est[:, par, base + 128 * di:base + 128 * (di + 1)],
                           tri_s[:], ALU.mult, ["est%d" % par, "tri"], ["est%d" % par])
                if kt > 0:
                    emit_ctx(kt - 1)
            emit_ctx(nkt - 1)
            act(rows[0:1, 0, :], pf2[64:65, 0:512], AF.Ln, pq("pf2", 0, 512) + ["lnz"],
                ["lnz"])
            act(zr_s[:], rows[0:1, 0, :], AF.Exp, ["lnz"], ["zr"], scale=-1.0)
            mm(pf0[0:64, 0:512], onerowb_s[0:1, 0:64], zr_s[:], ["onerowb", "zr"],
               pq("pf0", 0, 512))
            cp(zb_s[:], pf0[0:64, 0:512], pq("pf0", 0, 512), ["zb"])
            tt(ctxf[0:64, pp_, :], pf2[0:64, 0:512], zb_s[:], ALU.mult,
               pq("pf2", 0, 512) + ["zb", "ctxf%d" % pp_], ["ctxf%d" % pp_])
            act(rows[0:1, 0, :], pf2[64:65, 512:1024], AF.Ln, pq("pf2", 512, 1024) + ["lnz"],
                ["lnz"])
            act(zr_s[:], rows[0:1, 0, :], AF.Exp, ["lnz"], ["zr"], scale=-1.0)
            mm(pf0[0:64, 0:512], onerowb_s[0:1, 0:64], zr_s[:], ["onerowb", "zr"],
               pq("pf0", 0, 512))
            cp(zb_s[:], pf0[0:64, 0:512], pq("pf0", 0, 512), ["zb"])
            tt(ctxf[64:128, pp_, :], pf2[0:64, 512:1024], zb_s[:], ALU.mult,
               pq("pf2", 512, 1024) + ["zb", "ctxf%d" % pp_], ["ctxf%d" % pp_])
        for ofc in range(8):
            par = ofc % 2
            po = (pf0 if par == 0 else pf1)[:, 0:512]
            ptn = "pf0" if par == 0 else "pf1"
            toks = pq(ptn, 0, 512)
            for pp_ in range(NP):
                mm(po, wfob[:, pp_, 128 * ofc:128 * (ofc + 1)], ctxf[:, pp_, :],
                   ["wfob", "ctxf%d" % pp_], toks,
                   start=(pp_ == 0), stop=(pp_ == 7))
            abv = aob[:] if ofc % 2 == 0 else est[:, 1, 0:512]
            abtok = "aob%d" % (ofc % 2)
            cp(abv, po, toks + ["adma%d" % (ofc % 2)] + ([] if ofc % 2 == 0 else ["est1"]),
               [abtok] + ([] if ofc % 2 == 0 else ["est1"]))
            dma(ao_s[128 * ofc:128 * (ofc + 1), q0:q0 + 512], abv, [abtok],
                ["adma%d" % (ofc % 2), "aod"])

    S.fence(col_s[:, 7:8])
    for qt in range(NQT):
        q0 = qt * 512
        dma(xti[:], xt.rearrange("(c p) t -> p c t", p=128)[:, :, q0:q0 + 512],
            ["xtir", "kbw0", "kbw1", "qbw", "est0", "est1"] if qt == 0 else ["xtir"],
            ["xtiw"])
        dma(aol[:], ao_s.rearrange("(c p) t -> p c t", p=128)[:, :, q0:q0 + 512],
            ["aod", "aolr"], ["aolw"], q="gp")
        for c in range(8):
            stt(ybf[:, c, :], aol[:, c, :], bias_s[:, 2, c:c + 1], xti[:, c, :],
                ALU.add, ALU.add, ["aolw", "xtiw", "bias"], ["ybf", "xtir", "aolr"])
        for c in range(8):
            tt(ysq[:], ybf[:, c, :], ybf[:, c, :], ALU.mult, ["ybf"], ["ysq"])
            mm(pf2[0:1, 0:512], onecol_s[:], ybf[:, c, :], ["onecol", "ybf"],
               pq("pf2", 0, 512), start=(c == 0), stop=(c == 7))
            mm(pf2[0:1, 512:1024], onecol_s[:], ysq[:], ["onecol", "ysq"],
               pq("pf2", 512, 1024), start=(c == 0), stop=(c == 7))
        ts(rows[0:1, 0, :], pf2[0:1, 0:512], inv_d, ALU.mult, pq("pf2", 0, 512), ["r0"])
        ts(rows[0:1, 1, :], pf2[0:1, 512:1024], inv_d, ALU.mult, pq("pf2", 512, 1024), ["r1"])
        tt(rows[0:1, 2, :], rows[0:1, 0, :], rows[0:1, 0, :], ALU.mult, ["r0"], ["r2"])
        tt(rows[0:1, 3, :], rows[0:1, 1, :], rows[0:1, 2, :], ALU.subtract, ["r1", "r2"], ["r3"])
        act(rows[0:1, 4, :], rows[0:1, 3, :], AF.Ln, ["r3"], ["r4"], bias=1e-5)
        act(rows[0:1, 5, :], rows[0:1, 4, :], AF.Exp, ["r4"], ["r5"], scale=-0.5)
        mm(pf1[:, 0:512], onerow_s[:], rows[0:1, 0, :], ["onerow", "r0"], pq("pf1", 0, 512))
        mm(pf1[:, 512:1024], onerow_s[:], rows[0:1, 5, :], ["onerow", "r5"],
           pq("pf1", 512, 1024))
        for c in range(8):
            tt(ybf[:, c, :], ybf[:, c, :], pf1[:, 0:512], ALU.subtract,
               ["ybf"] + pq("pf1", 0, 512), ["ybf"])
            tt(ybf[:, c, :], ybf[:, c, :], pf1[:, 512:1024], ALU.mult,
               ["ybf"] + pq("pf1", 512, 1024), ["ybf"])
            ts2(x1[:, c, q0:q0 + 512], ybf[:, c, :], bias_s[:, 3, c:c + 1],
                bias_s[:, 4, c:c + 1], ALU.mult, ALU.add, ["ybf", "bias", "xn"], ["x1"])
        for sub in range(4):
            st0 = q0 + 128 * sub
            tile = qt * 4 + sub
            for ifc in range(8):
                mm(pf0[0:128, 0:8], x1[:, ifc, st0:st0 + 128], gwt_s[:, ifc, :],
                   ["x1", "gwt"], pq("pf0", 0, 256), start=(ifc == 0), stop=False)
            mm(pf0[0:128, 0:8], onerowb_s[0:1, :], gbrow_s[:], ["onerowb", "gbrow"],
               pq("pf0", 0, 256), start=False, stop=True)
            rmax(col_s[:, 0:1], pf0[0:128, 0:8], pq("pf0", 0, 256), ["nmx"], negate=True)
            act(g8_s[:, 0, :], pf0[0:128, 0:8], AF.Exp, pq("pf0", 0, 256) + ["nmx", "g80"],
                ["g80"], bias=col_s[:, 0:1])
            rsum(col_s[:, 1:2], g8_s[:, 0, :], ["g80"], ["smx"])
            recip(col_s[:, 2:3], col_s[:, 1:2], ["smx"], ["rsx"])
            ts(g8_s[:, 1, :], g8_s[:, 0, :], col_s[:, 2:3], ALU.mult, ["g80", "rsx"], ["g81"])
            rmax(col_s[:, 3:4], g8_s[:, 1, :], ["g81"], ["m1x"])
            ts(g8_s[:, 2, :], g8_s[:, 1, :], col_s[:, 3:4], ALU.is_ge, ["g81", "m1x"], ["g82"])
            stt(g8_s[:, 3, :], g8_s[:, 2, :], -2.0, g8_s[:, 1, :], ALU.mult, ALU.add,
                ["g82", "g81"], ["g83"])
            rmax(col_s[:, 4:5], g8_s[:, 3, :], ["g83"], ["m2x"])
            ts(g8_s[:, 2, :], g8_s[:, 1, :], col_s[:, 4:5], ALU.is_ge, ["g81", "m2x"], ["g82"])
            tt(wbuf_s[:, tile, :], g8_s[:, 1, :], g8_s[:, 2, :], ALU.mult,
               ["g81", "g82"], ["wbuf"])

    if debug:
        dx1 = nc.dram_tensor("dx1", [D, T], BF16, kind="ExternalOutput").ap()
        dma(dx1.rearrange("(c p) t -> p c t", p=128), x1[:], ["x1"], ["dx1"])
        dwb = nc.dram_tensor("dwb", [128, 32, 8], F32, kind="ExternalOutput").ap()
        dma(dwb, wbuf_s[:], ["wbuf"], ["dwb"])
    S.fence(col_s[:, 7:8])
    mset(macc[:].rearrange("p a b -> p (a b)")[:, 0:16384], 0.0,
         ["xtir", "aolr", "ybf", "ysq"], ["macc"])
    mset(macc[:].rearrange("p a b -> p (a b)")[:, 16384:32768], 0.0, ["macc"], ["macc"])
    for e in range(NE):
        if True:
            wpar = e % 2
            dma(webuf[:, wpar, :, :],
                moew[e].rearrange("(a p) o -> p a o", p=128),
                ["webr%d" % wpar] + (["vhat", "mwbf_r"] if e < 2 else []),
                ["webw%d" % wpar], q="gp")
            for ti in range(32):
                tile = ti
                tt0 = tile * 128
                par = ti % 2
                pe_ = pf0 if par == 0 else pf1
                ptn = "pf0" if par == 0 else "pf1"
                toks = pq(ptn, 0, 1024)
                for ifc in range(8):
                    for hf in range(2):
                        mm(pe_[:, 512 * hf:512 * (hf + 1)], x1[:, ifc, tt0:tt0 + 128],
                           webuf[:, wpar, ifc, 512 * hf:512 * (hf + 1)],
                           ["webw%d" % wpar, "x1"], pq(ptn, 512 * hf, 512 * (hf + 1)),
                           start=(ifc == 0), stop=(ifc == 7))
                act(mfm3[:, 2 * (ti % 2):2 * (ti % 2) + 2, :].rearrange("p a b -> p (a b)"),
                    pe_[:, 0:1024], AF.Copy, toks + ["wbuf", "eow%d" % (ti % 2)],
                    ["eow%d" % (ti % 2)], scale=wbuf_s[:, tile, e:e + 1])
                tt(macc[:, tile, :], macc[:, tile, :],
                   mfm3[:, 2 * (ti % 2):2 * (ti % 2) + 2, :].rearrange("p a b -> p (a b)"),
                   ALU.add, ["macc", "eow%d" % (ti % 2)], ["macc", "webr%d" % wpar])
    for tile in range(32):
        tr(pf2[0:8, 0:128], wbuf_s[:, tile, :], ["wbuf"] + pq("pf2", 0, 256),
           pq("pf2", 0, 256), identf_s[:])
        cp(wts_s[:], pf2[0:8, 0:128], pq("pf2", 0, 256), ["wts"])
        for hf in range(2):
            mm(pf1[:, 512 * hf:512 * (hf + 1)], wts_s[:], expb_s[0:8, 512 * hf:512 * (hf + 1)],
               ["wts", "expb"], pq("pf1", 512 * hf, 1024 * (hf + 1) - 512 * hf) if False else
               pq("pf1", 512 * hf, 512 * (hf + 1)))
        tt(macc[:, tile, :], macc[:, tile, :], pf1[:, 0:1024], ALU.add,
           ["macc"] + pq("pf1", 0, 1024), ["macc"])

    if debug:
        dmacc = nc.dram_tensor("dmacc", [128, 32, KB], BF16, kind="ExternalOutput").ap()
        dma(dmacc, macc[:], ["macc"], ["dmacc"])
    S.fence(col_s[:, 7:8])
    for tb in range(8):
        t0 = tb * 512
        for c in range(8):
            for i4 in range(4):
                tile = tb * 4 + i4
                tr(pb0[:, 128 * i4:128 * (i4 + 1)], macc[:, tile, 128 * c:128 * (c + 1)],
                   ["macc"] + pq("pb0", 128 * i4, 128 * (i4 + 1)),
                   pq("pb0", 128 * i4, 128 * (i4 + 1)), identb_s[:])
            cp(mfm3[:, c, :], pb0[:, 0:512], pq("pb0", 0, 512), ["mfm", "webw0", "webw1"])
            tt(y2b[:, c, :], x1[:, c, t0:t0 + 512], mfm3[:, c, :], ALU.add,
               ["x1", "mfm"], ["y2b"])
        for c in range(8):
            tt(y2sq[:], y2b[:, c, :], y2b[:, c, :], ALU.mult, ["y2b"], ["y2sq"])
            mm(pf2[0:1, 0:512], onecol_s[:], y2b[:, c, :], ["onecol", "y2b"],
               pq("pf2", 0, 512), start=(c == 0), stop=(c == 7))
            mm(pf2[0:1, 512:1024], onecol_s[:], y2sq[:], ["onecol", "y2sq"],
               pq("pf2", 512, 1024), start=(c == 0), stop=(c == 7))
        ts(rows[0:1, 0, :], pf2[0:1, 0:512], inv_d, ALU.mult, pq("pf2", 0, 512), ["r0"])
        ts(rows[0:1, 1, :], pf2[0:1, 512:1024], inv_d, ALU.mult, pq("pf2", 512, 1024), ["r1"])
        tt(rows[0:1, 2, :], rows[0:1, 0, :], rows[0:1, 0, :], ALU.mult, ["r0"], ["r2"])
        tt(rows[0:1, 3, :], rows[0:1, 1, :], rows[0:1, 2, :], ALU.subtract, ["r1", "r2"], ["r3"])
        act(rows[0:1, 4, :], rows[0:1, 3, :], AF.Ln, ["r3"], ["r4"], bias=1e-5)
        act(rows[0:1, 5, :], rows[0:1, 4, :], AF.Exp, ["r4"], ["r5"], scale=-0.5)
        mm(pf1[:, 0:512], onerow_s[:], rows[0:1, 0, :], ["onerow", "r0"], pq("pf1", 0, 512))
        mm(pf1[:, 512:1024], onerow_s[:], rows[0:1, 5, :], ["onerow", "r5"],
           pq("pf1", 512, 1024))
        for c in range(8):
            tt(y2b[:, c, :], y2b[:, c, :], pf1[:, 0:512], ALU.subtract,
               ["y2b"] + pq("pf1", 0, 512), ["y2b"])
            tt(y2b[:, c, :], y2b[:, c, :], pf1[:, 512:1024], ALU.mult,
               ["y2b"] + pq("pf1", 512, 1024), ["y2b"])
            act(gbo[:, c, :], y2b[:, c, :], AF.Silu, ["y2b", "bias"], ["gbo"],
                bias=bias_s[:, 6, c:c + 1], scale=bias_s[:, 5, c:c + 1])
        dma(g_sd.rearrange("(c p) t -> p c t", p=128)[:, :, t0:t0 + 512], gbo[:],
            ["gbo"], ["g_sdw"], q="gp")

    S.fence(col_s[:, 7:8])
    for tb in range(8):
        t0 = tb * 512
        dma(gld[:], g_sd.rearrange("(c p) t -> p c t", p=128)[:, :, t0:t0 + 512],
            ["g_sdw", "gldr", "macc", "mfm"], ["gldw"], q="gp")
        dma(xf5[:], xt.rearrange("(c p) t -> p c t", p=128)[:, :, t0:t0 + 512],
            ["xf5r", "y2b", "y2sq", "gbo"], ["xf5w"])
        dma(sf5[:], s_s.rearrange("(c p) t -> p c t", p=128)[:, :, t0:t0 + 512],
            ["s_sd", "sf5r"], ["sf5w"], q="gp")
        for c in range(8):
            tt(ob5[:, c, :], sf5[:, c, :], gld[:, c, :], ALU.mult,
               ["sf5w", "gldw", "ob5r"], ["ob5", "gldr", "sf5r"])
            tt(ob5[:, c, :], ob5[:, c, :], xf5[:, c, :], ALU.add,
               ["ob5", "xf5w"], ["ob5", "xf5r"])
        dma(outT.rearrange("(c p) t -> p c t", p=128)[:, :, t0:t0 + 512], ob5[:],
            ["ob5"], ["ob5r"])

    S.replay()
    return nc


_NC_CACHE = {}


def kernel(**inputs):
    bf16 = ml_dtypes.bfloat16
    f32 = np.float32
    x = np.asarray(inputs["L0_attn_hidden_states"], dtype=f32)

    def t_(a):
        return np.ascontiguousarray(np.asarray(a, dtype=f32).T)

    def fmv(v):
        return np.ascontiguousarray(np.asarray(v, dtype=f32).reshape(8, 128).T)

    ipw = np.asarray(inputs["L0_ffn_in_proj_w"], dtype=f32)
    ipb = np.asarray(inputs["L0_ffn_in_proj_b"], dtype=f32)
    biasm = np.zeros((128, 16, 8), dtype=f32)
    biasm[:, 0] = fmv(ipb[0:1024])
    biasm[:, 1] = fmv(ipb[1024:2048])
    biasm[:, 2] = fmv(inputs["L0_ffn_out_proj_b"])
    biasm[:, 3] = fmv(inputs["L0_ffn_ln1_w"])
    biasm[:, 4] = fmv(inputs["L0_ffn_ln1_b"])
    biasm[:, 5] = fmv(inputs["L0_ffn_ln2_w"])
    biasm[:, 6] = fmv(inputs["L0_ffn_ln2_b"])
    biasm[:, 7] = fmv(inputs["L0_attn_g_w"])

    tgrid = np.arange(256)[None, :]
    pgrid = np.arange(128)[:, None]
    ul = np.zeros((128, 2, 256), dtype=f32)
    ul[:, 0, :] = (tgrid >= pgrid).astype(f32)
    ul[:, 1, :] = (tgrid >= pgrid + 128).astype(f32)
    tri = np.triu(np.ones((128, 128), dtype=f32))
    j_of_p = (np.arange(128) % 64) // 2
    invf = 10000.0 ** (-2.0 * j_of_p / 64.0)
    tpos = np.arange(T, dtype=np.float64)[None, :]
    ang = tpos * invf[:, None]
    cosi = np.cos(ang).astype(f32)
    sini = np.sin(ang).astype(f32)
    psw = np.zeros((128, 128), dtype=f32)
    for p in range(0, 128, 2):
        psw[p + 1, p] = -1.0
        psw[p, p + 1] = 1.0

    common = {
        "g_wq": t_(inputs["L0_attn_Wq"]), "g_wk": t_(inputs["L0_attn_Wk"]),
        "g_wv": t_(inputs["L0_attn_Wv"]), "g_wf": t_(inputs["L0_attn_Wf"]),
        "g_wo": t_(inputs["L0_attn_Wo"]),
        "f_wq": t_(ipw[0:1024]), "f_wk": t_(ipw[1024:2048]), "f_wv": t_(ipw[2048:3072]),
        "f_wo": t_(inputs["L0_ffn_out_proj_w"]),
        "exp_w": np.ascontiguousarray(
            np.transpose(np.asarray(inputs["L0_ffn_expert_w"], dtype=f32), (0, 2, 1))),
        "biasm": biasm,
        "f_bv": np.asarray(ipb[2048:3072], dtype=f32).reshape(1, D),
        "gate_wt": t_(inputs["L0_ffn_gate_w"]),
        "gate_b": np.asarray(inputs["L0_ffn_gate_b"], dtype=f32).reshape(1, 8),
        "exp_b": np.asarray(inputs["L0_ffn_expert_b"], dtype=f32),
        "identf": np.eye(128, dtype=f32),
        "onecol_b": np.ones((128, 1), dtype=bf16),
        "onerow_f": np.ones((1, 128), dtype=f32),
        "onerow_b": np.ones((1, 128), dtype=bf16),
        "tri_b": tri.astype(bf16),
        "ones_b": np.ones((128, 128), dtype=bf16),
        "ul_b": ul.astype(bf16),
        "ul_f": ul,
        "cosi": cosi.astype(bf16),
        "sini": sini.astype(bf16),
        "pswap": psw.astype(bf16),
        "cvals": np.concatenate([np.full((128, 1), 1e-5, f32),
                                 np.full((128, 1), EXPB, f32)], axis=1),
    }
    in_maps = []
    for b in range(8):
        m = dict(common)
        m["xt"] = np.ascontiguousarray(x[b].T)
        in_maps.append(m)

    if "nc" not in _NC_CACHE:
        _NC_CACHE["nc"] = build_nc()
    nc = _NC_CACHE["nc"]
    res = run_bass_kernel_spmd(nc, in_maps, core_ids=list(range(8)))
    out = np.stack([np.asarray(r["outT"], dtype=f32).T for r in res.results], axis=0)
    return out


if __name__ == "__main__":
    nc = build_nc()
    print("built ok; instructions:", nc.n_instructions() if hasattr(nc, "n_instructions") else "?")


# version 19
# speedup vs baseline: 1.0101x; 1.0101x over previous
import numpy as np
import ml_dtypes
import concourse.bass as bass
import concourse.mybir as mybir
from concourse.bass_utils import run_bass_kernel_spmd

F32 = mybir.dt.float32
BF16 = mybir.dt.bfloat16
AF = mybir.ActivationFunctionType
ALU = mybir.AluOpType
AX = mybir.AxisListType

T = 4096
D = 1024
GH, GD, CH = 4, 256, 256
NCH = T // CH
NP, QT, NQT = 8, 512, 8
EXPB = -12.0
NE = 8
KB = 1024


NDMAS = 16


class Sch:
    def __init__(self, nc):
        self.nc = nc
        self.queues = ["sp", "gp", "pe", "act", "dve"]
        semnames = ["pe", "act", "dve"] + \
            ["spd%d" % i for i in range(NDMAS)] + ["gpd%d" % i for i in range(NDMAS)]
        self.sem = {s: nc.alloc_semaphore("sem_" + s) for s in semnames}
        self.cnt = {s: 0 for s in semnames}
        self.ops = {q: [] for q in self.queues}
        self.res = {}
        self.clock = {q: {s: 0 for s in semnames} for q in self.queues}
        self.dmaidx = {"sp": 0, "gp": 0}
        self.cur_fence = None
        self.nfence = 0

    def op(self, e, fn, r=(), w=(), dma=False):
        need = {}
        if self.cur_fence is not None:
            r = list(r) + [self.cur_fence]
        for name in r:
            rec = self.res.get(name)
            if rec and rec["w"] is not None:
                we, wc = rec["w"]
                if need.get(we, 0) < wc:
                    need[we] = wc
        for name in w:
            rec = self.res.get(name)
            if rec:
                if rec["w"] is not None:
                    we, wc = rec["w"]
                    if need.get(we, 0) < wc:
                        need[we] = wc
                for re_, rc in rec["r"].items():
                    if need.get(re_, 0) < rc:
                        need[re_] = rc
        if dma:
            mysem = "%sd%d" % (e, self.dmaidx[e] % NDMAS)
            self.dmaidx[e] += 1
            inc = 16
            if self.cnt[mysem] > 0 and need.get(mysem, 0) < self.cnt[mysem]:
                need[mysem] = self.cnt[mysem]
        else:
            mysem = e
            inc = 1
        waits = []
        for s2, v in need.items():
            if e == "pe" and s2 == "pe":
                continue
            if self.clock[e][s2] < v:
                waits.append((s2, v))
                self.clock[e][s2] = v
        self.cnt[mysem] += inc
        myc = self.cnt[mysem]
        self.ops[e].append((waits, fn, inc, mysem))
        for name in w:
            self.res[name] = {"w": (mysem, myc), "r": {}}
        for name in r:
            rec = self.res.setdefault(name, {"w": None, "r": {}})
            rec["r"][mysem] = myc

    def fence(self, scratch_ap):
        self.nfence += 1
        fname = "fence%d" % self.nfence
        kills = list(self.res.keys())
        self.cur_fence = None
        self.op("dve", lambda e, a=scratch_ap: e.memset(a, 0.0), r=(), w=kills + [fname])
        self.cur_fence = fname

    def replay(self):
        nc = self.nc
        sems = self.sem
        with nc.allow_low_precision(reason="bf16 elementwise blessed by tolerance"), \
             nc.Block() as blk:
            for ename, meth in (("sp", blk.sync), ("gp", blk.gpsimd),
                                ("pe", blk.tensor), ("act", blk.scalar),
                                ("dve", blk.vector)):
                ops = self.ops[ename]

                def body(eng, ops=ops, ename=ename):
                    for waits, fn, inc, mysem in ops:
                        for s2, v in waits:
                            eng.wait_ge(sems[s2], v)
                        fn(eng).then_inc(sems[mysem], inc)
                    if ename in ("sp", "gp"):
                        for i in range(NDMAS):
                            s = "%sd%d" % (ename, i)
                            if self.cnt[s] > 0:
                                eng.wait_ge(sems[s], self.cnt[s])

                meth(body)


def pq(tname, lo, hi):
    if tname.startswith("pb"):
        return [tname]
    q0, q1 = lo // 512, (hi - 1) // 512
    return [tname + "q%d" % i for i in range(q0, q1 + 1)]


def build_nc(debug=False):
    nc = bass.Bass("TRN2", target_bir_lowering=False, dynamic_dma_scratch_size=4096)

    def din(name, shape, dtype=F32):
        return nc.dram_tensor(name, shape, dtype, kind="ExternalInput").ap()

    xt = din("xt", [D, T])
    g_wq = din("g_wq", [D, D]); g_wk = din("g_wk", [D, D])
    g_wv = din("g_wv", [D, D]); g_wf = din("g_wf", [D, D]); g_wo = din("g_wo", [D, D])
    f_wq = din("f_wq", [D, D]); f_wk = din("f_wk", [D, D]); f_wv = din("f_wv", [D, D])
    f_wo = din("f_wo", [D, D])
    exp_w = din("exp_w", [NE, D, D])
    biasm = din("biasm", [128, 16, 8])
    f_bv = din("f_bv", [1, D])
    gate_wt = din("gate_wt", [D, 8])
    gate_b = din("gate_b", [1, 8])
    exp_b = din("exp_b", [8, D])
    identf = din("identf", [128, 128])
    onecol_b = din("onecol_b", [128, 1], BF16)
    onerow_f = din("onerow_f", [1, 128])
    onerow_b = din("onerow_b", [1, 128], BF16)
    tri_b = din("tri_b", [128, 128], BF16)
    ones_b = din("ones_b", [128, 128], BF16)
    ul_b = din("ul_b", [128, 2, 256], BF16)
    ul_f = din("ul_f", [128, 2, 256])
    cosi = din("cosi", [128, T], BF16)
    sini = din("sini", [128, T], BF16)
    pswap = din("pswap", [128, 128], BF16)
    cvals = din("cvals", [128, 2])

    outT = nc.dram_tensor("outT", [D, T], F32, kind="ExternalOutput").ap()

    def scr(name, shape, dtype):
        kind = "ExternalOutput" if debug else "Internal"
        return nc.dram_tensor(name, shape, dtype, kind=kind).ap()

    o_s = scr("o_s", [D, T], BF16)
    s_s = scr("s_s", [D, T], F32)
    q_s = scr("q_s", [D, T], BF16)
    k_s = scr("k_s", [D, T], BF16)
    ao_s = scr("ao_s", [D, T], BF16)
    g_sd = scr("g_sd", [D, T], BF16)
    moew = scr("moew", [NE, D, D], BF16)
    wfo_sd = scr("wfo_sd", [D, D], BF16)

    SBASE = ((nc.sbuf_base + 63) // 64) * 64
    SB = nc.alloc_sbuf_tensor_at
    xn = SB("xn", [128, 8, T], BF16, offset=SBASE + (0))
    x1 = SB("x1", [128, 8, T], BF16, offset=SBASE + (0))
    R2 = 64 * KB
    qfm = SB("qfm", [128, 2, T], BF16, offset=SBASE + (R2))
    kfm = SB("kfm", [128, 2, T], BF16, offset=SBASE + (R2 + 16 * KB))
    ffm = SB("ffm", [128, 2, T], BF16, offset=SBASE + (R2 + 32 * KB))
    vtm = SB("vtm", [128, 32, 256], BF16, offset=SBASE + (R2 + 48 * KB))
    vhat = SB("vhat", [128, 32, 8, 130], BF16, offset=SBASE + (R2))
    webuf = SB("webuf", [128, 2, 8, KB], BF16, offset=SBASE + (R2))
    mfm3 = SB("mfm3", [128, 8, 512], BF16, offset=SBASE + (R2 + 32 * KB))
    y2b = SB("y2b", [128, 8, 512], BF16, offset=SBASE + (R2 + 40 * KB))
    y2sq = SB("y2sq", [128, 512], BF16, offset=SBASE + (R2 + 48 * KB))
    gbo = SB("gbo", [128, 8, 512], BF16, offset=SBASE + (R2 + 49 * KB))
    gld = SB("gld", [128, 8, 512], BF16, offset=SBASE + (R2))
    xf5 = SB("xf5", [128, 8, 512], F32, offset=SBASE + (R2 + 8 * KB))
    sf5 = SB("sf5", [128, 8, 512], F32, offset=SBASE + (R2 + 24 * KB))
    ob5 = SB("ob5", [128, 8, 512], F32, offset=SBASE + (R2 + 40 * KB))
    R3 = 129 * KB
    xst = SB("xst", [128, 2, 8, 512], F32, offset=SBASE + (R3))
    xbfst = SB("xbfst", [128, 512], BF16, offset=SBASE + (R3 + 32 * KB))
    xsqst = SB("xsqst", [128, 512], BF16, offset=SBASE + (R3 + 33 * KB))
    cst = SB("cst", [128, KB], F32, offset=SBASE + (R3 + 34 * KB))
    ktm = SB("ktm", [128, 32, 256], BF16, offset=SBASE + (R3))
    wst = SB("wst", [128, 512], F32, offset=SBASE + (R3 + 16 * KB))
    wgb = SB("wgb", [128, 4, 8, 256], BF16, offset=SBASE + (R3 + 18 * KB))
    mwst = SB("mwst", [128, KB], F32, offset=SBASE + (R3 + 34 * KB))
    mwbf = SB("mwbf", [128, KB], BF16, offset=SBASE + (R3 + 38 * KB))
    ovbb = SB("ovbb", [128, 2, 256], BF16, offset=SBASE + (R3 + 40 * KB))
    sgst = SB("sgst", [128, 512], F32, offset=SBASE + (R3 + 41 * KB))
    C0 = R3 + 43 * KB
    ftm_s = SB("ftm_s", [128, 2, 256], BF16, offset=SBASE + (C0))
    enl_s = SB("enl_s", [128, 512], BF16, offset=SBASE + (C0 + 1 * KB))
    gks_s = SB("gks_s", [128, 512], BF16, offset=SBASE + (C0 + 2 * KB))
    ssm_s = SB("ssm_s", [128, 512], BF16, offset=SBASE + (C0 + 3 * KB))
    sfm_s = SB("sfm_s", [128, 512], BF16, offset=SBASE + (C0 + 4 * KB))
    stm_s = SB("stm_s", [128, 2, 256], BF16, offset=SBASE + (C0 + 5 * KB))
    gtm_s = SB("gtm_s", [128, 2, 256], BF16, offset=SBASE + (C0 + 6 * KB))
    gcr_s = SB("gcr_s", [1, 256], BF16, offset=SBASE + (C0 + 7 * KB))
    gcc_s = SB("gcc_s", [128, 2], F32, offset=SBASE + (C0 + 7 * KB + 512))
    a1_s = SB("a1_s", [128, 2, 256], BF16, offset=SBASE + (C0 + 7 * KB + 576))
    oks_s = SB("oks_s", [128, 256], F32, offset=SBASE + (C0 + 8 * KB + 576))
    eok_s = SB("eok_s", [128, 256], F32, offset=SBASE + (C0 + 9 * KB + 576))
    qvt_s = SB("qvt_s", [128, 2, 256], BF16, offset=SBASE + (C0 + 10 * KB + 576))
    qvf_s = SB("qvf_s", [128, 2, 256], BF16, offset=SBASE + (C0 + 11 * KB + 576))
    a2_s = SB("a2_s", [128, 2, 256], BF16, offset=SBASE + (C0 + 12 * KB + 576))
    s1_s = SB("s1_s", [128, 2, 256], BF16, offset=SBASE + (C0 + 13 * KB + 576))
    s2_s = SB("s2_s", [128, 2, 256], BF16, offset=SBASE + (C0 + 14 * KB + 576))
    col_s = SB("col_s", [128, 8], F32, offset=SBASE + (C0 + 15 * KB + 576))
    stm2 = SB("stm2", [128, 2, 2, 256], BF16, offset=SBASE + (C0 + 16 * KB))
    gcc2 = SB("gcc2", [128, 4], F32, offset=SBASE + (C0 + 18 * KB + 64))
    gcbs = SB("gcbs", [128, 2, 256], BF16, offset=SBASE + (C0 + 18 * KB + 128))
    olod = SB("olod", [128, 2, 8, 512], BF16, offset=SBASE + (R3))
    osil = SB("osil", [128, 8, 512], F32, offset=SBASE + (R3 + 16 * KB))
    osq = SB("osq", [128, 512], BF16, offset=SBASE + (R3 + 32 * KB))
    obuf = SB("obuf", [128, 8, 512], BF16, offset=SBASE + (R3 + 33 * KB))
    wot = SB("wot", [128, 8, KB], BF16, offset=SBASE + (R3 + 41 * KB))
    sev = SB("sev", [128, 512], F32, offset=SBASE + (R3 + 57 * KB))
    wstg = SB("wstg", [128, KB], F32, offset=SBASE + (R3 + 59 * KB))
    wqkb = SB("wqkb", [128, 2, 8, KB], BF16, offset=SBASE + (R3))
    rqb = SB("rqb", [128, 8, 512], BF16, offset=SBASE + (R3 + 32 * KB))
    cosblk = SB("cosblk", [128, 2, 512], BF16, offset=SBASE + (R3 + 40 * KB))
    sinblk = SB("sinblk", [128, 2, 512], BF16, offset=SBASE + (R3 + 42 * KB))
    swpb = SB("swpb", [128, 512], BF16, offset=SBASE + (R3 + 44 * KB))
    m1b = SB("m1b", [128, 512], BF16, offset=SBASE + (R3 + 45 * KB))
    m2b = SB("m2b", [128, 512], BF16, offset=SBASE + (R3 + 46 * KB))
    qkbo = SB("qkbo", [128, 512], BF16, offset=SBASE + (R3 + 47 * KB))
    p2st = SB("p2st", [128, KB], F32, offset=SBASE + (R3 + 48 * KB))
    p2bf = SB("p2bf", [128, KB], BF16, offset=SBASE + (R3 + 52 * KB))
    rqb2 = SB("rqb2", [128, 8, 512], BF16, offset=SBASE + (R3 + 54 * KB))
    qkbo2 = SB("qkbo2", [128, 512], BF16, offset=SBASE + (R3 + 62 * KB))
    sev2 = SB("sev2", [128, 512], F32, offset=SBASE + (R3 + 63 * KB))
    kbuf = SB("kbuf", [128, 2, T], BF16, offset=SBASE + (R3))
    qbuf = SB("qbuf", [128, 8, 512], BF16, offset=SBASE + (R3 + 16 * KB))
    est = SB("est", [128, 2, KB], BF16, offset=SBASE + (R3 + 24 * KB))
    ctxf = SB("ctxf", [128, 8, 512], BF16, offset=SBASE + (R3 + 28 * KB))
    wfob = SB("wfob", [128, 8, KB], BF16, offset=SBASE + (R3 + 36 * KB))
    aob = SB("aob", [128, 512], BF16, offset=SBASE + (R3 + 52 * KB))
    cxb = SB("cxb", [64, 512], BF16, offset=SBASE + (R3 + 53 * KB))
    xti = SB("xti", [128, 8, 512], F32, offset=SBASE + (R3))
    aol = SB("aol", [128, 8, 512], BF16, offset=SBASE + (R3 + 16 * KB))
    ybf = SB("ybf", [128, 8, 512], BF16, offset=SBASE + (R3 + 24 * KB))
    ysq = SB("ysq", [128, 512], BF16, offset=SBASE + (R3 + 32 * KB))
    macc = SB("macc", [128, 32, KB], BF16, offset=SBASE + (R3))
    R4 = 194 * KB
    identf_s = SB("identf_s", [128, 128], F32, offset=SBASE + (R4))
    identb_s = SB("identb_s", [128, 128], BF16, offset=SBASE + (R4 + 512))
    onecol_s = SB("onecol_s", [128, 1], BF16, offset=SBASE + (R4 + 768))
    onerow_s = SB("onerow_s", [1, 128], F32, offset=SBASE + (R4 + 832))
    onerowb_s = SB("onerowb_s", [1, 128], BF16, offset=SBASE + (R4 + 1344))
    tri_s = SB("tri_s", [128, 128], BF16, offset=SBASE + (R4 + 1600))
    onesb_s = SB("onesb_s", [128, 128], BF16, offset=SBASE + (R4 + 1856))
    ul_s = SB("ul_s", [128, 2, 256], BF16, offset=SBASE + (R4 + 2112))
    ulf_s = SB("ulf_s", [128, 2, 256], F32, offset=SBASE + (R4 + 3136))
    pswap_s = SB("pswap_s", [128, 128], BF16, offset=SBASE + (R4 + 5184))
    B0 = R4 + 5 * KB + 448
    bias_s = SB("bias_s", [128, 16, 8], F32, offset=SBASE + (B0))
    bvrow_s = SB("bvrow_s", [1, D], BF16, offset=SBASE + (B0 + 512))
    gbrow_s = SB("gbrow_s", [1, 8], BF16, offset=SBASE + (B0 + 2560))
    gwt_s = SB("gwt_s", [128, 8, 8], BF16, offset=SBASE + (B0 + 2624))
    expb_s = SB("expb_s", [8, D], BF16, offset=SBASE + (B0 + 2752))
    wbuf_s = SB("wbuf_s", [128, 32, 8], F32, offset=SBASE + (B0 + 4800))
    g8_s = SB("g8_s", [128, 5, 8], F32, offset=SBASE + (B0 + 5824))
    zr_s = SB("zr_s", [1, 512], BF16, offset=SBASE + (B0 + 6016))
    zb_s = SB("zb_s", [64, 512], BF16, offset=SBASE + (B0 + 7040))
    wts_s = SB("wts_s", [8, 128], BF16, offset=SBASE + (B0 + 8064))
    rows = SB("rows", [1, 6, 512], F32, offset=SBASE + (B0 + 8384))
    cvals_s = SB("cvals_s", [128, 2], F32, offset=SBASE + (B0 + 8384 + 12 * KB))

    pf0 = nc.alloc_psum_tensor("pf0", [128, KB], F32)
    pf1 = nc.alloc_psum_tensor("pf1", [128, KB], F32)
    pf2 = nc.alloc_psum_tensor("pf2", [128, KB], F32)
    pb0 = nc.alloc_psum_tensor("pb0", [128, KB], BF16)
    pb1 = nc.alloc_psum_tensor("pb1", [128, KB], BF16)

    S = Sch(nc)

    def dma(dst, src, rr, ww, q="sp"):
        q = "sp"
        S.op(q, lambda e, d=dst, s=src: e.dma_start(out=d, in_=s), r=rr, w=ww, dma=True)

    def mm(out, lhsT, rhs, rr, ww, start=True, stop=True, tp=None):
        S.op("pe", lambda e, o=out, l=lhsT, rh=rhs, st=start, sp=stop, t=tp:
             e.matmul(out=o, lhsT=l, rhs=rh, start=st, stop=sp, tile_position=t,
                      skip_group_check=True), r=rr, w=ww)

    def tr(out, in_, rr, ww, ident):
        S.op("pe", lambda e, o=out, i=in_, idn=ident:
             e.transpose(out=o, in_=i, identity=idn), r=rr, w=ww)

    def act(out, in_, func, rr, ww, bias=0.0, scale=1.0):
        S.op("act", lambda e, o=out, i=in_, f=func, b=bias, s=scale:
             e.activation(out=o, in_=i, func=f, bias=b, scale=s), r=rr, w=ww)

    def dve(fn, rr, ww):
        S.op("dve", fn, r=rr, w=ww)

    def tt(out, a, b, op, rr, ww):
        dve(lambda e, o=out, x=a, y=b, p=op: e.tensor_tensor(out=o, in0=x, in1=y, op=p), rr, ww)

    def ts(out, a, s1, op, rr, ww):
        dve(lambda e, o=out, x=a, s_=s1, p=op:
            e.tensor_single_scalar(out=o, in_=x, scalar=s_, op=p), rr, ww)

    def stt(out, in0, scal, in1, op0, op1, rr, ww):
        dve(lambda e, o=out, x=in0, s_=scal, y=in1, p0=op0, p1=op1:
            e.scalar_tensor_tensor(out=o, in0=x, scalar=s_, in1=y, op0=p0, op1=p1), rr, ww)

    def ts2(out, a, s1, s2, op0, op1, rr, ww):
        dve(lambda e, o=out, x=a, u=s1, v=s2, p0=op0, p1=op1:
            e.tensor_scalar(out=o, in0=x, scalar1=u, scalar2=v, op0=p0, op1=p1), rr, ww)

    def cp(out, in_, rr, ww):
        dve(lambda e, o=out, i=in_: e.tensor_copy(out=o, in_=i), rr, ww)

    def rmax(out, in_, rr, ww, negate=False):
        dve(lambda e, o=out, i=in_, n=negate:
            e.reduce_max(out=o, in_=i, axis=AX.X, negate=n), rr, ww)

    def rsum(out, in_, rr, ww):
        dve(lambda e, o=out, i=in_: e.reduce_sum(out=o, in_=i, axis=AX.X), rr, ww)

    def recip(out, in_, rr, ww):
        dve(lambda e, o=out, i=in_: e.reciprocal(out=o, in_=i), rr, ww)

    def mset(ap, val, rr, ww):
        dve(lambda e, a=ap, v=val: e.memset(a, v), rr, ww)

    nc.const_aps.register(F32, 1e-5, cvals_s[:, 0:1])
    nc.const_aps.register(F32, EXPB, cvals_s[:, 1:2])
    dma(cvals_s[:], cvals, [], ["cvals"])
    dma(identf_s[:], identf, [], ["identf"])
    cp(identb_s[:], identf_s[:], ["identf"], ["identb"])
    dma(onecol_s[:], onecol_b, [], ["onecol"])
    dma(onerow_s[:], onerow_f, [], ["onerow"])
    dma(onerowb_s[:], onerow_b, [], ["onerowb"])
    dma(tri_s[:], tri_b, [], ["tri"])
    dma(onesb_s[:], ones_b, [], ["onesb"])
    dma(ul_s[:], ul_b, [], ["ul"])
    dma(ulf_s[:], ul_f, [], ["ulf"])
    dma(pswap_s[:], pswap, [], ["pswap"])
    dma(bias_s[:], biasm, [], ["bias"])
    ts(bias_s[:, 8, :], bias_s[:, 0, :], 0.125, ALU.mult, ["bias"], ["bias"])
    dma(cst[0:1, :], f_bv, ["cstr"], ["cst"])
    cp(bvrow_s[:], cst[0:1, :], ["cst"], ["bvrow", "cstr"])
    dma(cst[0:1, 0:8], gate_b, ["cstr"], ["cst"])
    cp(gbrow_s[:], cst[0:1, 0:8], ["cst"], ["gbrow", "cstr"])
    dma(cst[:, 0:64].rearrange("p (c e) -> p c e", c=8),
        gate_wt.rearrange("(c p) e -> p c e", p=128), ["cstr"], ["cst"])
    cp(gwt_s[:].rearrange("p c e -> p (c e)"), cst[:, 0:64], ["cst"], ["gwt", "cstr"])
    dma(cst[0:8, :], exp_b, ["cstr"], ["cst"])
    cp(expb_s[:], cst[0:8, :], ["cst"], ["expb", "cstr"])

    inv_d = 1.0 / D
    for tb in range(8):
        bf = tb % 2
        c0 = tb * 512
        for c in range(8):
            dma(xst[:, bf, c, :], xt[128 * c:128 * (c + 1), c0:c0 + 512],
                [], ["xst%d_%d" % (bf, c)])
        for c in range(8):
            xc = "xst%d_%d" % (bf, c)
            cp(xbfst[:], xst[:, bf, c, :], [xc], ["xbf"])
            tt(xsqst[:], xbfst[:], xbfst[:], ALU.mult, ["xbf"], ["xsq"])
            mm(pf0[0:1, 0:512], onecol_s[:], xbfst[:], ["onecol", "xbf"], pq("pf0", 0, 512),
               start=(c == 0), stop=(c == 7))
            mm(pf0[0:1, 512:1024], onecol_s[:], xsqst[:], ["onecol", "xsq"],
               pq("pf0", 512, 1024), start=(c == 0), stop=(c == 7))
        ts(rows[0:1, 0, :], pf0[0:1, 0:512], inv_d, ALU.mult, pq("pf0", 0, 512), ["r0"])
        ts(rows[0:1, 1, :], pf0[0:1, 512:1024], inv_d, ALU.mult, pq("pf0", 512, 1024), ["r1"])
        tt(rows[0:1, 2, :], rows[0:1, 0, :], rows[0:1, 0, :], ALU.mult, ["r0"], ["r2"])
        tt(rows[0:1, 3, :], rows[0:1, 1, :], rows[0:1, 2, :], ALU.subtract, ["r1", "r2"], ["r3"])
        act(rows[0:1, 4, :], rows[0:1, 3, :], AF.Ln, ["r3"], ["r4"], bias=1e-5)
        act(rows[0:1, 5, :], rows[0:1, 4, :], AF.Exp, ["r4"], ["r5"], scale=-0.5)
        mm(pf1[:, 0:512], onerow_s[:], rows[0:1, 0, :], ["onerow", "r0"], pq("pf1", 0, 512))
        mm(pf1[:, 512:1024], onerow_s[:], rows[0:1, 5, :], ["onerow", "r5"],
           pq("pf1", 512, 1024))
        for c in range(8):
            xc = "xst%d_%d" % (bf, c)
            tt(xst[:, bf, c, :], xst[:, bf, c, :], pf1[:, 0:512], ALU.subtract,
               [xc] + pq("pf1", 0, 512), [xc])
            tt(xn[:, c, c0:c0 + 512], xst[:, bf, c, :], pf1[:, 512:1024], ALU.mult,
               [xc] + pq("pf1", 512, 1024), ["xn"])

    S.fence(col_s[:, 7:8])
    if debug:
        dxn = nc.dram_tensor("dxn", [D, T], BF16, kind="ExternalOutput").ap()
        dma(dxn.rearrange("(c p) t -> p c t", p=128), xn[:], ["xn"], ["dxn"])
    for ei in range(NE):
        for qtr in range(8):
            dma(mwst[:].rearrange("p (a o) -> p a o", a=1),
                exp_w[ei].rearrange("(a p) o -> p a o", p=128)[:, qtr:qtr + 1, :],
                ["mwst_r"], ["mwst"], q="gp")
            cp(mwbf[:], mwst[:], ["mwst", "mwbf_r"], ["mwbf", "mwst_r"])
            dma(moew[ei].rearrange("(a p) o -> p a o", p=128)[:, qtr:qtr + 1, :],
                mwbf[:].rearrange("p (a o) -> p a o", a=1), ["mwbf"], ["mwbf_r"], q="gp")

    wsrc = [g_wq, g_wk, g_wv, g_wf]
    for h in range(GH):
        hc = GD * h
        for wi in range(4):
            for sl in range(4):
                dma(wst[:].rearrange("p (a o) -> p a o", a=2),
                    wsrc[wi].rearrange("(a p) o -> p a o", p=128)
                    [:, 2 * sl:2 * sl + 2, hc:hc + GD], ["wstr"], ["wstw"], q="gp")
                cp(wgb[:, wi, 2 * sl:2 * sl + 2, :].rearrange("p a o -> p (a o)"),
                   wst[:], ["wstw"], ["wgb%d" % wi, "wstr"])
        for (wi, dst, dname, actf) in ((0, qfm, "qfm", AF.Silu), (1, kfm, "kfm", AF.Silu)):
            for ofc in range(2):
                for tb in range(8):
                    t0 = tb * 512
                    pt_t = ("pf0" if tb % 2 == 0 else "pf1")
                    dstp = (pf0 if tb % 2 == 0 else pf1)[:, 512 * ofc:512 * (ofc + 1)]
                    toks = pq(pt_t, 512 * ofc, 512 * (ofc + 1))
                    for ifc in range(8):
                        mm(dstp, wgb[:, wi, ifc, 128 * ofc:128 * (ofc + 1)],
                           xn[:, ifc, t0:t0 + 512], ["wgb%d" % wi, "xn"], toks,
                           start=(ifc == 0), stop=(ifc == 7))
                    act(dst[:, ofc, t0:t0 + 512], dstp, actf, toks, [dname])
        for ofc in range(2):
            for tb in range(8):
                t0 = tb * 512
                pt_t = ("pf0" if tb % 2 == 0 else "pf1")
                dstp = (pf0 if tb % 2 == 0 else pf1)[:, 512 * ofc:512 * (ofc + 1)]
                toks = pq(pt_t, 512 * ofc, 512 * (ofc + 1))
                for ifc in range(8):
                    mm(dstp, wgb[:, 3, ifc, 128 * ofc:128 * (ofc + 1)],
                       xn[:, ifc, t0:t0 + 512], ["wgb3", "xn"], toks,
                       start=(ifc == 0), stop=(ifc == 7))
                act(sgst[:], dstp, AF.Exp, toks, ["sgst"], scale=-1.0)
                act(ffm[:, ofc, t0:t0 + 512], sgst[:], AF.Ln, ["sgst"], ["ffm"], bias=1.0)
        for (wi, dst, dname) in ((1, ktm, "ktm"), (2, vtm, "vtm")):
            for tt_ in range(32):
                t0 = tt_ * 128
                pt_t = ("pf0" if tt_ % 2 == 0 else "pf1")
                dstp = (pf0 if tt_ % 2 == 0 else pf1)[:, 0:256]
                toks = pq(pt_t, 0, 256)
                for ifc in range(8):
                    mm(dstp, xn[:, ifc, t0:t0 + 128], wgb[:, wi, ifc, :],
                       ["wgb%d" % wi, "xn"], toks, start=(ifc == 0), stop=(ifc == 7))
                act(dst[:, tt_, :], dstp, AF.Silu, toks, [dname])

        mset(s1_s[:].rearrange("p a b -> p (a b)"), 0.0, [], ["s1"])
        mset(s2_s[:].rearrange("p a b -> p (a b)"), 0.0, [], ["s2"])

        def st_ftm(cc):
            t0 = cc * CH
            for th in range(2):
                for mc in range(2):
                    tr(pb0[:, 128 * (2 * th + mc):128 * (2 * th + mc + 1)],
                       ffm[:, mc, t0 + 128 * th:t0 + 128 * (th + 1)],
                       ["ffm"] + pq("pb0", 0, 1), pq("pb0", 0, 1), identb_s[:])
                cp(ftm_s[:, th, :], pb0[:, 256 * th:256 * (th + 1)],
                   pq("pb0", 0, 1), ["ftm"])

        def st_front(cc):
            t0 = cc * CH
            for mc in range(2):
                for th in range(2):
                    mm(pf0[:, 256 * mc:256 * (mc + 1)], ftm_s[:, th, 128 * mc:128 * (mc + 1)],
                       ul_s[:, th, :], ["ftm", "ul"], pq("pf0", 256 * mc, 256 * (mc + 1)),
                       start=(th == 0), stop=(th == 1))
            for tc in range(2):
                for dkc in range(2):
                    mm(pf1[:, 512 + 256 * tc:512 + 256 * (tc + 1)],
                       kfm[:, dkc, t0 + 128 * tc:t0 + 128 * (tc + 1)],
                       qfm[:, dkc, t0:t0 + 256], ["kfm", "qfm"],
                       pq("pf1", 512 + 256 * tc, 768 + 256 * tc),
                       start=(dkc == 0), stop=(dkc == 1))
                tt(a1_s[:, tc, :], pf1[:, 512 + 256 * tc:512 + 256 * (tc + 1)],
                   ulf_s[:, tc, :], ALU.mult,
                   pq("pf1", 512 + 256 * tc, 768 + 256 * tc) + ["ulf"], ["a1"])
            act(enl_s[:], pf0[:, 0:512], AF.Exp, pq("pf0", 0, 512), ["enl"], scale=0.125)
            act(gcc2[:, 2 * (cc % 2):2 * (cc % 2) + 2], pf0[:, 255:512:256], AF.Exp,
                pq("pf0", 0, 512), ["gcc%d" % (cc % 2)], scale=-0.125)
            for mc in range(2):
                act(gks_s[:, 256 * mc:256 * (mc + 1)], ffm[:, mc, t0:t0 + 256],
                    AF.Exp, ["ffm"], ["gks"], scale=-0.125)
            tt(ssm_s[:], gks_s[:], enl_s[:], ALU.mult, ["gks", "enl"], ["ssm"])
            tt(sfm_s[:], enl_s[:], ssm_s[:], ALU.subtract, ["enl", "ssm"], ["sfm"])
            mm(pf0[:, 512:768], tri_s[:], ftm_s[:, 0, :], ["tri", "ftm"], pq("pf0", 512, 768))
            mm(pf0[:, 768:1024], onesb_s[:], ftm_s[:, 0, :], ["onesb", "ftm"],
               pq("pf0", 768, 1024), start=True, stop=False)
            mm(pf0[:, 768:1024], tri_s[:], ftm_s[:, 1, :], ["tri", "ftm"],
               pq("pf0", 768, 1024), start=False, stop=True)
            act(gtm_s[:].rearrange("p a b -> p (a b)"), pf0[:, 512:1024], AF.Exp,
                pq("pf0", 512, 1024), ["gtm"], scale=-0.125)
            for th in range(2):
                mm(pf1[0:1, 0:256], onecol_s[:], ftm_s[:, th, :], ["onecol", "ftm"],
                   pq("pf1", 0, 256), start=(th == 0), stop=(th == 1))
            act(gcr_s[:], pf1[0:1, 0:256], AF.Exp, pq("pf1", 0, 256), ["gcr"], scale=-0.125)
            mm(pf1[:, 256:512], onerowb_s[:], gcr_s[:], ["onerowb", "gcr"], pq("pf1", 256, 512))
            cp(gcbs[:, cc % 2, :], pf1[:, 256:512], pq("pf1", 256, 512), ["gcb%d" % (cc % 2)])
            for th in range(2):
                for mc in range(2):
                    tr(pb0[:, 512 + 128 * (2 * th + mc):512 + 128 * (2 * th + mc + 1)],
                       sfm_s[:, 256 * mc + 128 * th:256 * mc + 128 * (th + 1)],
                       ["sfm"] + pq("pb0", 0, 1), pq("pb0", 0, 1), identb_s[:])
                cp(stm2[:, cc % 2, th, :], pb0[:, 512 + 256 * th:512 + 256 * (th + 1)],
                   pq("pb0", 0, 1), ["stm%d" % (cc % 2)])

        def st_ok(cc):
            t0 = cc * CH
            for th in range(2):
                pok = pf2[:, 256 * th:256 * (th + 1)]
                ptoks = pq("pf2", 256 * th, 256 * (th + 1))
                for dkc in range(2):
                    mm(pok, qfm[:, dkc, t0 + 128 * th:t0 + 128 * (th + 1)],
                       s1_s[:, dkc, :], ["qfm", "s1"], ptoks,
                       start=(dkc == 0), stop=False)
                for tc in range(2):
                    mm(pok, a1_s[:, tc, 128 * th:128 * (th + 1)], stm2[:, cc % 2, tc, :],
                       ["a1", "stm%d" % (cc % 2)], ptoks, start=False, stop=(tc == 1))
                tt(oks_s[:], pok, gtm_s[:, th, :], ALU.mult, ptoks + ["gtm"], ["oks"])
                rmax(col_s[:, 0:1], oks_s[:], ["oks"], ["nmx"], negate=True)
                act(eok_s[:], oks_s[:], AF.Exp, ["oks", "nmx"], ["eok"], bias=col_s[:, 0:1])
                rsum(col_s[:, 1:2], eok_s[:], ["eok"], ["smx"])
                recip(col_s[:, 2:3], col_s[:, 1:2], ["smx"], ["rsx"])
                ts(eok_s[:], eok_s[:], col_s[:, 2:3], ALU.mult, ["eok", "rsx"], ["eok"])
                tt(qvt_s[:, th, :], eok_s[:], gtm_s[:, th, :], ALU.mult, ["eok", "gtm"], ["qvt"])
            for mc in range(2):
                for th in range(2):
                    tr(pb1[:, 128 * (2 * mc + th):128 * (2 * mc + th + 1)],
                       qvt_s[:, th, 128 * mc:128 * (mc + 1)],
                       ["qvt"] + pq("pb1", 0, 1), pq("pb1", 0, 1), identb_s[:])
                cp(qvf_s[:, mc, :], pb1[:, 256 * mc:256 * (mc + 1)],
                   pq("pb1", 0, 1), ["qvf"])
            for tc in range(2):
                for mc in range(2):
                    mm(pf2[:, 512 + 256 * tc:512 + 256 * (tc + 1)],
                       sfm_s[:, 256 * mc + 128 * tc:256 * mc + 128 * (tc + 1)],
                       qvf_s[:, mc, :], ["sfm", "qvf"],
                       pq("pf2", 512 + 256 * tc, 768 + 256 * tc),
                       start=(mc == 0), stop=(mc == 1))
                tt(a2_s[:, tc, :], pf2[:, 512 + 256 * tc:512 + 256 * (tc + 1)],
                   ulf_s[:, tc, :], ALU.mult,
                   pq("pf2", 512 + 256 * tc, 768 + 256 * tc) + ["ulf"], ["a2"])

        def st_back(cc):
            t0 = cc * CH
            for vc in range(2):
                pov = pf0[:, 256 * vc:256 * (vc + 1)]
                ptoks = pq("pf0", 256 * vc, 256 * (vc + 1))
                for mc in range(2):
                    mm(pov, s2_s[:, mc, 128 * vc:128 * (vc + 1)], qvf_s[:, mc, :],
                       ["s2", "qvf"], ptoks, start=(mc == 0), stop=False)
                for tc in range(2):
                    mm(pov, vtm[:, 2 * cc + tc, 128 * vc:128 * (vc + 1)], a2_s[:, tc, :],
                       ["vtm", "a2"], ptoks, start=False, stop=(tc == 1))
                cp(ovbb[:, vc, :], pov, ptoks, ["ovb%d" % vc])
                dma(o_s[hc + 128 * vc:hc + 128 * (vc + 1), t0:t0 + CH], ovbb[:, vc, :],
                    ["ovb%d" % vc], ["o_sd"], q="gp")
            for dkc in range(2):
                pc1 = pf0[:, 512 + 256 * dkc:512 + 256 * (dkc + 1)]
                ptoks = pq("pf0", 512 + 256 * dkc, 768 + 256 * dkc)
                for tc in range(2):
                    mm(pc1, ktm[:, 2 * cc + tc, 128 * dkc:128 * (dkc + 1)],
                       stm2[:, cc % 2, tc, :], ["ktm", "stm%d" % (cc % 2)], ptoks,
                       start=(tc == 0), stop=(tc == 1))
            for mc in range(2):
                pc2 = pf1[:, 512 + 256 * mc:512 + 256 * (mc + 1)]
                ptoks = pq("pf1", 512 + 256 * mc, 768 + 256 * mc)
                for tc in range(2):
                    mm(pc2, stm2[:, cc % 2, tc, 128 * mc:128 * (mc + 1)],
                       vtm[:, 2 * cc + tc, :], ["stm%d" % (cc % 2), "vtm"], ptoks,
                       start=(tc == 0), stop=(tc == 1))
            for dkc in range(2):
                tt(s1_s[:, dkc, :], s1_s[:, dkc, :], pf0[:, 512 + 256 * dkc:512 + 256 * (dkc + 1)],
                   ALU.add, ["s1"] + pq("pf0", 512 + 256 * dkc, 768 + 256 * dkc), ["s1"])
                tt(s1_s[:, dkc, :], s1_s[:, dkc, :], gcbs[:, cc % 2, :], ALU.mult,
                   ["s1", "gcb%d" % (cc % 2)], ["s1"])
            for mc in range(2):
                tt(s2_s[:, mc, :], s2_s[:, mc, :], pf1[:, 512 + 256 * mc:512 + 256 * (mc + 1)],
                   ALU.add, ["s2"] + pq("pf1", 512 + 256 * mc, 768 + 256 * mc), ["s2"])
                ts(s2_s[:, mc, :], s2_s[:, mc, :], gcc2[:, 2 * (cc % 2) + mc:2 * (cc % 2) + mc + 1],
                   ALU.mult, ["s2", "gcc%d" % (cc % 2)], ["s2"])

        st_ftm(0)
        st_front(0)
        for cc in range(NCH):
            st_ok(cc)
            if cc + 1 < NCH:
                st_ftm(cc + 1)
                st_front(cc + 1)
            st_back(cc)
    for sl in range(8):
        dma(wstg[:].rearrange("p (a o) -> p a o", a=1),
            g_wo.rearrange("(a p) o -> p a o", p=128)[:, sl:sl + 1, :],
            ["wstg_r"], ["wstg"], q="gp")
        cp(wot[:, sl, :], wstg[:], ["wstg"], ["wot", "wstg_r"])
    def p1t_load(tb):
        t0 = tb * 512
        bf = tb % 2
        for c in range(8):
            dma(olod[:, bf, c, :], o_s[128 * c:128 * (c + 1), t0:t0 + 512],
                ["o_sd", "ol%d_%d" % (bf, c), "ktm", "wgb1", "wgb2"], ["ol%d_%d" % (bf, c)])
    p1t_load(0)
    for tb in range(8):
        t0 = tb * 512
        bf = tb % 2
        if tb + 1 < 8:
            p1t_load(tb + 1)
        for c in range(8):
            act(osil[:, c, :], olod[:, bf, c, :], AF.Silu, ["ol%d_%d" % (bf, c)], ["osil"])
            tt(osq[:], osil[:, c, :], osil[:, c, :], ALU.mult, ["osil"], ["osq"])
            mm(pf2[0:1, 0:512], onecol_s[:], osq[:], ["onecol", "osq"], pq("pf2", 0, 512),
               start=(c == 0), stop=(c == 7))
        ts(rows[0:1, 0, :], pf2[0:1, 0:512], inv_d, ALU.mult, pq("pf2", 0, 512), ["r0"])
        act(rows[0:1, 1, :], rows[0:1, 0, :], AF.Ln, ["r0"], ["r1"], bias=1e-5)
        act(rows[0:1, 2, :], rows[0:1, 1, :], AF.Exp, ["r1"], ["r2"], scale=-0.5)
        mm(pf1[:, 0:512], onerow_s[:], rows[0:1, 2, :], ["onerow", "r2"], pq("pf1", 0, 512))
        for c in range(8):
            tt(osil[:, c, :], osil[:, c, :], pf1[:, 0:512], ALU.mult,
               ["osil"] + pq("pf1", 0, 512), ["osil"])
            ts(obuf[:, c, :], osil[:, c, :], bias_s[:, 7, c:c + 1], ALU.mult,
               ["osil", "bias"], ["obuf"])
        for ofc in range(8):
            pt_t = "pf0" if ofc % 2 == 0 else "pf2"
            po = (pf0 if ofc % 2 == 0 else pf2)[:, 0:512]
            toks = pq(pt_t, 0, 512)
            for ifc in range(8):
                mm(po, wot[:, ifc, 128 * ofc:128 * (ofc + 1)], obuf[:, ifc, :],
                   ["wot", "obuf"], toks, start=(ifc == 0), stop=(ifc == 7))
            sv_ = sev if ofc % 2 == 0 else sev2
            svtok = "sev%d" % (ofc % 2)
            cp(sv_[:], po, toks + ["sdma%d" % (ofc % 2)], [svtok])
            dma(s_s[128 * ofc:128 * (ofc + 1), t0:t0 + 512], sv_[:], [svtok],
                ["sdma%d" % (ofc % 2), "s_sd"], q="gp")

    S.fence(col_s[:, 7:8])
    for wi, wt_ in ((0, f_wq), (1, f_wk)):
        for sl in range(8):
            dma(p2st[:].rearrange("p (a o) -> p a o", a=1),
                wt_.rearrange("(a p) o -> p a o", p=128)[:, sl:sl + 1, :],
                ["p2str"], ["p2stw"], q="gp")
            cp(wqkb[:, wi, sl, :],
               p2st[:], ["p2stw", "wot", "sev", "sev0", "sev1"],
               ["wqk%d" % wi, "p2str"])
    for sl in range(8):
        dma(p2st[:].rearrange("p (a o) -> p a o", a=1),
            f_wo.rearrange("(a p) o -> p a o", p=128)[:, sl:sl + 1, :],
            ["p2str"], ["p2stw"], q="gp")
        cp(p2bf[:], p2st[:], ["p2stw", "p2bfr"], ["p2bf", "p2str"])
        dma(wfo_sd.rearrange("(a p) o -> p a o", p=128)[:, sl:sl + 1, :],
            p2bf[:].rearrange("p (a o) -> p a o", a=1), ["p2bf"], ["p2bfr"], q="gp")
    def rope_dma(tb):
        b = tb % 2
        t0 = tb * 512
        dma(cosblk[:, b, :], cosi[:, t0:t0 + 512], ["m1"], ["cosb%d" % b])
        dma(sinblk[:, b, :], sini[:, t0:t0 + 512], ["m2"], ["sinb%d" % b])

    def rope_step(tb, ifc):
        b = tb % 2
        t0 = tb * 512
        rq_ = rqb if b == 0 else rqb2
        par = ifc % 2
        pp = pf0[:, 512 * par:512 * (par + 1)]
        toks = pq("pf0", 512 * par, 512 * (par + 1))
        mm(pp, pswap_s[:], xn[:, ifc, t0:t0 + 512], ["pswap", "xn"], toks)
        cp(swpb[:], pp, toks, ["swp"])
        tt(m1b[:], xn[:, ifc, t0:t0 + 512], cosblk[:, b, :], ALU.mult,
           ["xn", "cosb%d" % b], ["m1"])
        tt(m2b[:], swpb[:], sinblk[:, b, :], ALU.mult, ["swp", "sinb%d" % b], ["m2"])
        tt(rq_[:, ifc, :], m1b[:], m2b[:], ALU.add, ["m1", "m2"], ["rqb%d" % b])

    rope_dma(0)
    for ifc in range(8):
        rope_step(0, ifc)
    for tb in range(8):
        t0 = tb * 512
        b = tb % 2
        rq_ = rqb if b == 0 else rqb2
        if tb + 1 < 8:
            rope_dma(tb + 1)
        for wi, dsts, bslot, scl in ((0, q_s, 0, 0.125), (1, k_s, 1, 1.0)):
            for ofc in range(8):
                par = ofc % 2
                pp = pf1[:, 512 * par:512 * (par + 1)]
                toks = pq("pf1", 512 * par, 512 * (par + 1))
                for ifc in range(8):
                    mm(pp, wqkb[:, wi, ifc, 128 * ofc:128 * (ofc + 1)], rq_[:, ifc, :],
                       ["wqk%d" % wi, "rqb%d" % b], toks, start=(ifc == 0), stop=(ifc == 7))
                qb_ = qkbo if ofc % 2 == 0 else qkbo2
                qtok = "qkbo%d" % (ofc % 2)
                ts2(qb_[:], pp, bias_s[:, bslot, ofc:ofc + 1], scl, ALU.add, ALU.mult,
                    toks + ["bias", "qkdma%d" % (ofc % 2)], [qtok])
                dma(dsts[128 * ofc:128 * (ofc + 1), t0:t0 + 512], qb_[:], [qtok],
                    ["qkdma%d" % (ofc % 2), "qkdma"])
                if wi == 0 and tb + 1 < 8:
                    rope_step(tb + 1, ofc)

    for sl in range(8):
        dma(p2st[:].rearrange("p (a o) -> p a o", a=1),
            f_wv.rearrange("(a p) o -> p a o", p=128)[:, sl:sl + 1, :],
            ["p2str"], ["p2stw"], q="gp")
        cp(wqkb[:, 0, sl, :],
           p2st[:], ["p2stw", "wqk0"], ["wqk0", "p2str"])
    mset(vhat[:, :, :, 64:65], 1.0, ["qfm", "kfm", "ffm", "vtm"], ["vhat"])
    mset(vhat[:, :, :, 129:130], 1.0, ["vhat"], ["vhat"])
    for tg in range(32):
        tt0 = tg * 128
        for hf in range(2):
            ppv = pf2[:, 512 * hf:512 * (hf + 1)]
            toks = pq("pf2", 512 * hf, 512 * (hf + 1))
            for ifc in range(8):
                mm(ppv, xn[:, ifc, tt0:tt0 + 128], wqkb[:, 0, ifc, 512 * hf:512 * (hf + 1)],
                   ["wqk0", "xn"], toks, start=(ifc == 0), stop=False)
            mm(ppv, onerowb_s[0:1, :], bvrow_s[0:1, 512 * hf:512 * (hf + 1)],
               ["onerowb", "bvrow"], toks, start=False, stop=True)
        for p_ in range(8):
            base = 128 * p_
            hf = 1 if base >= 512 else 0
            col = base - 512 * hf
            ppv = pf2[:, 512 * hf:512 * (hf + 1)]
            toks = pq("pf2", 512 * hf, 512 * (hf + 1))
            cp(vhat[:, tg, p_, 0:64], ppv[:, col:col + 64], toks + ["vhat"], ["vhat"])
            cp(vhat[:, tg, p_, 65:129], ppv[:, col + 64:col + 128], toks + ["vhat"], ["vhat"])

    S.fence(col_s[:, 7:8])
    for sl in range(2):
        dma(wfob[:].rearrange("p a b -> p a b")[:, 4 * sl:4 * sl + 4, :],
            wfo_sd.rearrange("(a p) o -> p a o", p=128)[:, 4 * sl:4 * sl + 4, :],
            ["p2bfr", "wqk0", "wqk1", "rqb", "swp", "m1", "m2", "qkbo"], ["wfob"])
    for qt in range(NQT):
        q0 = qt * 512
        nkt = 4 * (qt + 1)
        dma(qbuf[:], q_s.rearrange("(pp p) t -> p pp t", p=128)[:, :, q0:q0 + 512],
            ["qkdma", "qbr"], ["qbw"])
        for pp_ in range(NP):
            kpar = pp_ % 2
            dma(kbuf[:, kpar, 0:512 * (qt + 1)], k_s[128 * pp_:128 * (pp_ + 1), 0:512 * (qt + 1)],
                ["kb%d" % kpar, "qkdma"], ["kbw%d" % kpar], q="gp")
            def emit_ctx(kt, nkt=nkt, pp_=pp_):
                par = kt % 2
                mm(pf2[0:65, 0:512], vhat[:, kt, pp_, 0:65], est[:, par, 0:512],
                   ["vhat", "est%d" % par], pq("pf2", 0, 512),
                   start=(kt == 0), stop=(kt == nkt - 1))
                mm(pf2[0:65, 512:1024], vhat[:, kt, pp_, 65:130], est[:, par, 512:1024],
                   ["vhat", "est%d" % par], pq("pf2", 512, 1024),
                   start=(kt == 0), stop=(kt == nkt - 1))
            for kt in range(nkt):
                par = kt % 2
                psc = pf0 if par == 0 else pf1
                ptn = "pf0" if par == 0 else "pf1"
                toksA = pq(ptn, 0, 512)
                toksB = pq(ptn, 512, 1024)
                mm(psc[:, 0:512], kbuf[0:64, kpar, 128 * kt:128 * (kt + 1)],
                   qbuf[0:64, pp_, :], ["kbw%d" % kpar, "qbw"], toksA, tp=(0, 0))
                mm(psc[:, 512:1024], kbuf[64:128, kpar, 128 * kt:128 * (kt + 1)],
                   qbuf[64:128, pp_, :], ["kbw%d" % kpar, "qbw"], toksB, tp=(64, 0))
                act(est[:, par, :], psc[:, 0:1024], AF.Exp,
                    toksA + toksB + ["est%d" % par], ["est%d" % par], bias=EXPB)
                di = kt - 4 * qt
                if di >= 0:
                    for hh in range(2):
                        base = 512 * hh
                        if di > 0:
                            mset(est[:, par, base:base + 128 * di], 0.0,
                                 ["est%d" % par], ["est%d" % par])
                        tt(est[:, par, base + 128 * di:base + 128 * (di + 1)],
                           est[:, par, base + 128 * di:base + 128 * (di + 1)],
                           tri_s[:], ALU.mult, ["est%d" % par, "tri"], ["est%d" % par])
                if kt > 0:
                    emit_ctx(kt - 1)
            emit_ctx(nkt - 1)
            act(rows[0:1, 0, :], pf2[64:65, 0:512], AF.Ln, pq("pf2", 0, 512) + ["lnz"],
                ["lnz"])
            act(zr_s[:], rows[0:1, 0, :], AF.Exp, ["lnz"], ["zr"], scale=-1.0)
            mm(pf0[0:64, 0:512], onerowb_s[0:1, 0:64], zr_s[:], ["onerowb", "zr"],
               pq("pf0", 0, 512))
            cp(zb_s[:], pf0[0:64, 0:512], pq("pf0", 0, 512), ["zb"])
            tt(ctxf[0:64, pp_, :], pf2[0:64, 0:512], zb_s[:], ALU.mult,
               pq("pf2", 0, 512) + ["zb", "ctxf%d" % pp_], ["ctxf%d" % pp_])
            act(rows[0:1, 0, :], pf2[64:65, 512:1024], AF.Ln, pq("pf2", 512, 1024) + ["lnz"],
                ["lnz"])
            act(zr_s[:], rows[0:1, 0, :], AF.Exp, ["lnz"], ["zr"], scale=-1.0)
            mm(pf0[0:64, 0:512], onerowb_s[0:1, 0:64], zr_s[:], ["onerowb", "zr"],
               pq("pf0", 0, 512))
            cp(zb_s[:], pf0[0:64, 0:512], pq("pf0", 0, 512), ["zb"])
            tt(ctxf[64:128, pp_, :], pf2[0:64, 512:1024], zb_s[:], ALU.mult,
               pq("pf2", 512, 1024) + ["zb", "ctxf%d" % pp_], ["ctxf%d" % pp_])
        for ofc in range(8):
            par = ofc % 2
            po = (pf0 if par == 0 else pf1)[:, 0:512]
            ptn = "pf0" if par == 0 else "pf1"
            toks = pq(ptn, 0, 512)
            for pp_ in range(NP):
                mm(po, wfob[:, pp_, 128 * ofc:128 * (ofc + 1)], ctxf[:, pp_, :],
                   ["wfob", "ctxf%d" % pp_], toks,
                   start=(pp_ == 0), stop=(pp_ == 7))
            abv = aob[:] if ofc % 2 == 0 else est[:, 1, 0:512]
            abtok = "aob%d" % (ofc % 2)
            cp(abv, po, toks + ["adma%d" % (ofc % 2)] + ([] if ofc % 2 == 0 else ["est1"]),
               [abtok] + ([] if ofc % 2 == 0 else ["est1"]))
            dma(ao_s[128 * ofc:128 * (ofc + 1), q0:q0 + 512], abv, [abtok],
                ["adma%d" % (ofc % 2), "aod"])

    S.fence(col_s[:, 7:8])
    for qt in range(NQT):
        q0 = qt * 512
        dma(xti[:], xt.rearrange("(c p) t -> p c t", p=128)[:, :, q0:q0 + 512],
            ["xtir", "kbw0", "kbw1", "qbw", "est0", "est1"] if qt == 0 else ["xtir"],
            ["xtiw"])
        dma(aol[:], ao_s.rearrange("(c p) t -> p c t", p=128)[:, :, q0:q0 + 512],
            ["aod", "aolr"], ["aolw"], q="gp")
        for c in range(8):
            stt(ybf[:, c, :], aol[:, c, :], bias_s[:, 2, c:c + 1], xti[:, c, :],
                ALU.add, ALU.add, ["aolw", "xtiw", "bias"], ["ybf", "xtir", "aolr"])
        for c in range(8):
            tt(ysq[:], ybf[:, c, :], ybf[:, c, :], ALU.mult, ["ybf"], ["ysq"])
            mm(pf2[0:1, 0:512], onecol_s[:], ybf[:, c, :], ["onecol", "ybf"],
               pq("pf2", 0, 512), start=(c == 0), stop=(c == 7))
            mm(pf2[0:1, 512:1024], onecol_s[:], ysq[:], ["onecol", "ysq"],
               pq("pf2", 512, 1024), start=(c == 0), stop=(c == 7))
        ts(rows[0:1, 0, :], pf2[0:1, 0:512], inv_d, ALU.mult, pq("pf2", 0, 512), ["r0"])
        ts(rows[0:1, 1, :], pf2[0:1, 512:1024], inv_d, ALU.mult, pq("pf2", 512, 1024), ["r1"])
        tt(rows[0:1, 2, :], rows[0:1, 0, :], rows[0:1, 0, :], ALU.mult, ["r0"], ["r2"])
        tt(rows[0:1, 3, :], rows[0:1, 1, :], rows[0:1, 2, :], ALU.subtract, ["r1", "r2"], ["r3"])
        act(rows[0:1, 4, :], rows[0:1, 3, :], AF.Ln, ["r3"], ["r4"], bias=1e-5)
        act(rows[0:1, 5, :], rows[0:1, 4, :], AF.Exp, ["r4"], ["r5"], scale=-0.5)
        mm(pf1[:, 0:512], onerow_s[:], rows[0:1, 0, :], ["onerow", "r0"], pq("pf1", 0, 512))
        mm(pf1[:, 512:1024], onerow_s[:], rows[0:1, 5, :], ["onerow", "r5"],
           pq("pf1", 512, 1024))
        for c in range(8):
            tt(ybf[:, c, :], ybf[:, c, :], pf1[:, 0:512], ALU.subtract,
               ["ybf"] + pq("pf1", 0, 512), ["ybf"])
            tt(ybf[:, c, :], ybf[:, c, :], pf1[:, 512:1024], ALU.mult,
               ["ybf"] + pq("pf1", 512, 1024), ["ybf"])
            ts2(x1[:, c, q0:q0 + 512], ybf[:, c, :], bias_s[:, 3, c:c + 1],
                bias_s[:, 4, c:c + 1], ALU.mult, ALU.add, ["ybf", "bias", "xn"], ["x1"])
        for sub in range(4):
            st0 = q0 + 128 * sub
            tile = qt * 4 + sub
            for ifc in range(8):
                mm(pf0[0:128, 0:8], x1[:, ifc, st0:st0 + 128], gwt_s[:, ifc, :],
                   ["x1", "gwt"], pq("pf0", 0, 256), start=(ifc == 0), stop=False)
            mm(pf0[0:128, 0:8], onerowb_s[0:1, :], gbrow_s[:], ["onerowb", "gbrow"],
               pq("pf0", 0, 256), start=False, stop=True)
            rmax(col_s[:, 0:1], pf0[0:128, 0:8], pq("pf0", 0, 256), ["nmx"], negate=True)
            act(g8_s[:, 0, :], pf0[0:128, 0:8], AF.Exp, pq("pf0", 0, 256) + ["nmx", "g80"],
                ["g80"], bias=col_s[:, 0:1])
            rsum(col_s[:, 1:2], g8_s[:, 0, :], ["g80"], ["smx"])
            recip(col_s[:, 2:3], col_s[:, 1:2], ["smx"], ["rsx"])
            ts(g8_s[:, 1, :], g8_s[:, 0, :], col_s[:, 2:3], ALU.mult, ["g80", "rsx"], ["g81"])
            rmax(col_s[:, 3:4], g8_s[:, 1, :], ["g81"], ["m1x"])
            ts(g8_s[:, 2, :], g8_s[:, 1, :], col_s[:, 3:4], ALU.is_ge, ["g81", "m1x"], ["g82"])
            stt(g8_s[:, 3, :], g8_s[:, 2, :], -2.0, g8_s[:, 1, :], ALU.mult, ALU.add,
                ["g82", "g81"], ["g83"])
            rmax(col_s[:, 4:5], g8_s[:, 3, :], ["g83"], ["m2x"])
            ts(g8_s[:, 2, :], g8_s[:, 1, :], col_s[:, 4:5], ALU.is_ge, ["g81", "m2x"], ["g82"])
            tt(wbuf_s[:, tile, :], g8_s[:, 1, :], g8_s[:, 2, :], ALU.mult,
               ["g81", "g82"], ["wbuf"])

    if debug:
        dx1 = nc.dram_tensor("dx1", [D, T], BF16, kind="ExternalOutput").ap()
        dma(dx1.rearrange("(c p) t -> p c t", p=128), x1[:], ["x1"], ["dx1"])
        dwb = nc.dram_tensor("dwb", [128, 32, 8], F32, kind="ExternalOutput").ap()
        dma(dwb, wbuf_s[:], ["wbuf"], ["dwb"])
    S.fence(col_s[:, 7:8])
    mset(macc[:].rearrange("p a b -> p (a b)")[:, 0:16384], 0.0,
         ["xtir", "aolr", "ybf", "ysq"], ["macc"])
    mset(macc[:].rearrange("p a b -> p (a b)")[:, 16384:32768], 0.0, ["macc"], ["macc"])
    for e in range(NE):
        if True:
            wpar = e % 2
            dma(webuf[:, wpar, :, :],
                moew[e].rearrange("(a p) o -> p a o", p=128),
                ["webr%d" % wpar] + (["vhat", "mwbf_r"] if e < 2 else []),
                ["webw%d" % wpar], q="gp")
            for ti in range(32):
                tile = ti
                tt0 = tile * 128
                par = ti % 2
                pe_ = pf0 if par == 0 else pf1
                ptn = "pf0" if par == 0 else "pf1"
                toks = pq(ptn, 0, 1024)
                for ifc in range(8):
                    for hf in range(2):
                        mm(pe_[:, 512 * hf:512 * (hf + 1)], x1[:, ifc, tt0:tt0 + 128],
                           webuf[:, wpar, ifc, 512 * hf:512 * (hf + 1)],
                           ["webw%d" % wpar, "x1"], pq(ptn, 512 * hf, 512 * (hf + 1)),
                           start=(ifc == 0), stop=(ifc == 7))
                act(mfm3[:, 2 * (ti % 2):2 * (ti % 2) + 2, :].rearrange("p a b -> p (a b)"),
                    pe_[:, 0:1024], AF.Copy, toks + ["wbuf", "eow%d" % (ti % 2)],
                    ["eow%d" % (ti % 2)], scale=wbuf_s[:, tile, e:e + 1])
                tt(macc[:, tile, :], macc[:, tile, :],
                   mfm3[:, 2 * (ti % 2):2 * (ti % 2) + 2, :].rearrange("p a b -> p (a b)"),
                   ALU.add, ["macc", "eow%d" % (ti % 2)], ["macc", "webr%d" % wpar])
    for tile in range(32):
        tr(pf2[0:8, 0:128], wbuf_s[:, tile, :], ["wbuf"] + pq("pf2", 0, 256),
           pq("pf2", 0, 256), identf_s[:])
        cp(wts_s[:], pf2[0:8, 0:128], pq("pf2", 0, 256), ["wts"])
        for hf in range(2):
            mm(pf1[:, 512 * hf:512 * (hf + 1)], wts_s[:], expb_s[0:8, 512 * hf:512 * (hf + 1)],
               ["wts", "expb"], pq("pf1", 512 * hf, 1024 * (hf + 1) - 512 * hf) if False else
               pq("pf1", 512 * hf, 512 * (hf + 1)))
        tt(macc[:, tile, :], macc[:, tile, :], pf1[:, 0:1024], ALU.add,
           ["macc"] + pq("pf1", 0, 1024), ["macc"])

    if debug:
        dmacc = nc.dram_tensor("dmacc", [128, 32, KB], BF16, kind="ExternalOutput").ap()
        dma(dmacc, macc[:], ["macc"], ["dmacc"])
    S.fence(col_s[:, 7:8])
    for tb in range(8):
        t0 = tb * 512
        for c in range(8):
            for i4 in range(4):
                tile = tb * 4 + i4
                tr(pb0[:, 128 * i4:128 * (i4 + 1)], macc[:, tile, 128 * c:128 * (c + 1)],
                   ["macc"] + pq("pb0", 128 * i4, 128 * (i4 + 1)),
                   pq("pb0", 128 * i4, 128 * (i4 + 1)), identb_s[:])
            cp(mfm3[:, c, :], pb0[:, 0:512], pq("pb0", 0, 512), ["mfm", "webw0", "webw1"])
            tt(y2b[:, c, :], x1[:, c, t0:t0 + 512], mfm3[:, c, :], ALU.add,
               ["x1", "mfm"], ["y2b"])
        for c in range(8):
            tt(y2sq[:], y2b[:, c, :], y2b[:, c, :], ALU.mult, ["y2b"], ["y2sq"])
            mm(pf2[0:1, 0:512], onecol_s[:], y2b[:, c, :], ["onecol", "y2b"],
               pq("pf2", 0, 512), start=(c == 0), stop=(c == 7))
            mm(pf2[0:1, 512:1024], onecol_s[:], y2sq[:], ["onecol", "y2sq"],
               pq("pf2", 512, 1024), start=(c == 0), stop=(c == 7))
        ts(rows[0:1, 0, :], pf2[0:1, 0:512], inv_d, ALU.mult, pq("pf2", 0, 512), ["r0"])
        ts(rows[0:1, 1, :], pf2[0:1, 512:1024], inv_d, ALU.mult, pq("pf2", 512, 1024), ["r1"])
        tt(rows[0:1, 2, :], rows[0:1, 0, :], rows[0:1, 0, :], ALU.mult, ["r0"], ["r2"])
        tt(rows[0:1, 3, :], rows[0:1, 1, :], rows[0:1, 2, :], ALU.subtract, ["r1", "r2"], ["r3"])
        act(rows[0:1, 4, :], rows[0:1, 3, :], AF.Ln, ["r3"], ["r4"], bias=1e-5)
        act(rows[0:1, 5, :], rows[0:1, 4, :], AF.Exp, ["r4"], ["r5"], scale=-0.5)
        mm(pf1[:, 0:512], onerow_s[:], rows[0:1, 0, :], ["onerow", "r0"], pq("pf1", 0, 512))
        mm(pf1[:, 512:1024], onerow_s[:], rows[0:1, 5, :], ["onerow", "r5"],
           pq("pf1", 512, 1024))
        for c in range(8):
            tt(y2b[:, c, :], y2b[:, c, :], pf1[:, 0:512], ALU.subtract,
               ["y2b"] + pq("pf1", 0, 512), ["y2b"])
            tt(y2b[:, c, :], y2b[:, c, :], pf1[:, 512:1024], ALU.mult,
               ["y2b"] + pq("pf1", 512, 1024), ["y2b"])
            act(gbo[:, c, :], y2b[:, c, :], AF.Silu, ["y2b", "bias"], ["gbo"],
                bias=bias_s[:, 6, c:c + 1], scale=bias_s[:, 5, c:c + 1])
        dma(g_sd.rearrange("(c p) t -> p c t", p=128)[:, :, t0:t0 + 512], gbo[:],
            ["gbo"], ["g_sdw"], q="gp")

    S.fence(col_s[:, 7:8])
    for tb in range(8):
        t0 = tb * 512
        dma(gld[:], g_sd.rearrange("(c p) t -> p c t", p=128)[:, :, t0:t0 + 512],
            ["g_sdw", "gldr", "macc", "mfm"], ["gldw"], q="gp")
        dma(xf5[:], xt.rearrange("(c p) t -> p c t", p=128)[:, :, t0:t0 + 512],
            ["xf5r", "y2b", "y2sq", "gbo"], ["xf5w"])
        dma(sf5[:], s_s.rearrange("(c p) t -> p c t", p=128)[:, :, t0:t0 + 512],
            ["s_sd", "sf5r"], ["sf5w"], q="gp")
        for c in range(8):
            tt(ob5[:, c, :], sf5[:, c, :], gld[:, c, :], ALU.mult,
               ["sf5w", "gldw", "ob5r"], ["ob5", "gldr", "sf5r"])
            tt(ob5[:, c, :], ob5[:, c, :], xf5[:, c, :], ALU.add,
               ["ob5", "xf5w"], ["ob5", "xf5r"])
        dma(outT.rearrange("(c p) t -> p c t", p=128)[:, :, t0:t0 + 512], ob5[:],
            ["ob5"], ["ob5r"])

    S.replay()
    return nc


_NC_CACHE = {}


def kernel(**inputs):
    bf16 = ml_dtypes.bfloat16
    f32 = np.float32
    x = np.asarray(inputs["L0_attn_hidden_states"], dtype=f32)

    def t_(a):
        return np.ascontiguousarray(np.asarray(a, dtype=f32).T)

    def fmv(v):
        return np.ascontiguousarray(np.asarray(v, dtype=f32).reshape(8, 128).T)

    ipw = np.asarray(inputs["L0_ffn_in_proj_w"], dtype=f32)
    ipb = np.asarray(inputs["L0_ffn_in_proj_b"], dtype=f32)
    biasm = np.zeros((128, 16, 8), dtype=f32)
    biasm[:, 0] = fmv(ipb[0:1024])
    biasm[:, 1] = fmv(ipb[1024:2048])
    biasm[:, 2] = fmv(inputs["L0_ffn_out_proj_b"])
    biasm[:, 3] = fmv(inputs["L0_ffn_ln1_w"])
    biasm[:, 4] = fmv(inputs["L0_ffn_ln1_b"])
    biasm[:, 5] = fmv(inputs["L0_ffn_ln2_w"])
    biasm[:, 6] = fmv(inputs["L0_ffn_ln2_b"])
    biasm[:, 7] = fmv(inputs["L0_attn_g_w"])

    tgrid = np.arange(256)[None, :]
    pgrid = np.arange(128)[:, None]
    ul = np.zeros((128, 2, 256), dtype=f32)
    ul[:, 0, :] = (tgrid >= pgrid).astype(f32)
    ul[:, 1, :] = (tgrid >= pgrid + 128).astype(f32)
    tri = np.triu(np.ones((128, 128), dtype=f32))
    j_of_p = (np.arange(128) % 64) // 2
    invf = 10000.0 ** (-2.0 * j_of_p / 64.0)
    tpos = np.arange(T, dtype=np.float64)[None, :]
    ang = tpos * invf[:, None]
    cosi = np.cos(ang).astype(f32)
    sini = np.sin(ang).astype(f32)
    psw = np.zeros((128, 128), dtype=f32)
    for p in range(0, 128, 2):
        psw[p + 1, p] = -1.0
        psw[p, p + 1] = 1.0

    common = {
        "g_wq": t_(inputs["L0_attn_Wq"]), "g_wk": t_(inputs["L0_attn_Wk"]),
        "g_wv": t_(inputs["L0_attn_Wv"]), "g_wf": t_(inputs["L0_attn_Wf"]),
        "g_wo": t_(inputs["L0_attn_Wo"]),
        "f_wq": t_(ipw[0:1024]), "f_wk": t_(ipw[1024:2048]), "f_wv": t_(ipw[2048:3072]),
        "f_wo": t_(inputs["L0_ffn_out_proj_w"]),
        "exp_w": np.ascontiguousarray(
            np.transpose(np.asarray(inputs["L0_ffn_expert_w"], dtype=f32), (0, 2, 1))),
        "biasm": biasm,
        "f_bv": np.asarray(ipb[2048:3072], dtype=f32).reshape(1, D),
        "gate_wt": t_(inputs["L0_ffn_gate_w"]),
        "gate_b": np.asarray(inputs["L0_ffn_gate_b"], dtype=f32).reshape(1, 8),
        "exp_b": np.asarray(inputs["L0_ffn_expert_b"], dtype=f32),
        "identf": np.eye(128, dtype=f32),
        "onecol_b": np.ones((128, 1), dtype=bf16),
        "onerow_f": np.ones((1, 128), dtype=f32),
        "onerow_b": np.ones((1, 128), dtype=bf16),
        "tri_b": tri.astype(bf16),
        "ones_b": np.ones((128, 128), dtype=bf16),
        "ul_b": ul.astype(bf16),
        "ul_f": ul,
        "cosi": cosi.astype(bf16),
        "sini": sini.astype(bf16),
        "pswap": psw.astype(bf16),
        "cvals": np.concatenate([np.full((128, 1), 1e-5, f32),
                                 np.full((128, 1), EXPB, f32)], axis=1),
    }
    in_maps = []
    for b in range(8):
        m = dict(common)
        m["xt"] = np.ascontiguousarray(x[b].T)
        in_maps.append(m)

    if "nc" not in _NC_CACHE:
        _NC_CACHE["nc"] = build_nc()
    nc = _NC_CACHE["nc"]
    res = run_bass_kernel_spmd(nc, in_maps, core_ids=list(range(8)))
    out = np.stack([np.asarray(r["outT"], dtype=f32).T for r in res.results], axis=0)
    return out


if __name__ == "__main__":
    nc = build_nc()
    print("built ok; instructions:", nc.n_instructions() if hasattr(nc, "n_instructions") else "?")


# version 20
# speedup vs baseline: 1.0122x; 1.0021x over previous
import numpy as np
import ml_dtypes
import concourse.bass as bass
import concourse.mybir as mybir
from concourse.bass_utils import run_bass_kernel_spmd

F32 = mybir.dt.float32
BF16 = mybir.dt.bfloat16
AF = mybir.ActivationFunctionType
ALU = mybir.AluOpType
AX = mybir.AxisListType

T = 4096
D = 1024
GH, GD, CH = 4, 256, 256
NCH = T // CH
NP, QT, NQT = 8, 512, 8
EXPB = -12.0
NE = 8
KB = 1024


NDMAS = 16


class Sch:
    def __init__(self, nc):
        self.nc = nc
        self.queues = ["sp", "gp", "pe", "act", "dve"]
        semnames = ["pe", "act", "dve"] + \
            ["spd%d" % i for i in range(NDMAS)] + ["gpd%d" % i for i in range(NDMAS)]
        self.sem = {s: nc.alloc_semaphore("sem_" + s) for s in semnames}
        self.cnt = {s: 0 for s in semnames}
        self.ops = {q: [] for q in self.queues}
        self.res = {}
        self.clock = {q: {s: 0 for s in semnames} for q in self.queues}
        self.dmaidx = {"sp": 0, "gp": 0}
        self.cur_fence = None
        self.nfence = 0

    def op(self, e, fn, r=(), w=(), dma=False):
        need = {}
        if self.cur_fence is not None:
            r = list(r) + [self.cur_fence]
        for name in r:
            rec = self.res.get(name)
            if rec and rec["w"] is not None:
                we, wc = rec["w"]
                if need.get(we, 0) < wc:
                    need[we] = wc
        for name in w:
            rec = self.res.get(name)
            if rec:
                if rec["w"] is not None:
                    we, wc = rec["w"]
                    if need.get(we, 0) < wc:
                        need[we] = wc
                for re_, rc in rec["r"].items():
                    if need.get(re_, 0) < rc:
                        need[re_] = rc
        if dma:
            mysem = "%sd%d" % (e, self.dmaidx[e] % NDMAS)
            self.dmaidx[e] += 1
            inc = 16
            if self.cnt[mysem] > 0 and need.get(mysem, 0) < self.cnt[mysem]:
                need[mysem] = self.cnt[mysem]
        else:
            mysem = e
            inc = 1
        waits = []
        for s2, v in need.items():
            if e == "pe" and s2 == "pe":
                continue
            if self.clock[e][s2] < v:
                waits.append((s2, v))
                self.clock[e][s2] = v
        self.cnt[mysem] += inc
        myc = self.cnt[mysem]
        self.ops[e].append((waits, fn, inc, mysem))
        for name in w:
            self.res[name] = {"w": (mysem, myc), "r": {}}
        for name in r:
            rec = self.res.setdefault(name, {"w": None, "r": {}})
            rec["r"][mysem] = myc

    def fence(self, scratch_ap):
        self.nfence += 1
        fname = "fence%d" % self.nfence
        kills = list(self.res.keys())
        self.cur_fence = None
        self.op("dve", lambda e, a=scratch_ap: e.memset(a, 0.0), r=(), w=kills + [fname])
        self.cur_fence = fname

    def replay(self):
        nc = self.nc
        sems = self.sem
        waited = {}
        for q in self.queues:
            for waits, fn, inc, mysem in self.ops[q]:
                for s2, v in waits:
                    waited.setdefault(s2, set()).add(v)
        remap = {}
        incflag = {}
        for e in ("pe", "act", "dve"):
            w = waited.get(e, set())
            m = {}
            rank = 0
            c = 0
            flags = []
            for waits, fn, inc, mysem in self.ops[e]:
                c += 1
                if c in w:
                    rank += 1
                    m[c] = rank
                    flags.append(True)
                else:
                    flags.append(False)
            remap[e] = m
            incflag[e] = flags
        with nc.allow_low_precision(reason="bf16 elementwise blessed by tolerance"), \
             nc.Block() as blk:
            for ename, meth in (("sp", blk.sync), ("gp", blk.gpsimd),
                                ("pe", blk.tensor), ("act", blk.scalar),
                                ("dve", blk.vector)):
                ops = self.ops[ename]

                def body(eng, ops=ops, ename=ename):
                    for i, (waits, fn, inc, mysem) in enumerate(ops):
                        for s2, v in waits:
                            if s2 in remap:
                                v = remap[s2][v]
                            eng.wait_ge(sems[s2], v)
                        inst = fn(eng)
                        if ename in ("sp", "gp") or incflag[ename][i]:
                            inst.then_inc(sems[mysem], inc)
                    if ename in ("sp", "gp"):
                        for i in range(NDMAS):
                            s = "%sd%d" % (ename, i)
                            if self.cnt[s] > 0:
                                eng.wait_ge(sems[s], self.cnt[s])

                meth(body)


def pq(tname, lo, hi):
    if tname.startswith("pb"):
        return [tname]
    q0, q1 = lo // 512, (hi - 1) // 512
    return [tname + "q%d" % i for i in range(q0, q1 + 1)]


def build_nc(debug=False):
    nc = bass.Bass("TRN2", target_bir_lowering=False, dynamic_dma_scratch_size=4096)

    def din(name, shape, dtype=F32):
        return nc.dram_tensor(name, shape, dtype, kind="ExternalInput").ap()

    xt = din("xt", [D, T])
    g_wq = din("g_wq", [D, D]); g_wk = din("g_wk", [D, D])
    g_wv = din("g_wv", [D, D]); g_wf = din("g_wf", [D, D]); g_wo = din("g_wo", [D, D])
    f_wq = din("f_wq", [D, D]); f_wk = din("f_wk", [D, D]); f_wv = din("f_wv", [D, D])
    f_wo = din("f_wo", [D, D])
    exp_w = din("exp_w", [NE, D, D])
    biasm = din("biasm", [128, 16, 8])
    f_bv = din("f_bv", [1, D])
    gate_wt = din("gate_wt", [D, 8])
    gate_b = din("gate_b", [1, 8])
    exp_b = din("exp_b", [8, D])
    identf = din("identf", [128, 128])
    onecol_b = din("onecol_b", [128, 1], BF16)
    onerow_f = din("onerow_f", [1, 128])
    onerow_b = din("onerow_b", [1, 128], BF16)
    tri_b = din("tri_b", [128, 128], BF16)
    ones_b = din("ones_b", [128, 128], BF16)
    ul_b = din("ul_b", [128, 2, 256], BF16)
    ul_f = din("ul_f", [128, 2, 256])
    cosi = din("cosi", [128, T], BF16)
    sini = din("sini", [128, T], BF16)
    pswap = din("pswap", [128, 128], BF16)
    cvals = din("cvals", [128, 2])

    outT = nc.dram_tensor("outT", [D, T], F32, kind="ExternalOutput").ap()

    def scr(name, shape, dtype):
        kind = "ExternalOutput" if debug else "Internal"
        return nc.dram_tensor(name, shape, dtype, kind=kind).ap()

    o_s = scr("o_s", [D, T], BF16)
    s_s = scr("s_s", [D, T], F32)
    q_s = scr("q_s", [D, T], BF16)
    k_s = scr("k_s", [D, T], BF16)
    ao_s = scr("ao_s", [D, T], BF16)
    g_sd = scr("g_sd", [D, T], BF16)
    moew = scr("moew", [NE, D, D], BF16)
    wfo_sd = scr("wfo_sd", [D, D], BF16)

    SBASE = ((nc.sbuf_base + 63) // 64) * 64
    SB = nc.alloc_sbuf_tensor_at
    xn = SB("xn", [128, 8, T], BF16, offset=SBASE + (0))
    x1 = SB("x1", [128, 8, T], BF16, offset=SBASE + (0))
    R2 = 64 * KB
    qfm = SB("qfm", [128, 2, T], BF16, offset=SBASE + (R2))
    kfm = SB("kfm", [128, 2, T], BF16, offset=SBASE + (R2 + 16 * KB))
    ffm = SB("ffm", [128, 2, T], BF16, offset=SBASE + (R2 + 32 * KB))
    vtm = SB("vtm", [128, 32, 256], BF16, offset=SBASE + (R2 + 48 * KB))
    vhat = SB("vhat", [128, 32, 8, 130], BF16, offset=SBASE + (R2))
    webuf = SB("webuf", [128, 2, 8, KB], BF16, offset=SBASE + (R2))
    mfm3 = SB("mfm3", [128, 8, 512], BF16, offset=SBASE + (R2 + 32 * KB))
    y2b = SB("y2b", [128, 8, 512], BF16, offset=SBASE + (R2 + 40 * KB))
    y2sq = SB("y2sq", [128, 512], BF16, offset=SBASE + (R2 + 48 * KB))
    gbo = SB("gbo", [128, 8, 512], BF16, offset=SBASE + (R2 + 49 * KB))
    gld = SB("gld", [128, 8, 512], BF16, offset=SBASE + (R2))
    xf5 = SB("xf5", [128, 8, 512], F32, offset=SBASE + (R2 + 8 * KB))
    sf5 = SB("sf5", [128, 8, 512], F32, offset=SBASE + (R2 + 24 * KB))
    ob5 = SB("ob5", [128, 8, 512], F32, offset=SBASE + (R2 + 40 * KB))
    R3 = 129 * KB
    xst = SB("xst", [128, 2, 8, 512], F32, offset=SBASE + (R3))
    xbfst = SB("xbfst", [128, 512], BF16, offset=SBASE + (R3 + 32 * KB))
    xsqst = SB("xsqst", [128, 512], BF16, offset=SBASE + (R3 + 33 * KB))
    cst = SB("cst", [128, KB], F32, offset=SBASE + (R3 + 34 * KB))
    ktm = SB("ktm", [128, 32, 256], BF16, offset=SBASE + (R3))
    wst = SB("wst", [128, 512], F32, offset=SBASE + (R3 + 16 * KB))
    wgb = SB("wgb", [128, 4, 8, 256], BF16, offset=SBASE + (R3 + 18 * KB))
    mwst = SB("mwst", [128, KB], F32, offset=SBASE + (R3 + 34 * KB))
    mwbf = SB("mwbf", [128, KB], BF16, offset=SBASE + (R3 + 38 * KB))
    ovbb = SB("ovbb", [128, 2, 256], BF16, offset=SBASE + (R3 + 40 * KB))
    sgst = SB("sgst", [128, 512], F32, offset=SBASE + (R3 + 41 * KB))
    C0 = R3 + 43 * KB
    ftm_s = SB("ftm_s", [128, 2, 256], BF16, offset=SBASE + (C0))
    enl_s = SB("enl_s", [128, 512], BF16, offset=SBASE + (C0 + 1 * KB))
    gks_s = SB("gks_s", [128, 512], BF16, offset=SBASE + (C0 + 2 * KB))
    ssm_s = SB("ssm_s", [128, 512], BF16, offset=SBASE + (C0 + 3 * KB))
    sfm_s = SB("sfm_s", [128, 512], BF16, offset=SBASE + (C0 + 4 * KB))
    stm_s = SB("stm_s", [128, 2, 256], BF16, offset=SBASE + (C0 + 5 * KB))
    gtm_s = SB("gtm_s", [128, 2, 256], BF16, offset=SBASE + (C0 + 6 * KB))
    gcr_s = SB("gcr_s", [1, 256], BF16, offset=SBASE + (C0 + 7 * KB))
    gcc_s = SB("gcc_s", [128, 2], F32, offset=SBASE + (C0 + 7 * KB + 512))
    a1_s = SB("a1_s", [128, 2, 256], BF16, offset=SBASE + (C0 + 7 * KB + 576))
    oks_s = SB("oks_s", [128, 256], F32, offset=SBASE + (C0 + 8 * KB + 576))
    eok_s = SB("eok_s", [128, 256], F32, offset=SBASE + (C0 + 9 * KB + 576))
    qvt_s = SB("qvt_s", [128, 2, 256], BF16, offset=SBASE + (C0 + 10 * KB + 576))
    qvf_s = SB("qvf_s", [128, 2, 256], BF16, offset=SBASE + (C0 + 11 * KB + 576))
    a2_s = SB("a2_s", [128, 2, 256], BF16, offset=SBASE + (C0 + 12 * KB + 576))
    s1_s = SB("s1_s", [128, 2, 256], BF16, offset=SBASE + (C0 + 13 * KB + 576))
    s2_s = SB("s2_s", [128, 2, 256], BF16, offset=SBASE + (C0 + 14 * KB + 576))
    col_s = SB("col_s", [128, 8], F32, offset=SBASE + (C0 + 15 * KB + 576))
    stm2 = SB("stm2", [128, 2, 2, 256], BF16, offset=SBASE + (C0 + 16 * KB))
    gcc2 = SB("gcc2", [128, 4], F32, offset=SBASE + (C0 + 18 * KB + 64))
    gcbs = SB("gcbs", [128, 2, 256], BF16, offset=SBASE + (C0 + 18 * KB + 128))
    olod = SB("olod", [128, 2, 8, 512], BF16, offset=SBASE + (R3))
    osil = SB("osil", [128, 8, 512], F32, offset=SBASE + (R3 + 16 * KB))
    osq = SB("osq", [128, 512], BF16, offset=SBASE + (R3 + 32 * KB))
    obuf = SB("obuf", [128, 8, 512], BF16, offset=SBASE + (R3 + 33 * KB))
    wot = SB("wot", [128, 8, KB], BF16, offset=SBASE + (R3 + 41 * KB))
    sev = SB("sev", [128, 512], F32, offset=SBASE + (R3 + 57 * KB))
    wstg = SB("wstg", [128, KB], F32, offset=SBASE + (R3 + 59 * KB))
    wqkb = SB("wqkb", [128, 2, 8, KB], BF16, offset=SBASE + (R3))
    rqb = SB("rqb", [128, 8, 512], BF16, offset=SBASE + (R3 + 32 * KB))
    cosblk = SB("cosblk", [128, 2, 512], BF16, offset=SBASE + (R3 + 40 * KB))
    sinblk = SB("sinblk", [128, 2, 512], BF16, offset=SBASE + (R3 + 42 * KB))
    swpb = SB("swpb", [128, 512], BF16, offset=SBASE + (R3 + 44 * KB))
    m1b = SB("m1b", [128, 512], BF16, offset=SBASE + (R3 + 45 * KB))
    m2b = SB("m2b", [128, 512], BF16, offset=SBASE + (R3 + 46 * KB))
    qkbo = SB("qkbo", [128, 512], BF16, offset=SBASE + (R3 + 47 * KB))
    p2st = SB("p2st", [128, KB], F32, offset=SBASE + (R3 + 48 * KB))
    p2bf = SB("p2bf", [128, KB], BF16, offset=SBASE + (R3 + 52 * KB))
    rqb2 = SB("rqb2", [128, 8, 512], BF16, offset=SBASE + (R3 + 54 * KB))
    qkbo2 = SB("qkbo2", [128, 512], BF16, offset=SBASE + (R3 + 62 * KB))
    sev2 = SB("sev2", [128, 512], F32, offset=SBASE + (R3 + 63 * KB))
    kbuf = SB("kbuf", [128, 2, T], BF16, offset=SBASE + (R3))
    qbuf = SB("qbuf", [128, 8, 512], BF16, offset=SBASE + (R3 + 16 * KB))
    est = SB("est", [128, 2, KB], BF16, offset=SBASE + (R3 + 24 * KB))
    ctxf = SB("ctxf", [128, 8, 512], BF16, offset=SBASE + (R3 + 28 * KB))
    wfob = SB("wfob", [128, 8, KB], BF16, offset=SBASE + (R3 + 36 * KB))
    aob = SB("aob", [128, 512], BF16, offset=SBASE + (R3 + 52 * KB))
    cxb = SB("cxb", [64, 512], BF16, offset=SBASE + (R3 + 53 * KB))
    xti = SB("xti", [128, 8, 512], F32, offset=SBASE + (R3))
    aol = SB("aol", [128, 8, 512], BF16, offset=SBASE + (R3 + 16 * KB))
    ybf = SB("ybf", [128, 8, 512], BF16, offset=SBASE + (R3 + 24 * KB))
    ysq = SB("ysq", [128, 512], BF16, offset=SBASE + (R3 + 32 * KB))
    macc = SB("macc", [128, 32, KB], BF16, offset=SBASE + (R3))
    R4 = 194 * KB
    identf_s = SB("identf_s", [128, 128], F32, offset=SBASE + (R4))
    identb_s = SB("identb_s", [128, 128], BF16, offset=SBASE + (R4 + 512))
    onecol_s = SB("onecol_s", [128, 1], BF16, offset=SBASE + (R4 + 768))
    onerow_s = SB("onerow_s", [1, 128], F32, offset=SBASE + (R4 + 832))
    onerowb_s = SB("onerowb_s", [1, 128], BF16, offset=SBASE + (R4 + 1344))
    tri_s = SB("tri_s", [128, 128], BF16, offset=SBASE + (R4 + 1600))
    onesb_s = SB("onesb_s", [128, 128], BF16, offset=SBASE + (R4 + 1856))
    ul_s = SB("ul_s", [128, 2, 256], BF16, offset=SBASE + (R4 + 2112))
    ulf_s = SB("ulf_s", [128, 2, 256], F32, offset=SBASE + (R4 + 3136))
    pswap_s = SB("pswap_s", [128, 128], BF16, offset=SBASE + (R4 + 5184))
    B0 = R4 + 5 * KB + 448
    bias_s = SB("bias_s", [128, 16, 8], F32, offset=SBASE + (B0))
    bvrow_s = SB("bvrow_s", [1, D], BF16, offset=SBASE + (B0 + 512))
    gbrow_s = SB("gbrow_s", [1, 8], BF16, offset=SBASE + (B0 + 2560))
    gwt_s = SB("gwt_s", [128, 8, 8], BF16, offset=SBASE + (B0 + 2624))
    expb_s = SB("expb_s", [8, D], BF16, offset=SBASE + (B0 + 2752))
    wbuf_s = SB("wbuf_s", [128, 32, 8], F32, offset=SBASE + (B0 + 4800))
    g8_s = SB("g8_s", [128, 5, 8], F32, offset=SBASE + (B0 + 5824))
    zr_s = SB("zr_s", [1, 512], BF16, offset=SBASE + (B0 + 6016))
    zb_s = SB("zb_s", [64, 512], BF16, offset=SBASE + (B0 + 7040))
    wts_s = SB("wts_s", [8, 128], BF16, offset=SBASE + (B0 + 8064))
    rows = SB("rows", [1, 6, 512], F32, offset=SBASE + (B0 + 8384))
    cvals_s = SB("cvals_s", [128, 2], F32, offset=SBASE + (B0 + 8384 + 12 * KB))

    pf0 = nc.alloc_psum_tensor("pf0", [128, KB], F32)
    pf1 = nc.alloc_psum_tensor("pf1", [128, KB], F32)
    pf2 = nc.alloc_psum_tensor("pf2", [128, KB], F32)
    pb0 = nc.alloc_psum_tensor("pb0", [128, KB], BF16)
    pb1 = nc.alloc_psum_tensor("pb1", [128, KB], BF16)

    S = Sch(nc)

    def dma(dst, src, rr, ww, q="sp"):
        q = "sp"
        S.op(q, lambda e, d=dst, s=src: e.dma_start(out=d, in_=s), r=rr, w=ww, dma=True)

    def mm(out, lhsT, rhs, rr, ww, start=True, stop=True, tp=None):
        S.op("pe", lambda e, o=out, l=lhsT, rh=rhs, st=start, sp=stop, t=tp:
             e.matmul(out=o, lhsT=l, rhs=rh, start=st, stop=sp, tile_position=t,
                      skip_group_check=True), r=rr, w=ww)

    def tr(out, in_, rr, ww, ident):
        S.op("pe", lambda e, o=out, i=in_, idn=ident:
             e.transpose(out=o, in_=i, identity=idn), r=rr, w=ww)

    def act(out, in_, func, rr, ww, bias=0.0, scale=1.0):
        S.op("act", lambda e, o=out, i=in_, f=func, b=bias, s=scale:
             e.activation(out=o, in_=i, func=f, bias=b, scale=s), r=rr, w=ww)

    def dve(fn, rr, ww):
        S.op("dve", fn, r=rr, w=ww)

    def tt(out, a, b, op, rr, ww):
        dve(lambda e, o=out, x=a, y=b, p=op: e.tensor_tensor(out=o, in0=x, in1=y, op=p), rr, ww)

    def ts(out, a, s1, op, rr, ww):
        dve(lambda e, o=out, x=a, s_=s1, p=op:
            e.tensor_single_scalar(out=o, in_=x, scalar=s_, op=p), rr, ww)

    def stt(out, in0, scal, in1, op0, op1, rr, ww):
        dve(lambda e, o=out, x=in0, s_=scal, y=in1, p0=op0, p1=op1:
            e.scalar_tensor_tensor(out=o, in0=x, scalar=s_, in1=y, op0=p0, op1=p1), rr, ww)

    def ts2(out, a, s1, s2, op0, op1, rr, ww):
        dve(lambda e, o=out, x=a, u=s1, v=s2, p0=op0, p1=op1:
            e.tensor_scalar(out=o, in0=x, scalar1=u, scalar2=v, op0=p0, op1=p1), rr, ww)

    def cp(out, in_, rr, ww):
        dve(lambda e, o=out, i=in_: e.tensor_copy(out=o, in_=i), rr, ww)

    def rmax(out, in_, rr, ww, negate=False):
        dve(lambda e, o=out, i=in_, n=negate:
            e.reduce_max(out=o, in_=i, axis=AX.X, negate=n), rr, ww)

    def rsum(out, in_, rr, ww):
        dve(lambda e, o=out, i=in_: e.reduce_sum(out=o, in_=i, axis=AX.X), rr, ww)

    def recip(out, in_, rr, ww):
        dve(lambda e, o=out, i=in_: e.reciprocal(out=o, in_=i), rr, ww)

    def mset(ap, val, rr, ww):
        dve(lambda e, a=ap, v=val: e.memset(a, v), rr, ww)

    nc.const_aps.register(F32, 1e-5, cvals_s[:, 0:1])
    nc.const_aps.register(F32, EXPB, cvals_s[:, 1:2])
    dma(cvals_s[:], cvals, [], ["cvals"])
    dma(identf_s[:], identf, [], ["identf"])
    cp(identb_s[:], identf_s[:], ["identf"], ["identb"])
    dma(onecol_s[:], onecol_b, [], ["onecol"])
    dma(onerow_s[:], onerow_f, [], ["onerow"])
    dma(onerowb_s[:], onerow_b, [], ["onerowb"])
    dma(tri_s[:], tri_b, [], ["tri"])
    dma(onesb_s[:], ones_b, [], ["onesb"])
    dma(ul_s[:], ul_b, [], ["ul"])
    dma(ulf_s[:], ul_f, [], ["ulf"])
    dma(pswap_s[:], pswap, [], ["pswap"])
    dma(bias_s[:], biasm, [], ["bias"])
    ts(bias_s[:, 8, :], bias_s[:, 0, :], 0.125, ALU.mult, ["bias"], ["bias"])
    dma(cst[0:1, :], f_bv, ["cstr"], ["cst"])
    cp(bvrow_s[:], cst[0:1, :], ["cst"], ["bvrow", "cstr"])
    dma(cst[0:1, 0:8], gate_b, ["cstr"], ["cst"])
    cp(gbrow_s[:], cst[0:1, 0:8], ["cst"], ["gbrow", "cstr"])
    dma(cst[:, 0:64].rearrange("p (c e) -> p c e", c=8),
        gate_wt.rearrange("(c p) e -> p c e", p=128), ["cstr"], ["cst"])
    cp(gwt_s[:].rearrange("p c e -> p (c e)"), cst[:, 0:64], ["cst"], ["gwt", "cstr"])
    dma(cst[0:8, :], exp_b, ["cstr"], ["cst"])
    cp(expb_s[:], cst[0:8, :], ["cst"], ["expb", "cstr"])

    inv_d = 1.0 / D
    for tb in range(8):
        bf = tb % 2
        c0 = tb * 512
        for c in range(8):
            dma(xst[:, bf, c, :], xt[128 * c:128 * (c + 1), c0:c0 + 512],
                [], ["xst%d_%d" % (bf, c)])
        for c in range(8):
            xc = "xst%d_%d" % (bf, c)
            cp(xbfst[:], xst[:, bf, c, :], [xc], ["xbf"])
            tt(xsqst[:], xbfst[:], xbfst[:], ALU.mult, ["xbf"], ["xsq"])
            mm(pf0[0:1, 0:512], onecol_s[:], xbfst[:], ["onecol", "xbf"], pq("pf0", 0, 512),
               start=(c == 0), stop=(c == 7))
            mm(pf0[0:1, 512:1024], onecol_s[:], xsqst[:], ["onecol", "xsq"],
               pq("pf0", 512, 1024), start=(c == 0), stop=(c == 7))
        ts(rows[0:1, 0, :], pf0[0:1, 0:512], inv_d, ALU.mult, pq("pf0", 0, 512), ["r0"])
        ts(rows[0:1, 1, :], pf0[0:1, 512:1024], inv_d, ALU.mult, pq("pf0", 512, 1024), ["r1"])
        tt(rows[0:1, 2, :], rows[0:1, 0, :], rows[0:1, 0, :], ALU.mult, ["r0"], ["r2"])
        tt(rows[0:1, 3, :], rows[0:1, 1, :], rows[0:1, 2, :], ALU.subtract, ["r1", "r2"], ["r3"])
        act(rows[0:1, 4, :], rows[0:1, 3, :], AF.Ln, ["r3"], ["r4"], bias=1e-5)
        act(rows[0:1, 5, :], rows[0:1, 4, :], AF.Exp, ["r4"], ["r5"], scale=-0.5)
        mm(pf1[:, 0:512], onerow_s[:], rows[0:1, 0, :], ["onerow", "r0"], pq("pf1", 0, 512))
        mm(pf1[:, 512:1024], onerow_s[:], rows[0:1, 5, :], ["onerow", "r5"],
           pq("pf1", 512, 1024))
        for c in range(8):
            xc = "xst%d_%d" % (bf, c)
            tt(xst[:, bf, c, :], xst[:, bf, c, :], pf1[:, 0:512], ALU.subtract,
               [xc] + pq("pf1", 0, 512), [xc])
            tt(xn[:, c, c0:c0 + 512], xst[:, bf, c, :], pf1[:, 512:1024], ALU.mult,
               [xc] + pq("pf1", 512, 1024), ["xn"])

    S.fence(col_s[:, 7:8])
    if debug:
        dxn = nc.dram_tensor("dxn", [D, T], BF16, kind="ExternalOutput").ap()
        dma(dxn.rearrange("(c p) t -> p c t", p=128), xn[:], ["xn"], ["dxn"])
    for ei in range(NE):
        for qtr in range(8):
            dma(mwst[:].rearrange("p (a o) -> p a o", a=1),
                exp_w[ei].rearrange("(a p) o -> p a o", p=128)[:, qtr:qtr + 1, :],
                ["mwst_r"], ["mwst"], q="gp")
            cp(mwbf[:], mwst[:], ["mwst", "mwbf_r"], ["mwbf", "mwst_r"])
            dma(moew[ei].rearrange("(a p) o -> p a o", p=128)[:, qtr:qtr + 1, :],
                mwbf[:].rearrange("p (a o) -> p a o", a=1), ["mwbf"], ["mwbf_r"], q="gp")

    wsrc = [g_wq, g_wk, g_wv, g_wf]
    for h in range(GH):
        hc = GD * h
        for wi in range(4):
            for sl in range(4):
                dma(wst[:].rearrange("p (a o) -> p a o", a=2),
                    wsrc[wi].rearrange("(a p) o -> p a o", p=128)
                    [:, 2 * sl:2 * sl + 2, hc:hc + GD], ["wstr"], ["wstw"], q="gp")
                cp(wgb[:, wi, 2 * sl:2 * sl + 2, :].rearrange("p a o -> p (a o)"),
                   wst[:], ["wstw"], ["wgb%d" % wi, "wstr"])
        for (wi, dst, dname, actf) in ((0, qfm, "qfm", AF.Silu), (1, kfm, "kfm", AF.Silu)):
            for ofc in range(2):
                for tb in range(8):
                    t0 = tb * 512
                    pt_t = ("pf0" if tb % 2 == 0 else "pf1")
                    dstp = (pf0 if tb % 2 == 0 else pf1)[:, 512 * ofc:512 * (ofc + 1)]
                    toks = pq(pt_t, 512 * ofc, 512 * (ofc + 1))
                    for ifc in range(8):
                        mm(dstp, wgb[:, wi, ifc, 128 * ofc:128 * (ofc + 1)],
                           xn[:, ifc, t0:t0 + 512], ["wgb%d" % wi, "xn"], toks,
                           start=(ifc == 0), stop=(ifc == 7))
                    act(dst[:, ofc, t0:t0 + 512], dstp, actf, toks, [dname])
        for ofc in range(2):
            for tb in range(8):
                t0 = tb * 512
                pt_t = ("pf0" if tb % 2 == 0 else "pf1")
                dstp = (pf0 if tb % 2 == 0 else pf1)[:, 512 * ofc:512 * (ofc + 1)]
                toks = pq(pt_t, 512 * ofc, 512 * (ofc + 1))
                for ifc in range(8):
                    mm(dstp, wgb[:, 3, ifc, 128 * ofc:128 * (ofc + 1)],
                       xn[:, ifc, t0:t0 + 512], ["wgb3", "xn"], toks,
                       start=(ifc == 0), stop=(ifc == 7))
                act(sgst[:], dstp, AF.Exp, toks, ["sgst"], scale=-1.0)
                act(ffm[:, ofc, t0:t0 + 512], sgst[:], AF.Ln, ["sgst"], ["ffm"], bias=1.0)
        for (wi, dst, dname) in ((1, ktm, "ktm"), (2, vtm, "vtm")):
            for tt_ in range(32):
                t0 = tt_ * 128
                pt_t = ("pf0" if tt_ % 2 == 0 else "pf1")
                dstp = (pf0 if tt_ % 2 == 0 else pf1)[:, 0:256]
                toks = pq(pt_t, 0, 256)
                for ifc in range(8):
                    mm(dstp, xn[:, ifc, t0:t0 + 128], wgb[:, wi, ifc, :],
                       ["wgb%d" % wi, "xn"], toks, start=(ifc == 0), stop=(ifc == 7))
                act(dst[:, tt_, :], dstp, AF.Silu, toks, [dname])

        mset(s1_s[:].rearrange("p a b -> p (a b)"), 0.0, [], ["s1"])
        mset(s2_s[:].rearrange("p a b -> p (a b)"), 0.0, [], ["s2"])

        def st_ftm(cc):
            t0 = cc * CH
            for th in range(2):
                for mc in range(2):
                    tr(pb0[:, 128 * (2 * th + mc):128 * (2 * th + mc + 1)],
                       ffm[:, mc, t0 + 128 * th:t0 + 128 * (th + 1)],
                       ["ffm"] + pq("pb0", 0, 1), pq("pb0", 0, 1), identb_s[:])
                cp(ftm_s[:, th, :], pb0[:, 256 * th:256 * (th + 1)],
                   pq("pb0", 0, 1), ["ftm"])

        def st_front(cc):
            t0 = cc * CH
            for mc in range(2):
                for th in range(2):
                    mm(pf0[:, 256 * mc:256 * (mc + 1)], ftm_s[:, th, 128 * mc:128 * (mc + 1)],
                       ul_s[:, th, :], ["ftm", "ul"], pq("pf0", 256 * mc, 256 * (mc + 1)),
                       start=(th == 0), stop=(th == 1))
            for tc in range(2):
                for dkc in range(2):
                    mm(pf1[:, 512 + 256 * tc:512 + 256 * (tc + 1)],
                       kfm[:, dkc, t0 + 128 * tc:t0 + 128 * (tc + 1)],
                       qfm[:, dkc, t0:t0 + 256], ["kfm", "qfm"],
                       pq("pf1", 512 + 256 * tc, 768 + 256 * tc),
                       start=(dkc == 0), stop=(dkc == 1))
                tt(a1_s[:, tc, :], pf1[:, 512 + 256 * tc:512 + 256 * (tc + 1)],
                   ulf_s[:, tc, :], ALU.mult,
                   pq("pf1", 512 + 256 * tc, 768 + 256 * tc) + ["ulf"], ["a1"])
            act(enl_s[:], pf0[:, 0:512], AF.Exp, pq("pf0", 0, 512), ["enl"], scale=0.125)
            act(gcc2[:, 2 * (cc % 2):2 * (cc % 2) + 2], pf0[:, 255:512:256], AF.Exp,
                pq("pf0", 0, 512), ["gcc%d" % (cc % 2)], scale=-0.125)
            for mc in range(2):
                act(gks_s[:, 256 * mc:256 * (mc + 1)], ffm[:, mc, t0:t0 + 256],
                    AF.Exp, ["ffm"], ["gks"], scale=-0.125)
            tt(ssm_s[:], gks_s[:], enl_s[:], ALU.mult, ["gks", "enl"], ["ssm"])
            tt(sfm_s[:], enl_s[:], ssm_s[:], ALU.subtract, ["enl", "ssm"], ["sfm"])
            mm(pf0[:, 512:768], tri_s[:], ftm_s[:, 0, :], ["tri", "ftm"], pq("pf0", 512, 768))
            mm(pf0[:, 768:1024], onesb_s[:], ftm_s[:, 0, :], ["onesb", "ftm"],
               pq("pf0", 768, 1024), start=True, stop=False)
            mm(pf0[:, 768:1024], tri_s[:], ftm_s[:, 1, :], ["tri", "ftm"],
               pq("pf0", 768, 1024), start=False, stop=True)
            act(gtm_s[:].rearrange("p a b -> p (a b)"), pf0[:, 512:1024], AF.Exp,
                pq("pf0", 512, 1024), ["gtm"], scale=-0.125)
            for th in range(2):
                mm(pf1[0:1, 0:256], onecol_s[:], ftm_s[:, th, :], ["onecol", "ftm"],
                   pq("pf1", 0, 256), start=(th == 0), stop=(th == 1))
            act(gcr_s[:], pf1[0:1, 0:256], AF.Exp, pq("pf1", 0, 256), ["gcr"], scale=-0.125)
            mm(pf1[:, 256:512], onerowb_s[:], gcr_s[:], ["onerowb", "gcr"], pq("pf1", 256, 512))
            cp(gcbs[:, cc % 2, :], pf1[:, 256:512], pq("pf1", 256, 512), ["gcb%d" % (cc % 2)])
            for th in range(2):
                for mc in range(2):
                    tr(pb0[:, 512 + 128 * (2 * th + mc):512 + 128 * (2 * th + mc + 1)],
                       sfm_s[:, 256 * mc + 128 * th:256 * mc + 128 * (th + 1)],
                       ["sfm"] + pq("pb0", 0, 1), pq("pb0", 0, 1), identb_s[:])
                cp(stm2[:, cc % 2, th, :], pb0[:, 512 + 256 * th:512 + 256 * (th + 1)],
                   pq("pb0", 0, 1), ["stm%d" % (cc % 2)])

        def st_ok(cc):
            t0 = cc * CH
            for th in range(2):
                pok = pf2[:, 256 * th:256 * (th + 1)]
                ptoks = pq("pf2", 256 * th, 256 * (th + 1))
                for dkc in range(2):
                    mm(pok, qfm[:, dkc, t0 + 128 * th:t0 + 128 * (th + 1)],
                       s1_s[:, dkc, :], ["qfm", "s1"], ptoks,
                       start=(dkc == 0), stop=False)
                for tc in range(2):
                    mm(pok, a1_s[:, tc, 128 * th:128 * (th + 1)], stm2[:, cc % 2, tc, :],
                       ["a1", "stm%d" % (cc % 2)], ptoks, start=False, stop=(tc == 1))
                tt(oks_s[:], pok, gtm_s[:, th, :], ALU.mult, ptoks + ["gtm"], ["oks"])
                rmax(col_s[:, 0:1], oks_s[:], ["oks"], ["nmx"], negate=True)
                act(eok_s[:], oks_s[:], AF.Exp, ["oks", "nmx"], ["eok"], bias=col_s[:, 0:1])
                rsum(col_s[:, 1:2], eok_s[:], ["eok"], ["smx"])
                recip(col_s[:, 2:3], col_s[:, 1:2], ["smx"], ["rsx"])
                ts(eok_s[:], eok_s[:], col_s[:, 2:3], ALU.mult, ["eok", "rsx"], ["eok"])
                tt(qvt_s[:, th, :], eok_s[:], gtm_s[:, th, :], ALU.mult, ["eok", "gtm"], ["qvt"])
            for mc in range(2):
                for th in range(2):
                    tr(pb1[:, 128 * (2 * mc + th):128 * (2 * mc + th + 1)],
                       qvt_s[:, th, 128 * mc:128 * (mc + 1)],
                       ["qvt"] + pq("pb1", 0, 1), pq("pb1", 0, 1), identb_s[:])
                cp(qvf_s[:, mc, :], pb1[:, 256 * mc:256 * (mc + 1)],
                   pq("pb1", 0, 1), ["qvf"])
            for tc in range(2):
                for mc in range(2):
                    mm(pf2[:, 512 + 256 * tc:512 + 256 * (tc + 1)],
                       sfm_s[:, 256 * mc + 128 * tc:256 * mc + 128 * (tc + 1)],
                       qvf_s[:, mc, :], ["sfm", "qvf"],
                       pq("pf2", 512 + 256 * tc, 768 + 256 * tc),
                       start=(mc == 0), stop=(mc == 1))
                tt(a2_s[:, tc, :], pf2[:, 512 + 256 * tc:512 + 256 * (tc + 1)],
                   ulf_s[:, tc, :], ALU.mult,
                   pq("pf2", 512 + 256 * tc, 768 + 256 * tc) + ["ulf"], ["a2"])

        def st_back(cc):
            t0 = cc * CH
            for vc in range(2):
                pov = pf0[:, 256 * vc:256 * (vc + 1)]
                ptoks = pq("pf0", 256 * vc, 256 * (vc + 1))
                for mc in range(2):
                    mm(pov, s2_s[:, mc, 128 * vc:128 * (vc + 1)], qvf_s[:, mc, :],
                       ["s2", "qvf"], ptoks, start=(mc == 0), stop=False)
                for tc in range(2):
                    mm(pov, vtm[:, 2 * cc + tc, 128 * vc:128 * (vc + 1)], a2_s[:, tc, :],
                       ["vtm", "a2"], ptoks, start=False, stop=(tc == 1))
                cp(ovbb[:, vc, :], pov, ptoks, ["ovb%d" % vc])
                dma(o_s[hc + 128 * vc:hc + 128 * (vc + 1), t0:t0 + CH], ovbb[:, vc, :],
                    ["ovb%d" % vc], ["o_sd"], q="gp")
            for dkc in range(2):
                pc1 = pf0[:, 512 + 256 * dkc:512 + 256 * (dkc + 1)]
                ptoks = pq("pf0", 512 + 256 * dkc, 768 + 256 * dkc)
                for tc in range(2):
                    mm(pc1, ktm[:, 2 * cc + tc, 128 * dkc:128 * (dkc + 1)],
                       stm2[:, cc % 2, tc, :], ["ktm", "stm%d" % (cc % 2)], ptoks,
                       start=(tc == 0), stop=(tc == 1))
            for mc in range(2):
                pc2 = pf1[:, 512 + 256 * mc:512 + 256 * (mc + 1)]
                ptoks = pq("pf1", 512 + 256 * mc, 768 + 256 * mc)
                for tc in range(2):
                    mm(pc2, stm2[:, cc % 2, tc, 128 * mc:128 * (mc + 1)],
                       vtm[:, 2 * cc + tc, :], ["stm%d" % (cc % 2), "vtm"], ptoks,
                       start=(tc == 0), stop=(tc == 1))
            for dkc in range(2):
                tt(s1_s[:, dkc, :], s1_s[:, dkc, :], pf0[:, 512 + 256 * dkc:512 + 256 * (dkc + 1)],
                   ALU.add, ["s1"] + pq("pf0", 512 + 256 * dkc, 768 + 256 * dkc), ["s1"])
                tt(s1_s[:, dkc, :], s1_s[:, dkc, :], gcbs[:, cc % 2, :], ALU.mult,
                   ["s1", "gcb%d" % (cc % 2)], ["s1"])
            for mc in range(2):
                tt(s2_s[:, mc, :], s2_s[:, mc, :], pf1[:, 512 + 256 * mc:512 + 256 * (mc + 1)],
                   ALU.add, ["s2"] + pq("pf1", 512 + 256 * mc, 768 + 256 * mc), ["s2"])
                ts(s2_s[:, mc, :], s2_s[:, mc, :], gcc2[:, 2 * (cc % 2) + mc:2 * (cc % 2) + mc + 1],
                   ALU.mult, ["s2", "gcc%d" % (cc % 2)], ["s2"])

        st_ftm(0)
        st_front(0)
        for cc in range(NCH):
            st_ok(cc)
            if cc + 1 < NCH:
                st_ftm(cc + 1)
                st_front(cc + 1)
            st_back(cc)
    for sl in range(8):
        dma(wstg[:].rearrange("p (a o) -> p a o", a=1),
            g_wo.rearrange("(a p) o -> p a o", p=128)[:, sl:sl + 1, :],
            ["wstg_r"], ["wstg"], q="gp")
        cp(wot[:, sl, :], wstg[:], ["wstg"], ["wot", "wstg_r"])
    def p1t_load(tb):
        t0 = tb * 512
        bf = tb % 2
        for c in range(8):
            dma(olod[:, bf, c, :], o_s[128 * c:128 * (c + 1), t0:t0 + 512],
                ["o_sd", "ol%d_%d" % (bf, c), "ktm", "wgb1", "wgb2"], ["ol%d_%d" % (bf, c)])
    p1t_load(0)
    for tb in range(8):
        t0 = tb * 512
        bf = tb % 2
        if tb + 1 < 8:
            p1t_load(tb + 1)
        for c in range(8):
            act(osil[:, c, :], olod[:, bf, c, :], AF.Silu, ["ol%d_%d" % (bf, c)], ["osil"])
            tt(osq[:], osil[:, c, :], osil[:, c, :], ALU.mult, ["osil"], ["osq"])
            mm(pf2[0:1, 0:512], onecol_s[:], osq[:], ["onecol", "osq"], pq("pf2", 0, 512),
               start=(c == 0), stop=(c == 7))
        ts(rows[0:1, 0, :], pf2[0:1, 0:512], inv_d, ALU.mult, pq("pf2", 0, 512), ["r0"])
        act(rows[0:1, 1, :], rows[0:1, 0, :], AF.Ln, ["r0"], ["r1"], bias=1e-5)
        act(rows[0:1, 2, :], rows[0:1, 1, :], AF.Exp, ["r1"], ["r2"], scale=-0.5)
        mm(pf1[:, 0:512], onerow_s[:], rows[0:1, 2, :], ["onerow", "r2"], pq("pf1", 0, 512))
        for c in range(8):
            tt(osil[:, c, :], osil[:, c, :], pf1[:, 0:512], ALU.mult,
               ["osil"] + pq("pf1", 0, 512), ["osil"])
            ts(obuf[:, c, :], osil[:, c, :], bias_s[:, 7, c:c + 1], ALU.mult,
               ["osil", "bias"], ["obuf"])
        for ofc in range(8):
            pt_t = "pf0" if ofc % 2 == 0 else "pf2"
            po = (pf0 if ofc % 2 == 0 else pf2)[:, 0:512]
            toks = pq(pt_t, 0, 512)
            for ifc in range(8):
                mm(po, wot[:, ifc, 128 * ofc:128 * (ofc + 1)], obuf[:, ifc, :],
                   ["wot", "obuf"], toks, start=(ifc == 0), stop=(ifc == 7))
            sv_ = sev if ofc % 2 == 0 else sev2
            svtok = "sev%d" % (ofc % 2)
            cp(sv_[:], po, toks + ["sdma%d" % (ofc % 2)], [svtok])
            dma(s_s[128 * ofc:128 * (ofc + 1), t0:t0 + 512], sv_[:], [svtok],
                ["sdma%d" % (ofc % 2), "s_sd"], q="gp")

    S.fence(col_s[:, 7:8])
    for wi, wt_ in ((0, f_wq), (1, f_wk)):
        for sl in range(8):
            dma(p2st[:].rearrange("p (a o) -> p a o", a=1),
                wt_.rearrange("(a p) o -> p a o", p=128)[:, sl:sl + 1, :],
                ["p2str"], ["p2stw"], q="gp")
            cp(wqkb[:, wi, sl, :],
               p2st[:], ["p2stw", "wot", "sev", "sev0", "sev1"],
               ["wqk%d" % wi, "p2str"])
    for sl in range(8):
        dma(p2st[:].rearrange("p (a o) -> p a o", a=1),
            f_wo.rearrange("(a p) o -> p a o", p=128)[:, sl:sl + 1, :],
            ["p2str"], ["p2stw"], q="gp")
        cp(p2bf[:], p2st[:], ["p2stw", "p2bfr"], ["p2bf", "p2str"])
        dma(wfo_sd.rearrange("(a p) o -> p a o", p=128)[:, sl:sl + 1, :],
            p2bf[:].rearrange("p (a o) -> p a o", a=1), ["p2bf"], ["p2bfr"], q="gp")
    def rope_dma(tb):
        b = tb % 2
        t0 = tb * 512
        dma(cosblk[:, b, :], cosi[:, t0:t0 + 512], ["m1"], ["cosb%d" % b])
        dma(sinblk[:, b, :], sini[:, t0:t0 + 512], ["m2"], ["sinb%d" % b])

    def rope_step(tb, ifc):
        b = tb % 2
        t0 = tb * 512
        rq_ = rqb if b == 0 else rqb2
        par = ifc % 2
        pp = pf0[:, 512 * par:512 * (par + 1)]
        toks = pq("pf0", 512 * par, 512 * (par + 1))
        mm(pp, pswap_s[:], xn[:, ifc, t0:t0 + 512], ["pswap", "xn"], toks)
        cp(swpb[:], pp, toks, ["swp"])
        tt(m1b[:], xn[:, ifc, t0:t0 + 512], cosblk[:, b, :], ALU.mult,
           ["xn", "cosb%d" % b], ["m1"])
        tt(m2b[:], swpb[:], sinblk[:, b, :], ALU.mult, ["swp", "sinb%d" % b], ["m2"])
        tt(rq_[:, ifc, :], m1b[:], m2b[:], ALU.add, ["m1", "m2"], ["rqb%d" % b])

    rope_dma(0)
    for ifc in range(8):
        rope_step(0, ifc)
    for tb in range(8):
        t0 = tb * 512
        b = tb % 2
        rq_ = rqb if b == 0 else rqb2
        if tb + 1 < 8:
            rope_dma(tb + 1)
        for wi, dsts, bslot, scl in ((0, q_s, 0, 0.125), (1, k_s, 1, 1.0)):
            for ofc in range(8):
                par = ofc % 2
                pp = pf1[:, 512 * par:512 * (par + 1)]
                toks = pq("pf1", 512 * par, 512 * (par + 1))
                for ifc in range(8):
                    mm(pp, wqkb[:, wi, ifc, 128 * ofc:128 * (ofc + 1)], rq_[:, ifc, :],
                       ["wqk%d" % wi, "rqb%d" % b], toks, start=(ifc == 0), stop=(ifc == 7))
                qb_ = qkbo if ofc % 2 == 0 else qkbo2
                qtok = "qkbo%d" % (ofc % 2)
                ts2(qb_[:], pp, bias_s[:, bslot, ofc:ofc + 1], scl, ALU.add, ALU.mult,
                    toks + ["bias", "qkdma%d" % (ofc % 2)], [qtok])
                dma(dsts[128 * ofc:128 * (ofc + 1), t0:t0 + 512], qb_[:], [qtok],
                    ["qkdma%d" % (ofc % 2), "qkdma"])
                if wi == 0 and tb + 1 < 8:
                    rope_step(tb + 1, ofc)

    for sl in range(8):
        dma(p2st[:].rearrange("p (a o) -> p a o", a=1),
            f_wv.rearrange("(a p) o -> p a o", p=128)[:, sl:sl + 1, :],
            ["p2str"], ["p2stw"], q="gp")
        cp(wqkb[:, 0, sl, :],
           p2st[:], ["p2stw", "wqk0"], ["wqk0", "p2str"])
    mset(vhat[:, :, :, 64:65], 1.0, ["qfm", "kfm", "ffm", "vtm"], ["vhat"])
    mset(vhat[:, :, :, 129:130], 1.0, ["vhat"], ["vhat"])
    for tg in range(32):
        tt0 = tg * 128
        for hf in range(2):
            ppv = pf2[:, 512 * hf:512 * (hf + 1)]
            toks = pq("pf2", 512 * hf, 512 * (hf + 1))
            for ifc in range(8):
                mm(ppv, xn[:, ifc, tt0:tt0 + 128], wqkb[:, 0, ifc, 512 * hf:512 * (hf + 1)],
                   ["wqk0", "xn"], toks, start=(ifc == 0), stop=False)
            mm(ppv, onerowb_s[0:1, :], bvrow_s[0:1, 512 * hf:512 * (hf + 1)],
               ["onerowb", "bvrow"], toks, start=False, stop=True)
        for p_ in range(8):
            base = 128 * p_
            hf = 1 if base >= 512 else 0
            col = base - 512 * hf
            ppv = pf2[:, 512 * hf:512 * (hf + 1)]
            toks = pq("pf2", 512 * hf, 512 * (hf + 1))
            cp(vhat[:, tg, p_, 0:64], ppv[:, col:col + 64], toks + ["vhat"], ["vhat"])
            cp(vhat[:, tg, p_, 65:129], ppv[:, col + 64:col + 128], toks + ["vhat"], ["vhat"])

    S.fence(col_s[:, 7:8])
    for sl in range(2):
        dma(wfob[:].rearrange("p a b -> p a b")[:, 4 * sl:4 * sl + 4, :],
            wfo_sd.rearrange("(a p) o -> p a o", p=128)[:, 4 * sl:4 * sl + 4, :],
            ["p2bfr", "wqk0", "wqk1", "rqb", "swp", "m1", "m2", "qkbo"], ["wfob"])
    for qt in range(NQT):
        q0 = qt * 512
        nkt = 4 * (qt + 1)
        dma(qbuf[:], q_s.rearrange("(pp p) t -> p pp t", p=128)[:, :, q0:q0 + 512],
            ["qkdma", "qbr"], ["qbw"])
        for pp_ in range(NP):
            kpar = pp_ % 2
            dma(kbuf[:, kpar, 0:512 * (qt + 1)], k_s[128 * pp_:128 * (pp_ + 1), 0:512 * (qt + 1)],
                ["kb%d" % kpar, "qkdma"], ["kbw%d" % kpar], q="gp")
            def emit_ctx(kt, nkt=nkt, pp_=pp_):
                par = kt % 2
                mm(pf2[0:65, 0:512], vhat[:, kt, pp_, 0:65], est[:, par, 0:512],
                   ["vhat", "est%d" % par], pq("pf2", 0, 512),
                   start=(kt == 0), stop=(kt == nkt - 1))
                mm(pf2[0:65, 512:1024], vhat[:, kt, pp_, 65:130], est[:, par, 512:1024],
                   ["vhat", "est%d" % par], pq("pf2", 512, 1024),
                   start=(kt == 0), stop=(kt == nkt - 1))
            for kt in range(nkt):
                par = kt % 2
                psc = pf0 if par == 0 else pf1
                ptn = "pf0" if par == 0 else "pf1"
                toksA = pq(ptn, 0, 512)
                toksB = pq(ptn, 512, 1024)
                mm(psc[:, 0:512], kbuf[0:64, kpar, 128 * kt:128 * (kt + 1)],
                   qbuf[0:64, pp_, :], ["kbw%d" % kpar, "qbw"], toksA, tp=(0, 0))
                mm(psc[:, 512:1024], kbuf[64:128, kpar, 128 * kt:128 * (kt + 1)],
                   qbuf[64:128, pp_, :], ["kbw%d" % kpar, "qbw"], toksB, tp=(64, 0))
                act(est[:, par, :], psc[:, 0:1024], AF.Exp,
                    toksA + toksB + ["est%d" % par], ["est%d" % par], bias=EXPB)
                di = kt - 4 * qt
                if di >= 0:
                    for hh in range(2):
                        base = 512 * hh
                        if di > 0:
                            mset(est[:, par, base:base + 128 * di], 0.0,
                                 ["est%d" % par], ["est%d" % par])
                        tt(est[:, par, base + 128 * di:base + 128 * (di + 1)],
                           est[:, par, base + 128 * di:base + 128 * (di + 1)],
                           tri_s[:], ALU.mult, ["est%d" % par, "tri"], ["est%d" % par])
                if kt > 0:
                    emit_ctx(kt - 1)
            emit_ctx(nkt - 1)
            act(rows[0:1, 0, :], pf2[64:65, 0:512], AF.Ln, pq("pf2", 0, 512) + ["lnz"],
                ["lnz"])
            act(zr_s[:], rows[0:1, 0, :], AF.Exp, ["lnz"], ["zr"], scale=-1.0)
            mm(pf0[0:64, 0:512], onerowb_s[0:1, 0:64], zr_s[:], ["onerowb", "zr"],
               pq("pf0", 0, 512))
            cp(zb_s[:], pf0[0:64, 0:512], pq("pf0", 0, 512), ["zb"])
            tt(ctxf[0:64, pp_, :], pf2[0:64, 0:512], zb_s[:], ALU.mult,
               pq("pf2", 0, 512) + ["zb", "ctxf%d" % pp_], ["ctxf%d" % pp_])
            act(rows[0:1, 0, :], pf2[64:65, 512:1024], AF.Ln, pq("pf2", 512, 1024) + ["lnz"],
                ["lnz"])
            act(zr_s[:], rows[0:1, 0, :], AF.Exp, ["lnz"], ["zr"], scale=-1.0)
            mm(pf0[0:64, 0:512], onerowb_s[0:1, 0:64], zr_s[:], ["onerowb", "zr"],
               pq("pf0", 0, 512))
            cp(zb_s[:], pf0[0:64, 0:512], pq("pf0", 0, 512), ["zb"])
            tt(ctxf[64:128, pp_, :], pf2[0:64, 512:1024], zb_s[:], ALU.mult,
               pq("pf2", 512, 1024) + ["zb", "ctxf%d" % pp_], ["ctxf%d" % pp_])
        for ofc in range(8):
            par = ofc % 2
            po = (pf0 if par == 0 else pf1)[:, 0:512]
            ptn = "pf0" if par == 0 else "pf1"
            toks = pq(ptn, 0, 512)
            for pp_ in range(NP):
                mm(po, wfob[:, pp_, 128 * ofc:128 * (ofc + 1)], ctxf[:, pp_, :],
                   ["wfob", "ctxf%d" % pp_], toks,
                   start=(pp_ == 0), stop=(pp_ == 7))
            abv = aob[:] if ofc % 2 == 0 else est[:, 1, 0:512]
            abtok = "aob%d" % (ofc % 2)
            cp(abv, po, toks + ["adma%d" % (ofc % 2)] + ([] if ofc % 2 == 0 else ["est1"]),
               [abtok] + ([] if ofc % 2 == 0 else ["est1"]))
            dma(ao_s[128 * ofc:128 * (ofc + 1), q0:q0 + 512], abv, [abtok],
                ["adma%d" % (ofc % 2), "aod"])

    S.fence(col_s[:, 7:8])
    for qt in range(NQT):
        q0 = qt * 512
        dma(xti[:], xt.rearrange("(c p) t -> p c t", p=128)[:, :, q0:q0 + 512],
            ["xtir", "kbw0", "kbw1", "qbw", "est0", "est1"] if qt == 0 else ["xtir"],
            ["xtiw"])
        dma(aol[:], ao_s.rearrange("(c p) t -> p c t", p=128)[:, :, q0:q0 + 512],
            ["aod", "aolr"], ["aolw"], q="gp")
        for c in range(8):
            stt(ybf[:, c, :], aol[:, c, :], bias_s[:, 2, c:c + 1], xti[:, c, :],
                ALU.add, ALU.add, ["aolw", "xtiw", "bias"], ["ybf", "xtir", "aolr"])
        for c in range(8):
            tt(ysq[:], ybf[:, c, :], ybf[:, c, :], ALU.mult, ["ybf"], ["ysq"])
            mm(pf2[0:1, 0:512], onecol_s[:], ybf[:, c, :], ["onecol", "ybf"],
               pq("pf2", 0, 512), start=(c == 0), stop=(c == 7))
            mm(pf2[0:1, 512:1024], onecol_s[:], ysq[:], ["onecol", "ysq"],
               pq("pf2", 512, 1024), start=(c == 0), stop=(c == 7))
        ts(rows[0:1, 0, :], pf2[0:1, 0:512], inv_d, ALU.mult, pq("pf2", 0, 512), ["r0"])
        ts(rows[0:1, 1, :], pf2[0:1, 512:1024], inv_d, ALU.mult, pq("pf2", 512, 1024), ["r1"])
        tt(rows[0:1, 2, :], rows[0:1, 0, :], rows[0:1, 0, :], ALU.mult, ["r0"], ["r2"])
        tt(rows[0:1, 3, :], rows[0:1, 1, :], rows[0:1, 2, :], ALU.subtract, ["r1", "r2"], ["r3"])
        act(rows[0:1, 4, :], rows[0:1, 3, :], AF.Ln, ["r3"], ["r4"], bias=1e-5)
        act(rows[0:1, 5, :], rows[0:1, 4, :], AF.Exp, ["r4"], ["r5"], scale=-0.5)
        mm(pf1[:, 0:512], onerow_s[:], rows[0:1, 0, :], ["onerow", "r0"], pq("pf1", 0, 512))
        mm(pf1[:, 512:1024], onerow_s[:], rows[0:1, 5, :], ["onerow", "r5"],
           pq("pf1", 512, 1024))
        for c in range(8):
            tt(ybf[:, c, :], ybf[:, c, :], pf1[:, 0:512], ALU.subtract,
               ["ybf"] + pq("pf1", 0, 512), ["ybf"])
            tt(ybf[:, c, :], ybf[:, c, :], pf1[:, 512:1024], ALU.mult,
               ["ybf"] + pq("pf1", 512, 1024), ["ybf"])
            ts2(x1[:, c, q0:q0 + 512], ybf[:, c, :], bias_s[:, 3, c:c + 1],
                bias_s[:, 4, c:c + 1], ALU.mult, ALU.add, ["ybf", "bias", "xn"], ["x1"])
        for sub in range(4):
            st0 = q0 + 128 * sub
            tile = qt * 4 + sub
            for ifc in range(8):
                mm(pf0[0:128, 0:8], x1[:, ifc, st0:st0 + 128], gwt_s[:, ifc, :],
                   ["x1", "gwt"], pq("pf0", 0, 256), start=(ifc == 0), stop=False)
            mm(pf0[0:128, 0:8], onerowb_s[0:1, :], gbrow_s[:], ["onerowb", "gbrow"],
               pq("pf0", 0, 256), start=False, stop=True)
            rmax(col_s[:, 0:1], pf0[0:128, 0:8], pq("pf0", 0, 256), ["nmx"], negate=True)
            act(g8_s[:, 0, :], pf0[0:128, 0:8], AF.Exp, pq("pf0", 0, 256) + ["nmx", "g80"],
                ["g80"], bias=col_s[:, 0:1])
            rsum(col_s[:, 1:2], g8_s[:, 0, :], ["g80"], ["smx"])
            recip(col_s[:, 2:3], col_s[:, 1:2], ["smx"], ["rsx"])
            ts(g8_s[:, 1, :], g8_s[:, 0, :], col_s[:, 2:3], ALU.mult, ["g80", "rsx"], ["g81"])
            rmax(col_s[:, 3:4], g8_s[:, 1, :], ["g81"], ["m1x"])
            ts(g8_s[:, 2, :], g8_s[:, 1, :], col_s[:, 3:4], ALU.is_ge, ["g81", "m1x"], ["g82"])
            stt(g8_s[:, 3, :], g8_s[:, 2, :], -2.0, g8_s[:, 1, :], ALU.mult, ALU.add,
                ["g82", "g81"], ["g83"])
            rmax(col_s[:, 4:5], g8_s[:, 3, :], ["g83"], ["m2x"])
            ts(g8_s[:, 2, :], g8_s[:, 1, :], col_s[:, 4:5], ALU.is_ge, ["g81", "m2x"], ["g82"])
            tt(wbuf_s[:, tile, :], g8_s[:, 1, :], g8_s[:, 2, :], ALU.mult,
               ["g81", "g82"], ["wbuf"])

    if debug:
        dx1 = nc.dram_tensor("dx1", [D, T], BF16, kind="ExternalOutput").ap()
        dma(dx1.rearrange("(c p) t -> p c t", p=128), x1[:], ["x1"], ["dx1"])
        dwb = nc.dram_tensor("dwb", [128, 32, 8], F32, kind="ExternalOutput").ap()
        dma(dwb, wbuf_s[:], ["wbuf"], ["dwb"])
    S.fence(col_s[:, 7:8])
    mset(macc[:].rearrange("p a b -> p (a b)")[:, 0:16384], 0.0,
         ["xtir", "aolr", "ybf", "ysq"], ["macc"])
    mset(macc[:].rearrange("p a b -> p (a b)")[:, 16384:32768], 0.0, ["macc"], ["macc"])
    for e in range(NE):
        if True:
            wpar = e % 2
            dma(webuf[:, wpar, :, :],
                moew[e].rearrange("(a p) o -> p a o", p=128),
                ["webr%d" % wpar] + (["vhat", "mwbf_r"] if e < 2 else []),
                ["webw%d" % wpar], q="gp")
            for ti in range(32):
                tile = ti
                tt0 = tile * 128
                par = ti % 2
                pe_ = pf0 if par == 0 else pf1
                ptn = "pf0" if par == 0 else "pf1"
                toks = pq(ptn, 0, 1024)
                for ifc in range(8):
                    for hf in range(2):
                        mm(pe_[:, 512 * hf:512 * (hf + 1)], x1[:, ifc, tt0:tt0 + 128],
                           webuf[:, wpar, ifc, 512 * hf:512 * (hf + 1)],
                           ["webw%d" % wpar, "x1"], pq(ptn, 512 * hf, 512 * (hf + 1)),
                           start=(ifc == 0), stop=(ifc == 7))
                act(mfm3[:, 2 * (ti % 2):2 * (ti % 2) + 2, :].rearrange("p a b -> p (a b)"),
                    pe_[:, 0:1024], AF.Copy, toks + ["wbuf", "eow%d" % (ti % 2)],
                    ["eow%d" % (ti % 2)], scale=wbuf_s[:, tile, e:e + 1])
                tt(macc[:, tile, :], macc[:, tile, :],
                   mfm3[:, 2 * (ti % 2):2 * (ti % 2) + 2, :].rearrange("p a b -> p (a b)"),
                   ALU.add, ["macc", "eow%d" % (ti % 2)], ["macc", "webr%d" % wpar])
    for tile in range(32):
        tr(pf2[0:8, 0:128], wbuf_s[:, tile, :], ["wbuf"] + pq("pf2", 0, 256),
           pq("pf2", 0, 256), identf_s[:])
        cp(wts_s[:], pf2[0:8, 0:128], pq("pf2", 0, 256), ["wts"])
        for hf in range(2):
            mm(pf1[:, 512 * hf:512 * (hf + 1)], wts_s[:], expb_s[0:8, 512 * hf:512 * (hf + 1)],
               ["wts", "expb"], pq("pf1", 512 * hf, 1024 * (hf + 1) - 512 * hf) if False else
               pq("pf1", 512 * hf, 512 * (hf + 1)))
        tt(macc[:, tile, :], macc[:, tile, :], pf1[:, 0:1024], ALU.add,
           ["macc"] + pq("pf1", 0, 1024), ["macc"])

    if debug:
        dmacc = nc.dram_tensor("dmacc", [128, 32, KB], BF16, kind="ExternalOutput").ap()
        dma(dmacc, macc[:], ["macc"], ["dmacc"])
    S.fence(col_s[:, 7:8])
    for tb in range(8):
        t0 = tb * 512
        for c in range(8):
            for i4 in range(4):
                tile = tb * 4 + i4
                tr(pb0[:, 128 * i4:128 * (i4 + 1)], macc[:, tile, 128 * c:128 * (c + 1)],
                   ["macc"] + pq("pb0", 128 * i4, 128 * (i4 + 1)),
                   pq("pb0", 128 * i4, 128 * (i4 + 1)), identb_s[:])
            cp(mfm3[:, c, :], pb0[:, 0:512], pq("pb0", 0, 512), ["mfm", "webw0", "webw1"])
            tt(y2b[:, c, :], x1[:, c, t0:t0 + 512], mfm3[:, c, :], ALU.add,
               ["x1", "mfm"], ["y2b"])
        for c in range(8):
            tt(y2sq[:], y2b[:, c, :], y2b[:, c, :], ALU.mult, ["y2b"], ["y2sq"])
            mm(pf2[0:1, 0:512], onecol_s[:], y2b[:, c, :], ["onecol", "y2b"],
               pq("pf2", 0, 512), start=(c == 0), stop=(c == 7))
            mm(pf2[0:1, 512:1024], onecol_s[:], y2sq[:], ["onecol", "y2sq"],
               pq("pf2", 512, 1024), start=(c == 0), stop=(c == 7))
        ts(rows[0:1, 0, :], pf2[0:1, 0:512], inv_d, ALU.mult, pq("pf2", 0, 512), ["r0"])
        ts(rows[0:1, 1, :], pf2[0:1, 512:1024], inv_d, ALU.mult, pq("pf2", 512, 1024), ["r1"])
        tt(rows[0:1, 2, :], rows[0:1, 0, :], rows[0:1, 0, :], ALU.mult, ["r0"], ["r2"])
        tt(rows[0:1, 3, :], rows[0:1, 1, :], rows[0:1, 2, :], ALU.subtract, ["r1", "r2"], ["r3"])
        act(rows[0:1, 4, :], rows[0:1, 3, :], AF.Ln, ["r3"], ["r4"], bias=1e-5)
        act(rows[0:1, 5, :], rows[0:1, 4, :], AF.Exp, ["r4"], ["r5"], scale=-0.5)
        mm(pf1[:, 0:512], onerow_s[:], rows[0:1, 0, :], ["onerow", "r0"], pq("pf1", 0, 512))
        mm(pf1[:, 512:1024], onerow_s[:], rows[0:1, 5, :], ["onerow", "r5"],
           pq("pf1", 512, 1024))
        for c in range(8):
            tt(y2b[:, c, :], y2b[:, c, :], pf1[:, 0:512], ALU.subtract,
               ["y2b"] + pq("pf1", 0, 512), ["y2b"])
            tt(y2b[:, c, :], y2b[:, c, :], pf1[:, 512:1024], ALU.mult,
               ["y2b"] + pq("pf1", 512, 1024), ["y2b"])
            act(gbo[:, c, :], y2b[:, c, :], AF.Silu, ["y2b", "bias"], ["gbo"],
                bias=bias_s[:, 6, c:c + 1], scale=bias_s[:, 5, c:c + 1])
        dma(g_sd.rearrange("(c p) t -> p c t", p=128)[:, :, t0:t0 + 512], gbo[:],
            ["gbo"], ["g_sdw"], q="gp")

    S.fence(col_s[:, 7:8])
    for tb in range(8):
        t0 = tb * 512
        dma(gld[:], g_sd.rearrange("(c p) t -> p c t", p=128)[:, :, t0:t0 + 512],
            ["g_sdw", "gldr", "macc", "mfm"], ["gldw"], q="gp")
        dma(xf5[:], xt.rearrange("(c p) t -> p c t", p=128)[:, :, t0:t0 + 512],
            ["xf5r", "y2b", "y2sq", "gbo"], ["xf5w"])
        dma(sf5[:], s_s.rearrange("(c p) t -> p c t", p=128)[:, :, t0:t0 + 512],
            ["s_sd", "sf5r"], ["sf5w"], q="gp")
        for c in range(8):
            tt(ob5[:, c, :], sf5[:, c, :], gld[:, c, :], ALU.mult,
               ["sf5w", "gldw", "ob5r"], ["ob5", "gldr", "sf5r"])
            tt(ob5[:, c, :], ob5[:, c, :], xf5[:, c, :], ALU.add,
               ["ob5", "xf5w"], ["ob5", "xf5r"])
        dma(outT.rearrange("(c p) t -> p c t", p=128)[:, :, t0:t0 + 512], ob5[:],
            ["ob5"], ["ob5r"])

    S.replay()
    return nc


_NC_CACHE = {}


def kernel(**inputs):
    bf16 = ml_dtypes.bfloat16
    f32 = np.float32
    x = np.asarray(inputs["L0_attn_hidden_states"], dtype=f32)

    def t_(a):
        return np.ascontiguousarray(np.asarray(a, dtype=f32).T)

    def fmv(v):
        return np.ascontiguousarray(np.asarray(v, dtype=f32).reshape(8, 128).T)

    ipw = np.asarray(inputs["L0_ffn_in_proj_w"], dtype=f32)
    ipb = np.asarray(inputs["L0_ffn_in_proj_b"], dtype=f32)
    biasm = np.zeros((128, 16, 8), dtype=f32)
    biasm[:, 0] = fmv(ipb[0:1024])
    biasm[:, 1] = fmv(ipb[1024:2048])
    biasm[:, 2] = fmv(inputs["L0_ffn_out_proj_b"])
    biasm[:, 3] = fmv(inputs["L0_ffn_ln1_w"])
    biasm[:, 4] = fmv(inputs["L0_ffn_ln1_b"])
    biasm[:, 5] = fmv(inputs["L0_ffn_ln2_w"])
    biasm[:, 6] = fmv(inputs["L0_ffn_ln2_b"])
    biasm[:, 7] = fmv(inputs["L0_attn_g_w"])

    tgrid = np.arange(256)[None, :]
    pgrid = np.arange(128)[:, None]
    ul = np.zeros((128, 2, 256), dtype=f32)
    ul[:, 0, :] = (tgrid >= pgrid).astype(f32)
    ul[:, 1, :] = (tgrid >= pgrid + 128).astype(f32)
    tri = np.triu(np.ones((128, 128), dtype=f32))
    j_of_p = (np.arange(128) % 64) // 2
    invf = 10000.0 ** (-2.0 * j_of_p / 64.0)
    tpos = np.arange(T, dtype=np.float64)[None, :]
    ang = tpos * invf[:, None]
    cosi = np.cos(ang).astype(f32)
    sini = np.sin(ang).astype(f32)
    psw = np.zeros((128, 128), dtype=f32)
    for p in range(0, 128, 2):
        psw[p + 1, p] = -1.0
        psw[p, p + 1] = 1.0

    common = {
        "g_wq": t_(inputs["L0_attn_Wq"]), "g_wk": t_(inputs["L0_attn_Wk"]),
        "g_wv": t_(inputs["L0_attn_Wv"]), "g_wf": t_(inputs["L0_attn_Wf"]),
        "g_wo": t_(inputs["L0_attn_Wo"]),
        "f_wq": t_(ipw[0:1024]), "f_wk": t_(ipw[1024:2048]), "f_wv": t_(ipw[2048:3072]),
        "f_wo": t_(inputs["L0_ffn_out_proj_w"]),
        "exp_w": np.ascontiguousarray(
            np.transpose(np.asarray(inputs["L0_ffn_expert_w"], dtype=f32), (0, 2, 1))),
        "biasm": biasm,
        "f_bv": np.asarray(ipb[2048:3072], dtype=f32).reshape(1, D),
        "gate_wt": t_(inputs["L0_ffn_gate_w"]),
        "gate_b": np.asarray(inputs["L0_ffn_gate_b"], dtype=f32).reshape(1, 8),
        "exp_b": np.asarray(inputs["L0_ffn_expert_b"], dtype=f32),
        "identf": np.eye(128, dtype=f32),
        "onecol_b": np.ones((128, 1), dtype=bf16),
        "onerow_f": np.ones((1, 128), dtype=f32),
        "onerow_b": np.ones((1, 128), dtype=bf16),
        "tri_b": tri.astype(bf16),
        "ones_b": np.ones((128, 128), dtype=bf16),
        "ul_b": ul.astype(bf16),
        "ul_f": ul,
        "cosi": cosi.astype(bf16),
        "sini": sini.astype(bf16),
        "pswap": psw.astype(bf16),
        "cvals": np.concatenate([np.full((128, 1), 1e-5, f32),
                                 np.full((128, 1), EXPB, f32)], axis=1),
    }
    in_maps = []
    for b in range(8):
        m = dict(common)
        m["xt"] = np.ascontiguousarray(x[b].T)
        in_maps.append(m)

    if "nc" not in _NC_CACHE:
        _NC_CACHE["nc"] = build_nc()
    nc = _NC_CACHE["nc"]
    res = run_bass_kernel_spmd(nc, in_maps, core_ids=list(range(8)))
    out = np.stack([np.asarray(r["outT"], dtype=f32).T for r in res.results], axis=0)
    return out


if __name__ == "__main__":
    nc = build_nc()
    print("built ok; instructions:", nc.n_instructions() if hasattr(nc, "n_instructions") else "?")


# version 21
# speedup vs baseline: 1.0174x; 1.0052x over previous
import numpy as np
import ml_dtypes
import concourse.bass as bass
import concourse.mybir as mybir
from concourse.bass_utils import run_bass_kernel_spmd

F32 = mybir.dt.float32
BF16 = mybir.dt.bfloat16
AF = mybir.ActivationFunctionType
ALU = mybir.AluOpType
AX = mybir.AxisListType

T = 4096
D = 1024
GH, GD, CH = 4, 256, 256
NCH = T // CH
NP, QT, NQT = 8, 512, 8
EXPB = -12.0
NE = 8
KB = 1024


NDMAS = 16


class Sch:
    def __init__(self, nc):
        self.nc = nc
        self.queues = ["sp", "gp", "pe", "act", "dve"]
        semnames = ["pe", "act", "dve"] + \
            ["spd%d" % i for i in range(NDMAS)] + ["gpd%d" % i for i in range(NDMAS)]
        self.sem = {s: nc.alloc_semaphore("sem_" + s) for s in semnames}
        self.cnt = {s: 0 for s in semnames}
        self.ops = {q: [] for q in self.queues}
        self.res = {}
        self.clock = {q: {s: 0 for s in semnames} for q in self.queues}
        self.dmaidx = {"sp": 0, "gp": 0}
        self.cur_fence = None
        self.nfence = 0

    def op(self, e, fn, r=(), w=(), dma=False):
        need = {}
        if self.cur_fence is not None:
            r = list(r) + [self.cur_fence]
        for name in r:
            rec = self.res.get(name)
            if rec and rec["w"] is not None:
                we, wc = rec["w"]
                if need.get(we, 0) < wc:
                    need[we] = wc
        for name in w:
            rec = self.res.get(name)
            if rec:
                if rec["w"] is not None:
                    we, wc = rec["w"]
                    if need.get(we, 0) < wc:
                        need[we] = wc
                for re_, rc in rec["r"].items():
                    if need.get(re_, 0) < rc:
                        need[re_] = rc
        if dma:
            mysem = "%sd%d" % (e, self.dmaidx[e] % NDMAS)
            self.dmaidx[e] += 1
            inc = 16
            if self.cnt[mysem] > 0 and need.get(mysem, 0) < self.cnt[mysem]:
                need[mysem] = self.cnt[mysem]
        else:
            mysem = e
            inc = 1
        waits = []
        for s2, v in need.items():
            if e == "pe" and s2 == "pe":
                continue
            if self.clock[e][s2] < v:
                waits.append((s2, v))
                self.clock[e][s2] = v
        self.cnt[mysem] += inc
        myc = self.cnt[mysem]
        self.ops[e].append((waits, fn, inc, mysem))
        for name in w:
            self.res[name] = {"w": (mysem, myc), "r": {}}
        for name in r:
            rec = self.res.setdefault(name, {"w": None, "r": {}})
            rec["r"][mysem] = myc

    def fence(self, scratch_ap):
        self.nfence += 1
        fname = "fence%d" % self.nfence
        kills = list(self.res.keys())
        self.cur_fence = None
        self.op("dve", lambda e, a=scratch_ap: e.memset(a, 0.0), r=(), w=kills + [fname])
        self.cur_fence = fname

    def replay(self):
        nc = self.nc
        sems = self.sem
        waited = {}
        for q in self.queues:
            for waits, fn, inc, mysem in self.ops[q]:
                for s2, v in waits:
                    waited.setdefault(s2, set()).add(v)
        remap = {}
        incflag = {}
        for e in ("pe", "act", "dve"):
            w = waited.get(e, set())
            m = {}
            rank = 0
            c = 0
            flags = []
            for waits, fn, inc, mysem in self.ops[e]:
                c += 1
                if c in w:
                    rank += 1
                    m[c] = rank
                    flags.append(True)
                else:
                    flags.append(False)
            remap[e] = m
            incflag[e] = flags
        with nc.allow_low_precision(reason="bf16 elementwise blessed by tolerance"), \
             nc.Block() as blk:
            for ename, meth in (("sp", blk.sync), ("gp", blk.gpsimd),
                                ("pe", blk.tensor), ("act", blk.scalar),
                                ("dve", blk.vector)):
                ops = self.ops[ename]

                def body(eng, ops=ops, ename=ename):
                    for i, (waits, fn, inc, mysem) in enumerate(ops):
                        for s2, v in waits:
                            if s2 in remap:
                                v = remap[s2][v]
                            eng.wait_ge(sems[s2], v)
                        inst = fn(eng)
                        if ename in ("sp", "gp") or incflag[ename][i]:
                            inst.then_inc(sems[mysem], inc)
                    if ename in ("sp", "gp"):
                        for i in range(NDMAS):
                            s = "%sd%d" % (ename, i)
                            if self.cnt[s] > 0:
                                eng.wait_ge(sems[s], self.cnt[s])

                meth(body)


def pq(tname, lo, hi):
    if tname.startswith("pb"):
        return [tname]
    q0, q1 = lo // 512, (hi - 1) // 512
    return [tname + "q%d" % i for i in range(q0, q1 + 1)]


def build_nc(debug=False):
    nc = bass.Bass("TRN2", target_bir_lowering=False, dynamic_dma_scratch_size=4096)

    def din(name, shape, dtype=F32):
        return nc.dram_tensor(name, shape, dtype, kind="ExternalInput").ap()

    xt = din("xt", [D, T])
    g_wq = din("g_wq", [D, D]); g_wk = din("g_wk", [D, D])
    g_wv = din("g_wv", [D, D]); g_wf = din("g_wf", [D, D]); g_wo = din("g_wo", [D, D])
    f_wq = din("f_wq", [D, D]); f_wk = din("f_wk", [D, D]); f_wv = din("f_wv", [D, D])
    f_wo = din("f_wo", [D, D])
    exp_w = din("exp_w", [NE, D, D])
    biasm = din("biasm", [128, 16, 8])
    f_bv = din("f_bv", [1, D])
    gate_wt = din("gate_wt", [D, 8])
    gate_b = din("gate_b", [1, 8])
    exp_b = din("exp_b", [8, D])
    identf = din("identf", [128, 128])
    onecol_b = din("onecol_b", [128, 1], BF16)
    onerow_f = din("onerow_f", [1, 128])
    onerow_b = din("onerow_b", [1, 128], BF16)
    tri_b = din("tri_b", [128, 128], BF16)
    ones_b = din("ones_b", [128, 128], BF16)
    ul_b = din("ul_b", [128, 2, 256], BF16)
    ul_f = din("ul_f", [128, 2, 256])
    cosi = din("cosi", [128, T], BF16)
    sini = din("sini", [128, T], BF16)
    pswap = din("pswap", [128, 128], BF16)
    cvals = din("cvals", [128, 2])

    outT = nc.dram_tensor("outT", [D, T], F32, kind="ExternalOutput").ap()

    def scr(name, shape, dtype):
        kind = "ExternalOutput" if debug else "Internal"
        return nc.dram_tensor(name, shape, dtype, kind=kind).ap()

    o_s = scr("o_s", [D, T], BF16)
    s_s = scr("s_s", [D, T], F32)
    q_s = scr("q_s", [D, T], BF16)
    k_s = scr("k_s", [D, T], BF16)
    ao_s = scr("ao_s", [D, T], BF16)
    g_sd = scr("g_sd", [D, T], BF16)
    moew = scr("moew", [NE, D, D], BF16)
    wfo_sd = scr("wfo_sd", [D, D], BF16)

    SBASE = ((nc.sbuf_base + 63) // 64) * 64
    SB = nc.alloc_sbuf_tensor_at
    xn = SB("xn", [128, 8, T], BF16, offset=SBASE + (0))
    x1 = SB("x1", [128, 8, T], BF16, offset=SBASE + (0))
    R2 = 64 * KB
    qfm = SB("qfm", [128, 2, T], BF16, offset=SBASE + (R2))
    kfm = SB("kfm", [128, 2, T], BF16, offset=SBASE + (R2 + 16 * KB))
    ffm = SB("ffm", [128, 2, T], BF16, offset=SBASE + (R2 + 32 * KB))
    vtm = SB("vtm", [128, 32, 256], BF16, offset=SBASE + (R2 + 48 * KB))
    vhat = SB("vhat", [128, 32, 8, 130], BF16, offset=SBASE + (R2))
    webuf = SB("webuf", [128, 2, 8, KB], BF16, offset=SBASE + (R2))
    mfm3 = SB("mfm3", [128, 8, 512], BF16, offset=SBASE + (R2 + 32 * KB))
    y2b = SB("y2b", [128, 8, 512], BF16, offset=SBASE + (R2 + 40 * KB))
    y2sq = SB("y2sq", [128, 512], BF16, offset=SBASE + (R2 + 48 * KB))
    gbo = SB("gbo", [128, 8, 512], BF16, offset=SBASE + (R2 + 49 * KB))
    gld = SB("gld", [128, 8, 512], BF16, offset=SBASE + (R2))
    xf5 = SB("xf5", [128, 8, 512], F32, offset=SBASE + (R2 + 8 * KB))
    sf5 = SB("sf5", [128, 8, 512], F32, offset=SBASE + (R2 + 24 * KB))
    ob5 = SB("ob5", [128, 8, 512], F32, offset=SBASE + (R2 + 40 * KB))
    R3 = 129 * KB
    xst = SB("xst", [128, 2, 8, 512], F32, offset=SBASE + (R3))
    xbfst = SB("xbfst", [128, 512], BF16, offset=SBASE + (R3 + 32 * KB))
    xsqst = SB("xsqst", [128, 512], BF16, offset=SBASE + (R3 + 33 * KB))
    cst = SB("cst", [128, KB], F32, offset=SBASE + (R3 + 34 * KB))
    ktm = SB("ktm", [128, 32, 256], BF16, offset=SBASE + (R3))
    wst = SB("wst", [128, 512], F32, offset=SBASE + (R3 + 16 * KB))
    wgb = SB("wgb", [128, 4, 8, 256], BF16, offset=SBASE + (R3 + 18 * KB))
    mwst = SB("mwst", [128, KB], F32, offset=SBASE + (R3 + 34 * KB))
    mwbf = SB("mwbf", [128, KB], BF16, offset=SBASE + (R3 + 38 * KB))
    ovbb = SB("ovbb", [128, 2, 256], BF16, offset=SBASE + (R3 + 40 * KB))
    sgst = SB("sgst", [128, 512], F32, offset=SBASE + (R3 + 41 * KB))
    C0 = R3 + 43 * KB
    ftm_s = SB("ftm_s", [128, 2, 256], BF16, offset=SBASE + (C0))
    enl_s = SB("enl_s", [128, 512], BF16, offset=SBASE + (C0 + 1 * KB))
    gks_s = SB("gks_s", [128, 512], BF16, offset=SBASE + (C0 + 2 * KB))
    ssm_s = SB("ssm_s", [128, 512], BF16, offset=SBASE + (C0 + 3 * KB))
    sfm_s = SB("sfm_s", [128, 512], BF16, offset=SBASE + (C0 + 4 * KB))
    stm_s = SB("stm_s", [128, 2, 256], BF16, offset=SBASE + (C0 + 5 * KB))
    gtm_s = SB("gtm_s", [128, 2, 256], BF16, offset=SBASE + (C0 + 6 * KB))
    gcr_s = SB("gcr_s", [1, 256], BF16, offset=SBASE + (C0 + 7 * KB))
    gcc_s = SB("gcc_s", [128, 2], F32, offset=SBASE + (C0 + 7 * KB + 512))
    a1_s = SB("a1_s", [128, 2, 256], BF16, offset=SBASE + (C0 + 7 * KB + 576))
    oks_s = SB("oks_s", [128, 256], F32, offset=SBASE + (C0 + 8 * KB + 576))
    eok_s = SB("eok_s", [128, 256], F32, offset=SBASE + (C0 + 9 * KB + 576))
    qvt_s = SB("qvt_s", [128, 2, 256], BF16, offset=SBASE + (C0 + 10 * KB + 576))
    qvf_s = SB("qvf_s", [128, 2, 256], BF16, offset=SBASE + (C0 + 11 * KB + 576))
    a2_s = SB("a2_s", [128, 2, 256], BF16, offset=SBASE + (C0 + 12 * KB + 576))
    s1_s = SB("s1_s", [128, 2, 256], BF16, offset=SBASE + (C0 + 13 * KB + 576))
    s2_s = SB("s2_s", [128, 2, 256], BF16, offset=SBASE + (C0 + 14 * KB + 576))
    col_s = SB("col_s", [128, 8], F32, offset=SBASE + (C0 + 15 * KB + 576))
    stm2 = SB("stm2", [128, 2, 2, 256], BF16, offset=SBASE + (C0 + 16 * KB))
    gcc2 = SB("gcc2", [128, 4], F32, offset=SBASE + (C0 + 18 * KB + 64))
    gcbs = SB("gcbs", [128, 2, 256], BF16, offset=SBASE + (C0 + 18 * KB + 128))
    olod = SB("olod", [128, 2, 8, 512], BF16, offset=SBASE + (R3))
    osil = SB("osil", [128, 8, 512], F32, offset=SBASE + (R3 + 16 * KB))
    osq = SB("osq", [128, 512], BF16, offset=SBASE + (R3 + 32 * KB))
    obuf = SB("obuf", [128, 8, 512], BF16, offset=SBASE + (R3 + 33 * KB))
    wot = SB("wot", [128, 8, KB], BF16, offset=SBASE + (R3 + 41 * KB))
    sev = SB("sev", [128, 512], F32, offset=SBASE + (R3 + 57 * KB))
    wstg = SB("wstg", [128, KB], F32, offset=SBASE + (R3 + 59 * KB))
    wqkb = SB("wqkb", [128, 2, 8, KB], BF16, offset=SBASE + (R3))
    rqb = SB("rqb", [128, 8, 512], BF16, offset=SBASE + (R3 + 32 * KB))
    cosblk = SB("cosblk", [128, 2, 512], BF16, offset=SBASE + (R3 + 40 * KB))
    sinblk = SB("sinblk", [128, 2, 512], BF16, offset=SBASE + (R3 + 42 * KB))
    swpb = SB("swpb", [128, 512], BF16, offset=SBASE + (R3 + 44 * KB))
    m1b = SB("m1b", [128, 512], BF16, offset=SBASE + (R3 + 45 * KB))
    m2b = SB("m2b", [128, 512], BF16, offset=SBASE + (R3 + 46 * KB))
    qkbo = SB("qkbo", [128, 512], BF16, offset=SBASE + (R3 + 47 * KB))
    p2st = SB("p2st", [128, KB], F32, offset=SBASE + (R3 + 48 * KB))
    p2bf = SB("p2bf", [128, KB], BF16, offset=SBASE + (R3 + 52 * KB))
    rqb2 = SB("rqb2", [128, 8, 512], BF16, offset=SBASE + (R3 + 54 * KB))
    qkbo2 = SB("qkbo2", [128, 512], BF16, offset=SBASE + (R3 + 62 * KB))
    sev2 = SB("sev2", [128, 512], F32, offset=SBASE + (R3 + 63 * KB))
    kbuf = SB("kbuf", [128, 2, T], BF16, offset=SBASE + (R3))
    qbuf = SB("qbuf", [128, 8, 512], BF16, offset=SBASE + (R3 + 16 * KB))
    est = SB("est", [128, 2, KB], BF16, offset=SBASE + (R3 + 24 * KB))
    ctxf = SB("ctxf", [128, 8, 512], BF16, offset=SBASE + (R3 + 28 * KB))
    wfob = SB("wfob", [128, 8, KB], BF16, offset=SBASE + (R3 + 36 * KB))
    aob = SB("aob", [128, 512], BF16, offset=SBASE + (R3 + 52 * KB))
    cxb = SB("cxb", [64, 512], BF16, offset=SBASE + (R3 + 53 * KB))
    xti = SB("xti", [128, 8, 512], F32, offset=SBASE + (R3))
    aol = SB("aol", [128, 8, 512], BF16, offset=SBASE + (R3 + 16 * KB))
    ybf = SB("ybf", [128, 8, 512], BF16, offset=SBASE + (R3 + 24 * KB))
    ysq = SB("ysq", [128, 512], BF16, offset=SBASE + (R3 + 32 * KB))
    macc = SB("macc", [128, 32, KB], BF16, offset=SBASE + (R3))
    R4 = 194 * KB
    identf_s = SB("identf_s", [128, 128], F32, offset=SBASE + (R4))
    identb_s = SB("identb_s", [128, 128], BF16, offset=SBASE + (R4 + 512))
    onecol_s = SB("onecol_s", [128, 1], BF16, offset=SBASE + (R4 + 768))
    onerow_s = SB("onerow_s", [1, 128], F32, offset=SBASE + (R4 + 832))
    onerowb_s = SB("onerowb_s", [1, 128], BF16, offset=SBASE + (R4 + 1344))
    tri_s = SB("tri_s", [128, 128], BF16, offset=SBASE + (R4 + 1600))
    onesb_s = SB("onesb_s", [128, 128], BF16, offset=SBASE + (R4 + 1856))
    ul_s = SB("ul_s", [128, 2, 256], BF16, offset=SBASE + (R4 + 2112))
    ulf_s = SB("ulf_s", [128, 2, 256], F32, offset=SBASE + (R4 + 3136))
    pswap_s = SB("pswap_s", [128, 128], BF16, offset=SBASE + (R4 + 5184))
    B0 = R4 + 5 * KB + 448
    bias_s = SB("bias_s", [128, 16, 8], F32, offset=SBASE + (B0))
    bvrow_s = SB("bvrow_s", [1, D], BF16, offset=SBASE + (B0 + 512))
    gbrow_s = SB("gbrow_s", [1, 8], BF16, offset=SBASE + (B0 + 2560))
    gwt_s = SB("gwt_s", [128, 8, 8], BF16, offset=SBASE + (B0 + 2624))
    expb_s = SB("expb_s", [8, D], BF16, offset=SBASE + (B0 + 2752))
    wbuf_s = SB("wbuf_s", [128, 32, 8], F32, offset=SBASE + (B0 + 4800))
    g8_s = SB("g8_s", [128, 5, 8], F32, offset=SBASE + (B0 + 5824))
    zr_s = SB("zr_s", [1, 512], BF16, offset=SBASE + (B0 + 6016))
    zb_s = SB("zb_s", [64, 512], BF16, offset=SBASE + (B0 + 7040))
    wts_s = SB("wts_s", [8, 128], BF16, offset=SBASE + (B0 + 8064))
    rows = SB("rows", [1, 6, 512], F32, offset=SBASE + (B0 + 8384))
    cvals_s = SB("cvals_s", [128, 2], F32, offset=SBASE + (B0 + 8384 + 12 * KB))

    pf0 = nc.alloc_psum_tensor("pf0", [128, KB], F32)
    pf1 = nc.alloc_psum_tensor("pf1", [128, KB], F32)
    pf2 = nc.alloc_psum_tensor("pf2", [128, KB], F32)
    pb0 = nc.alloc_psum_tensor("pb0", [128, KB], BF16)
    pb1 = nc.alloc_psum_tensor("pb1", [128, KB], BF16)

    S = Sch(nc)

    def dma(dst, src, rr, ww, q="sp"):
        q = "sp"
        S.op(q, lambda e, d=dst, s=src: e.dma_start(out=d, in_=s), r=rr, w=ww, dma=True)

    def mm(out, lhsT, rhs, rr, ww, start=True, stop=True, tp=None):
        S.op("pe", lambda e, o=out, l=lhsT, rh=rhs, st=start, sp=stop, t=tp:
             e.matmul(out=o, lhsT=l, rhs=rh, start=st, stop=sp, tile_position=t,
                      skip_group_check=True), r=rr, w=ww)

    def tr(out, in_, rr, ww, ident):
        S.op("pe", lambda e, o=out, i=in_, idn=ident:
             e.transpose(out=o, in_=i, identity=idn), r=rr, w=ww)

    def act(out, in_, func, rr, ww, bias=0.0, scale=1.0):
        S.op("act", lambda e, o=out, i=in_, f=func, b=bias, s=scale:
             e.activation(out=o, in_=i, func=f, bias=b, scale=s), r=rr, w=ww)

    def dve(fn, rr, ww):
        S.op("dve", fn, r=rr, w=ww)

    def tt(out, a, b, op, rr, ww):
        dve(lambda e, o=out, x=a, y=b, p=op: e.tensor_tensor(out=o, in0=x, in1=y, op=p), rr, ww)

    def ts(out, a, s1, op, rr, ww):
        dve(lambda e, o=out, x=a, s_=s1, p=op:
            e.tensor_single_scalar(out=o, in_=x, scalar=s_, op=p), rr, ww)

    def stt(out, in0, scal, in1, op0, op1, rr, ww):
        dve(lambda e, o=out, x=in0, s_=scal, y=in1, p0=op0, p1=op1:
            e.scalar_tensor_tensor(out=o, in0=x, scalar=s_, in1=y, op0=p0, op1=p1), rr, ww)

    def ts2(out, a, s1, s2, op0, op1, rr, ww):
        dve(lambda e, o=out, x=a, u=s1, v=s2, p0=op0, p1=op1:
            e.tensor_scalar(out=o, in0=x, scalar1=u, scalar2=v, op0=p0, op1=p1), rr, ww)

    def cp(out, in_, rr, ww):
        dve(lambda e, o=out, i=in_: e.tensor_copy(out=o, in_=i), rr, ww)

    def rmax(out, in_, rr, ww, negate=False):
        dve(lambda e, o=out, i=in_, n=negate:
            e.reduce_max(out=o, in_=i, axis=AX.X, negate=n), rr, ww)

    def rsum(out, in_, rr, ww):
        dve(lambda e, o=out, i=in_: e.reduce_sum(out=o, in_=i, axis=AX.X), rr, ww)

    def recip(out, in_, rr, ww):
        dve(lambda e, o=out, i=in_: e.reciprocal(out=o, in_=i), rr, ww)

    def mset(ap, val, rr, ww):
        dve(lambda e, a=ap, v=val: e.memset(a, v), rr, ww)

    nc.const_aps.register(F32, 1e-5, cvals_s[:, 0:1])
    nc.const_aps.register(F32, EXPB, cvals_s[:, 1:2])
    dma(cvals_s[:], cvals, [], ["cvals"])
    dma(identf_s[:], identf, [], ["identf"])
    cp(identb_s[:], identf_s[:], ["identf"], ["identb"])
    dma(onecol_s[:], onecol_b, [], ["onecol"])
    dma(onerow_s[:], onerow_f, [], ["onerow"])
    dma(onerowb_s[:], onerow_b, [], ["onerowb"])
    dma(tri_s[:], tri_b, [], ["tri"])
    dma(onesb_s[:], ones_b, [], ["onesb"])
    dma(ul_s[:], ul_b, [], ["ul"])
    dma(ulf_s[:], ul_f, [], ["ulf"])
    dma(pswap_s[:], pswap, [], ["pswap"])
    dma(bias_s[:], biasm, [], ["bias"])
    ts(bias_s[:, 8, :], bias_s[:, 0, :], 0.125, ALU.mult, ["bias"], ["bias"])
    dma(cst[0:1, :], f_bv, ["cstr"], ["cst"])
    cp(bvrow_s[:], cst[0:1, :], ["cst"], ["bvrow", "cstr"])
    dma(cst[0:1, 0:8], gate_b, ["cstr"], ["cst"])
    cp(gbrow_s[:], cst[0:1, 0:8], ["cst"], ["gbrow", "cstr"])
    dma(cst[:, 0:64].rearrange("p (c e) -> p c e", c=8),
        gate_wt.rearrange("(c p) e -> p c e", p=128), ["cstr"], ["cst"])
    cp(gwt_s[:].rearrange("p c e -> p (c e)"), cst[:, 0:64], ["cst"], ["gwt", "cstr"])
    dma(cst[0:8, :], exp_b, ["cstr"], ["cst"])
    cp(expb_s[:], cst[0:8, :], ["cst"], ["expb", "cstr"])

    inv_d = 1.0 / D
    for tb in range(8):
        bf = tb % 2
        c0 = tb * 512
        for c in range(8):
            dma(xst[:, bf, c, :], xt[128 * c:128 * (c + 1), c0:c0 + 512],
                [], ["xst%d_%d" % (bf, c)])
        for c in range(8):
            xc = "xst%d_%d" % (bf, c)
            cp(xbfst[:], xst[:, bf, c, :], [xc], ["xbf"])
            tt(xsqst[:], xbfst[:], xbfst[:], ALU.mult, ["xbf"], ["xsq"])
            mm(pf0[0:1, 0:512], onecol_s[:], xbfst[:], ["onecol", "xbf"], pq("pf0", 0, 512),
               start=(c == 0), stop=(c == 7))
            mm(pf0[0:1, 512:1024], onecol_s[:], xsqst[:], ["onecol", "xsq"],
               pq("pf0", 512, 1024), start=(c == 0), stop=(c == 7))
        ts(rows[0:1, 0, :], pf0[0:1, 0:512], inv_d, ALU.mult, pq("pf0", 0, 512), ["r0"])
        ts(rows[0:1, 1, :], pf0[0:1, 512:1024], inv_d, ALU.mult, pq("pf0", 512, 1024), ["r1"])
        tt(rows[0:1, 2, :], rows[0:1, 0, :], rows[0:1, 0, :], ALU.mult, ["r0"], ["r2"])
        tt(rows[0:1, 3, :], rows[0:1, 1, :], rows[0:1, 2, :], ALU.subtract, ["r1", "r2"], ["r3"])
        act(rows[0:1, 4, :], rows[0:1, 3, :], AF.Ln, ["r3"], ["r4"], bias=1e-5)
        act(rows[0:1, 5, :], rows[0:1, 4, :], AF.Exp, ["r4"], ["r5"], scale=-0.5)
        mm(pf1[:, 0:512], onerow_s[:], rows[0:1, 0, :], ["onerow", "r0"], pq("pf1", 0, 512))
        mm(pf1[:, 512:1024], onerow_s[:], rows[0:1, 5, :], ["onerow", "r5"],
           pq("pf1", 512, 1024))
        for c in range(8):
            xc = "xst%d_%d" % (bf, c)
            tt(xst[:, bf, c, :], xst[:, bf, c, :], pf1[:, 0:512], ALU.subtract,
               [xc] + pq("pf1", 0, 512), [xc])
            tt(xn[:, c, c0:c0 + 512], xst[:, bf, c, :], pf1[:, 512:1024], ALU.mult,
               [xc] + pq("pf1", 512, 1024), ["xn"])

    S.fence(col_s[:, 7:8])
    if debug:
        dxn = nc.dram_tensor("dxn", [D, T], BF16, kind="ExternalOutput").ap()
        dma(dxn.rearrange("(c p) t -> p c t", p=128), xn[:], ["xn"], ["dxn"])
    for ei in range(NE):
        for qtr in range(8):
            dma(mwst[:].rearrange("p (a o) -> p a o", a=1),
                exp_w[ei].rearrange("(a p) o -> p a o", p=128)[:, qtr:qtr + 1, :],
                ["mwst_r"], ["mwst"], q="gp")
            cp(mwbf[:], mwst[:], ["mwst", "mwbf_r"], ["mwbf", "mwst_r"])
            dma(moew[ei].rearrange("(a p) o -> p a o", p=128)[:, qtr:qtr + 1, :],
                mwbf[:].rearrange("p (a o) -> p a o", a=1), ["mwbf"], ["mwbf_r"], q="gp")

    wsrc = [g_wq, g_wk, g_wv, g_wf]
    for h in range(GH):
        hc = GD * h
        for wi in range(4):
            for sl in range(4):
                dma(wst[:].rearrange("p (a o) -> p a o", a=2),
                    wsrc[wi].rearrange("(a p) o -> p a o", p=128)
                    [:, 2 * sl:2 * sl + 2, hc:hc + GD], ["wstr"], ["wstw"], q="gp")
                cp(wgb[:, wi, 2 * sl:2 * sl + 2, :].rearrange("p a o -> p (a o)"),
                   wst[:], ["wstw"], ["wgb%d" % wi, "wstr"])
        for (wi, dst, dname, actf) in ((0, qfm, "qfm", AF.Silu), (1, kfm, "kfm", AF.Silu)):
            for ofc in range(2):
                for tb in range(8):
                    t0 = tb * 512
                    pt_t = ("pf0" if tb % 2 == 0 else "pf1")
                    dstp = (pf0 if tb % 2 == 0 else pf1)[:, 512 * ofc:512 * (ofc + 1)]
                    toks = pq(pt_t, 512 * ofc, 512 * (ofc + 1))
                    for ifc in range(8):
                        mm(dstp, wgb[:, wi, ifc, 128 * ofc:128 * (ofc + 1)],
                           xn[:, ifc, t0:t0 + 512], ["wgb%d" % wi, "xn"], toks,
                           start=(ifc == 0), stop=(ifc == 7))
                    act(dst[:, ofc, t0:t0 + 512], dstp, actf, toks, [dname])
        for ofc in range(2):
            for tb in range(8):
                t0 = tb * 512
                pt_t = ("pf0" if tb % 2 == 0 else "pf1")
                dstp = (pf0 if tb % 2 == 0 else pf1)[:, 512 * ofc:512 * (ofc + 1)]
                toks = pq(pt_t, 512 * ofc, 512 * (ofc + 1))
                for ifc in range(8):
                    mm(dstp, wgb[:, 3, ifc, 128 * ofc:128 * (ofc + 1)],
                       xn[:, ifc, t0:t0 + 512], ["wgb3", "xn"], toks,
                       start=(ifc == 0), stop=(ifc == 7))
                act(sgst[:], dstp, AF.Exp, toks, ["sgst"], scale=-1.0)
                act(ffm[:, ofc, t0:t0 + 512], sgst[:], AF.Ln, ["sgst"], ["ffm"], bias=1.0)
        for (wi, dst, dname) in ((1, ktm, "ktm"), (2, vtm, "vtm")):
            for tt_ in range(32):
                t0 = tt_ * 128
                pt_t = ("pf0" if tt_ % 2 == 0 else "pf1")
                dstp = (pf0 if tt_ % 2 == 0 else pf1)[:, 0:256]
                toks = pq(pt_t, 0, 256)
                for ifc in range(8):
                    mm(dstp, xn[:, ifc, t0:t0 + 128], wgb[:, wi, ifc, :],
                       ["wgb%d" % wi, "xn"], toks, start=(ifc == 0), stop=(ifc == 7))
                act(dst[:, tt_, :], dstp, AF.Silu, toks, [dname])

        mset(s1_s[:].rearrange("p a b -> p (a b)"), 0.0, [], ["s1"])
        mset(s2_s[:].rearrange("p a b -> p (a b)"), 0.0, [], ["s2"])

        def st_ftm(cc):
            t0 = cc * CH
            for th in range(2):
                for mc in range(2):
                    tr(pb0[:, 128 * (2 * th + mc):128 * (2 * th + mc + 1)],
                       ffm[:, mc, t0 + 128 * th:t0 + 128 * (th + 1)],
                       ["ffm"] + pq("pb0", 0, 1), pq("pb0", 0, 1), identb_s[:])
                cp(ftm_s[:, th, :], pb0[:, 256 * th:256 * (th + 1)],
                   pq("pb0", 0, 1), ["ftm"])

        def st_front(cc):
            t0 = cc * CH
            for mc in range(2):
                for th in range(2):
                    mm(pf0[:, 256 * mc:256 * (mc + 1)], ftm_s[:, th, 128 * mc:128 * (mc + 1)],
                       ul_s[:, th, :], ["ftm", "ul"], pq("pf0", 256 * mc, 256 * (mc + 1)),
                       start=(th == 0), stop=(th == 1))
            for tc in range(2):
                for dkc in range(2):
                    mm(pf1[:, 512 + 256 * tc:512 + 256 * (tc + 1)],
                       kfm[:, dkc, t0 + 128 * tc:t0 + 128 * (tc + 1)],
                       qfm[:, dkc, t0:t0 + 256], ["kfm", "qfm"],
                       pq("pf1", 512 + 256 * tc, 768 + 256 * tc),
                       start=(dkc == 0), stop=(dkc == 1))
                tt(a1_s[:, tc, :], pf1[:, 512 + 256 * tc:512 + 256 * (tc + 1)],
                   ulf_s[:, tc, :], ALU.mult,
                   pq("pf1", 512 + 256 * tc, 768 + 256 * tc) + ["ulf"], ["a1"])
            act(enl_s[:], pf0[:, 0:512], AF.Exp, pq("pf0", 0, 512), ["enl"], scale=0.125)
            act(gcc2[:, 2 * (cc % 2):2 * (cc % 2) + 2], pf0[:, 255:512:256], AF.Exp,
                pq("pf0", 0, 512), ["gcc%d" % (cc % 2)], scale=-0.125)
            for mc in range(2):
                act(gks_s[:, 256 * mc:256 * (mc + 1)], ffm[:, mc, t0:t0 + 256],
                    AF.Exp, ["ffm"], ["gks"], scale=-0.125)
            tt(ssm_s[:], gks_s[:], enl_s[:], ALU.mult, ["gks", "enl"], ["ssm"])
            tt(sfm_s[:], enl_s[:], ssm_s[:], ALU.subtract, ["enl", "ssm"], ["sfm"])
            mm(pf0[:, 512:768], tri_s[:], ftm_s[:, 0, :], ["tri", "ftm"], pq("pf0", 512, 768))
            mm(pf0[:, 768:1024], onesb_s[:], ftm_s[:, 0, :], ["onesb", "ftm"],
               pq("pf0", 768, 1024), start=True, stop=False)
            mm(pf0[:, 768:1024], tri_s[:], ftm_s[:, 1, :], ["tri", "ftm"],
               pq("pf0", 768, 1024), start=False, stop=True)
            act(gtm_s[:].rearrange("p a b -> p (a b)"), pf0[:, 512:1024], AF.Exp,
                pq("pf0", 512, 1024), ["gtm"], scale=-0.125)
            for th in range(2):
                mm(pf1[0:1, 0:256], onecol_s[:], ftm_s[:, th, :], ["onecol", "ftm"],
                   pq("pf1", 0, 256), start=(th == 0), stop=(th == 1))
            act(gcr_s[:], pf1[0:1, 0:256], AF.Exp, pq("pf1", 0, 256), ["gcr"], scale=-0.125)
            mm(pf1[:, 256:512], onerowb_s[:], gcr_s[:], ["onerowb", "gcr"], pq("pf1", 256, 512))
            cp(gcbs[:, cc % 2, :], pf1[:, 256:512], pq("pf1", 256, 512), ["gcb%d" % (cc % 2)])
            for th in range(2):
                for mc in range(2):
                    tr(pb0[:, 512 + 128 * (2 * th + mc):512 + 128 * (2 * th + mc + 1)],
                       sfm_s[:, 256 * mc + 128 * th:256 * mc + 128 * (th + 1)],
                       ["sfm"] + pq("pb0", 0, 1), pq("pb0", 0, 1), identb_s[:])
                cp(stm2[:, cc % 2, th, :], pb0[:, 512 + 256 * th:512 + 256 * (th + 1)],
                   pq("pb0", 0, 1), ["stm%d" % (cc % 2)])

        def st_ok(cc):
            t0 = cc * CH
            for th in range(2):
                pok = pf2[:, 256 * th:256 * (th + 1)]
                ptoks = pq("pf2", 256 * th, 256 * (th + 1))
                for dkc in range(2):
                    mm(pok, qfm[:, dkc, t0 + 128 * th:t0 + 128 * (th + 1)],
                       s1_s[:, dkc, :], ["qfm", "s1"], ptoks,
                       start=(dkc == 0), stop=False)
                for tc in range(2):
                    mm(pok, a1_s[:, tc, 128 * th:128 * (th + 1)], stm2[:, cc % 2, tc, :],
                       ["a1", "stm%d" % (cc % 2)], ptoks, start=False, stop=(tc == 1))
                tt(oks_s[:], pok, gtm_s[:, th, :], ALU.mult, ptoks + ["gtm"], ["oks"])
                rmax(col_s[:, 0:1], oks_s[:], ["oks"], ["nmx"], negate=True)
                act(eok_s[:], oks_s[:], AF.Exp, ["oks", "nmx"], ["eok"], bias=col_s[:, 0:1])
                rsum(col_s[:, 1:2], eok_s[:], ["eok"], ["smx"])
                recip(col_s[:, 2:3], col_s[:, 1:2], ["smx"], ["rsx"])
                stt(qvt_s[:, th, :], eok_s[:], col_s[:, 2:3], gtm_s[:, th, :],
                    ALU.mult, ALU.mult, ["eok", "rsx", "gtm"], ["qvt"])
            for mc in range(2):
                for th in range(2):
                    tr(pb1[:, 128 * (2 * mc + th):128 * (2 * mc + th + 1)],
                       qvt_s[:, th, 128 * mc:128 * (mc + 1)],
                       ["qvt"] + pq("pb1", 0, 1), pq("pb1", 0, 1), identb_s[:])
                cp(qvf_s[:, mc, :], pb1[:, 256 * mc:256 * (mc + 1)],
                   pq("pb1", 0, 1), ["qvf"])
            for tc in range(2):
                for mc in range(2):
                    mm(pf2[:, 512 + 256 * tc:512 + 256 * (tc + 1)],
                       sfm_s[:, 256 * mc + 128 * tc:256 * mc + 128 * (tc + 1)],
                       qvf_s[:, mc, :], ["sfm", "qvf"],
                       pq("pf2", 512 + 256 * tc, 768 + 256 * tc),
                       start=(mc == 0), stop=(mc == 1))
                tt(a2_s[:, tc, :], pf2[:, 512 + 256 * tc:512 + 256 * (tc + 1)],
                   ulf_s[:, tc, :], ALU.mult,
                   pq("pf2", 512 + 256 * tc, 768 + 256 * tc) + ["ulf"], ["a2"])

        def st_back(cc):
            t0 = cc * CH
            for vc in range(2):
                pov = pf0[:, 256 * vc:256 * (vc + 1)]
                ptoks = pq("pf0", 256 * vc, 256 * (vc + 1))
                for mc in range(2):
                    mm(pov, s2_s[:, mc, 128 * vc:128 * (vc + 1)], qvf_s[:, mc, :],
                       ["s2", "qvf"], ptoks, start=(mc == 0), stop=False)
                for tc in range(2):
                    mm(pov, vtm[:, 2 * cc + tc, 128 * vc:128 * (vc + 1)], a2_s[:, tc, :],
                       ["vtm", "a2"], ptoks, start=False, stop=(tc == 1))
                cp(ovbb[:, vc, :], pov, ptoks, ["ovb%d" % vc])
                dma(o_s[hc + 128 * vc:hc + 128 * (vc + 1), t0:t0 + CH], ovbb[:, vc, :],
                    ["ovb%d" % vc], ["o_sd"], q="gp")
            for dkc in range(2):
                pc1 = pf0[:, 512 + 256 * dkc:512 + 256 * (dkc + 1)]
                ptoks = pq("pf0", 512 + 256 * dkc, 768 + 256 * dkc)
                for tc in range(2):
                    mm(pc1, ktm[:, 2 * cc + tc, 128 * dkc:128 * (dkc + 1)],
                       stm2[:, cc % 2, tc, :], ["ktm", "stm%d" % (cc % 2)], ptoks,
                       start=(tc == 0), stop=(tc == 1))
            for mc in range(2):
                pc2 = pf1[:, 512 + 256 * mc:512 + 256 * (mc + 1)]
                ptoks = pq("pf1", 512 + 256 * mc, 768 + 256 * mc)
                for tc in range(2):
                    mm(pc2, stm2[:, cc % 2, tc, 128 * mc:128 * (mc + 1)],
                       vtm[:, 2 * cc + tc, :], ["stm%d" % (cc % 2), "vtm"], ptoks,
                       start=(tc == 0), stop=(tc == 1))
            for dkc in range(2):
                tt(s1_s[:, dkc, :], s1_s[:, dkc, :], pf0[:, 512 + 256 * dkc:512 + 256 * (dkc + 1)],
                   ALU.add, ["s1"] + pq("pf0", 512 + 256 * dkc, 768 + 256 * dkc), ["s1"])
                tt(s1_s[:, dkc, :], s1_s[:, dkc, :], gcbs[:, cc % 2, :], ALU.mult,
                   ["s1", "gcb%d" % (cc % 2)], ["s1"])
            for mc in range(2):
                tt(s2_s[:, mc, :], s2_s[:, mc, :], pf1[:, 512 + 256 * mc:512 + 256 * (mc + 1)],
                   ALU.add, ["s2"] + pq("pf1", 512 + 256 * mc, 768 + 256 * mc), ["s2"])
                ts(s2_s[:, mc, :], s2_s[:, mc, :], gcc2[:, 2 * (cc % 2) + mc:2 * (cc % 2) + mc + 1],
                   ALU.mult, ["s2", "gcc%d" % (cc % 2)], ["s2"])

        st_ftm(0)
        st_front(0)
        for cc in range(NCH):
            st_ok(cc)
            if cc + 1 < NCH:
                st_ftm(cc + 1)
                st_front(cc + 1)
            st_back(cc)
    for sl in range(8):
        dma(wstg[:].rearrange("p (a o) -> p a o", a=1),
            g_wo.rearrange("(a p) o -> p a o", p=128)[:, sl:sl + 1, :],
            ["wstg_r"], ["wstg"], q="gp")
        cp(wot[:, sl, :], wstg[:], ["wstg"], ["wot", "wstg_r"])
    def p1t_load(tb):
        t0 = tb * 512
        bf = tb % 2
        for c in range(8):
            dma(olod[:, bf, c, :], o_s[128 * c:128 * (c + 1), t0:t0 + 512],
                ["o_sd", "ol%d_%d" % (bf, c), "ktm", "wgb1", "wgb2"], ["ol%d_%d" % (bf, c)])
    p1t_load(0)
    for tb in range(8):
        t0 = tb * 512
        bf = tb % 2
        if tb + 1 < 8:
            p1t_load(tb + 1)
        for c in range(8):
            act(osil[:, c, :], olod[:, bf, c, :], AF.Silu, ["ol%d_%d" % (bf, c)], ["osil"])
            tt(osq[:], osil[:, c, :], osil[:, c, :], ALU.mult, ["osil"], ["osq"])
            mm(pf2[0:1, 0:512], onecol_s[:], osq[:], ["onecol", "osq"], pq("pf2", 0, 512),
               start=(c == 0), stop=(c == 7))
        ts(rows[0:1, 0, :], pf2[0:1, 0:512], inv_d, ALU.mult, pq("pf2", 0, 512), ["r0"])
        act(rows[0:1, 1, :], rows[0:1, 0, :], AF.Ln, ["r0"], ["r1"], bias=1e-5)
        act(rows[0:1, 2, :], rows[0:1, 1, :], AF.Exp, ["r1"], ["r2"], scale=-0.5)
        mm(pf1[:, 0:512], onerow_s[:], rows[0:1, 2, :], ["onerow", "r2"], pq("pf1", 0, 512))
        for c in range(8):
            tt(osil[:, c, :], osil[:, c, :], pf1[:, 0:512], ALU.mult,
               ["osil"] + pq("pf1", 0, 512), ["osil"])
            ts(obuf[:, c, :], osil[:, c, :], bias_s[:, 7, c:c + 1], ALU.mult,
               ["osil", "bias"], ["obuf"])
        for ofc in range(8):
            pt_t = "pf0" if ofc % 2 == 0 else "pf2"
            po = (pf0 if ofc % 2 == 0 else pf2)[:, 0:512]
            toks = pq(pt_t, 0, 512)
            for ifc in range(8):
                mm(po, wot[:, ifc, 128 * ofc:128 * (ofc + 1)], obuf[:, ifc, :],
                   ["wot", "obuf"], toks, start=(ifc == 0), stop=(ifc == 7))
            sv_ = sev if ofc % 2 == 0 else sev2
            svtok = "sev%d" % (ofc % 2)
            cp(sv_[:], po, toks + ["sdma%d" % (ofc % 2)], [svtok])
            dma(s_s[128 * ofc:128 * (ofc + 1), t0:t0 + 512], sv_[:], [svtok],
                ["sdma%d" % (ofc % 2), "s_sd"], q="gp")

    S.fence(col_s[:, 7:8])
    for wi, wt_ in ((0, f_wq), (1, f_wk)):
        for sl in range(8):
            dma(p2st[:].rearrange("p (a o) -> p a o", a=1),
                wt_.rearrange("(a p) o -> p a o", p=128)[:, sl:sl + 1, :],
                ["p2str"], ["p2stw"], q="gp")
            cp(wqkb[:, wi, sl, :],
               p2st[:], ["p2stw", "wot", "sev", "sev0", "sev1"],
               ["wqk%d" % wi, "p2str"])
    for sl in range(8):
        dma(p2st[:].rearrange("p (a o) -> p a o", a=1),
            f_wo.rearrange("(a p) o -> p a o", p=128)[:, sl:sl + 1, :],
            ["p2str"], ["p2stw"], q="gp")
        cp(p2bf[:], p2st[:], ["p2stw", "p2bfr"], ["p2bf", "p2str"])
        dma(wfo_sd.rearrange("(a p) o -> p a o", p=128)[:, sl:sl + 1, :],
            p2bf[:].rearrange("p (a o) -> p a o", a=1), ["p2bf"], ["p2bfr"], q="gp")
    def rope_dma(tb):
        b = tb % 2
        t0 = tb * 512
        dma(cosblk[:, b, :], cosi[:, t0:t0 + 512], ["m1"], ["cosb%d" % b])
        dma(sinblk[:, b, :], sini[:, t0:t0 + 512], ["m2"], ["sinb%d" % b])

    def rope_step(tb, ifc):
        b = tb % 2
        t0 = tb * 512
        rq_ = rqb if b == 0 else rqb2
        par = ifc % 2
        pp = pf0[:, 512 * par:512 * (par + 1)]
        toks = pq("pf0", 512 * par, 512 * (par + 1))
        mm(pp, pswap_s[:], xn[:, ifc, t0:t0 + 512], ["pswap", "xn"], toks)
        cp(swpb[:], pp, toks, ["swp"])
        tt(m1b[:], xn[:, ifc, t0:t0 + 512], cosblk[:, b, :], ALU.mult,
           ["xn", "cosb%d" % b], ["m1"])
        tt(m2b[:], swpb[:], sinblk[:, b, :], ALU.mult, ["swp", "sinb%d" % b], ["m2"])
        tt(rq_[:, ifc, :], m1b[:], m2b[:], ALU.add, ["m1", "m2"], ["rqb%d" % b])

    rope_dma(0)
    for ifc in range(8):
        rope_step(0, ifc)
    for tb in range(8):
        t0 = tb * 512
        b = tb % 2
        rq_ = rqb if b == 0 else rqb2
        if tb + 1 < 8:
            rope_dma(tb + 1)
        for wi, dsts, bslot, scl in ((0, q_s, 0, 0.125), (1, k_s, 1, 1.0)):
            for ofc in range(8):
                par = ofc % 2
                pp = pf1[:, 512 * par:512 * (par + 1)]
                toks = pq("pf1", 512 * par, 512 * (par + 1))
                for ifc in range(8):
                    mm(pp, wqkb[:, wi, ifc, 128 * ofc:128 * (ofc + 1)], rq_[:, ifc, :],
                       ["wqk%d" % wi, "rqb%d" % b], toks, start=(ifc == 0), stop=(ifc == 7))
                qb_ = qkbo if ofc % 2 == 0 else qkbo2
                qtok = "qkbo%d" % (ofc % 2)
                ts2(qb_[:], pp, bias_s[:, bslot, ofc:ofc + 1], scl, ALU.add, ALU.mult,
                    toks + ["bias", "qkdma%d" % (ofc % 2)], [qtok])
                dma(dsts[128 * ofc:128 * (ofc + 1), t0:t0 + 512], qb_[:], [qtok],
                    ["qkdma%d" % (ofc % 2), "qkdma"])
                if wi == 0 and tb + 1 < 8:
                    rope_step(tb + 1, ofc)

    for sl in range(8):
        dma(p2st[:].rearrange("p (a o) -> p a o", a=1),
            f_wv.rearrange("(a p) o -> p a o", p=128)[:, sl:sl + 1, :],
            ["p2str"], ["p2stw"], q="gp")
        cp(wqkb[:, 0, sl, :],
           p2st[:], ["p2stw", "wqk0"], ["wqk0", "p2str"])
    mset(vhat[:, :, :, 64:65], 1.0, ["qfm", "kfm", "ffm", "vtm"], ["vhat"])
    mset(vhat[:, :, :, 129:130], 1.0, ["vhat"], ["vhat"])
    for tg in range(32):
        tt0 = tg * 128
        for hf in range(2):
            ppv = pf2[:, 512 * hf:512 * (hf + 1)]
            toks = pq("pf2", 512 * hf, 512 * (hf + 1))
            for ifc in range(8):
                mm(ppv, xn[:, ifc, tt0:tt0 + 128], wqkb[:, 0, ifc, 512 * hf:512 * (hf + 1)],
                   ["wqk0", "xn"], toks, start=(ifc == 0), stop=False)
            mm(ppv, onerowb_s[0:1, :], bvrow_s[0:1, 512 * hf:512 * (hf + 1)],
               ["onerowb", "bvrow"], toks, start=False, stop=True)
        for p_ in range(8):
            base = 128 * p_
            hf = 1 if base >= 512 else 0
            col = base - 512 * hf
            ppv = pf2[:, 512 * hf:512 * (hf + 1)]
            toks = pq("pf2", 512 * hf, 512 * (hf + 1))
            cp(vhat[:, tg, p_, 0:64], ppv[:, col:col + 64], toks + ["vhat"], ["vhat"])
            cp(vhat[:, tg, p_, 65:129], ppv[:, col + 64:col + 128], toks + ["vhat"], ["vhat"])

    S.fence(col_s[:, 7:8])
    for sl in range(2):
        dma(wfob[:].rearrange("p a b -> p a b")[:, 4 * sl:4 * sl + 4, :],
            wfo_sd.rearrange("(a p) o -> p a o", p=128)[:, 4 * sl:4 * sl + 4, :],
            ["p2bfr", "wqk0", "wqk1", "rqb", "swp", "m1", "m2", "qkbo"], ["wfob"])
    for qt in range(NQT):
        q0 = qt * 512
        nkt = 4 * (qt + 1)
        dma(qbuf[:], q_s.rearrange("(pp p) t -> p pp t", p=128)[:, :, q0:q0 + 512],
            ["qkdma", "qbr"], ["qbw"])
        for pp_ in range(NP):
            kpar = pp_ % 2
            dma(kbuf[:, kpar, 0:512 * (qt + 1)], k_s[128 * pp_:128 * (pp_ + 1), 0:512 * (qt + 1)],
                ["kb%d" % kpar, "qkdma"], ["kbw%d" % kpar], q="gp")
            def emit_ctx(kt, nkt=nkt, pp_=pp_):
                par = kt % 2
                mm(pf2[0:65, 0:512], vhat[:, kt, pp_, 0:65], est[:, par, 0:512],
                   ["vhat", "est%d" % par], pq("pf2", 0, 512),
                   start=(kt == 0), stop=(kt == nkt - 1))
                mm(pf2[0:65, 512:1024], vhat[:, kt, pp_, 65:130], est[:, par, 512:1024],
                   ["vhat", "est%d" % par], pq("pf2", 512, 1024),
                   start=(kt == 0), stop=(kt == nkt - 1))
            for kt in range(nkt):
                par = kt % 2
                psc = pf0 if par == 0 else pf1
                ptn = "pf0" if par == 0 else "pf1"
                toksA = pq(ptn, 0, 512)
                toksB = pq(ptn, 512, 1024)
                mm(psc[:, 0:512], kbuf[0:64, kpar, 128 * kt:128 * (kt + 1)],
                   qbuf[0:64, pp_, :], ["kbw%d" % kpar, "qbw"], toksA, tp=(0, 0))
                mm(psc[:, 512:1024], kbuf[64:128, kpar, 128 * kt:128 * (kt + 1)],
                   qbuf[64:128, pp_, :], ["kbw%d" % kpar, "qbw"], toksB, tp=(64, 0))
                act(est[:, par, :], psc[:, 0:1024], AF.Exp,
                    toksA + toksB + ["est%d" % par], ["est%d" % par], bias=EXPB)
                di = kt - 4 * qt
                if di >= 0:
                    for hh in range(2):
                        base = 512 * hh
                        if di > 0:
                            mset(est[:, par, base:base + 128 * di], 0.0,
                                 ["est%d" % par], ["est%d" % par])
                        tt(est[:, par, base + 128 * di:base + 128 * (di + 1)],
                           est[:, par, base + 128 * di:base + 128 * (di + 1)],
                           tri_s[:], ALU.mult, ["est%d" % par, "tri"], ["est%d" % par])
                if kt > 0:
                    emit_ctx(kt - 1)
            emit_ctx(nkt - 1)
            act(rows[0:1, 0, :], pf2[64:65, 0:512], AF.Ln, pq("pf2", 0, 512) + ["lnz"],
                ["lnz"])
            act(zr_s[:], rows[0:1, 0, :], AF.Exp, ["lnz"], ["zr"], scale=-1.0)
            mm(pf0[0:64, 0:512], onerowb_s[0:1, 0:64], zr_s[:], ["onerowb", "zr"],
               pq("pf0", 0, 512))
            cp(zb_s[:], pf0[0:64, 0:512], pq("pf0", 0, 512), ["zb"])
            tt(ctxf[0:64, pp_, :], pf2[0:64, 0:512], zb_s[:], ALU.mult,
               pq("pf2", 0, 512) + ["zb", "ctxf%d" % pp_], ["ctxf%d" % pp_])
            act(rows[0:1, 0, :], pf2[64:65, 512:1024], AF.Ln, pq("pf2", 512, 1024) + ["lnz"],
                ["lnz"])
            act(zr_s[:], rows[0:1, 0, :], AF.Exp, ["lnz"], ["zr"], scale=-1.0)
            mm(pf0[0:64, 0:512], onerowb_s[0:1, 0:64], zr_s[:], ["onerowb", "zr"],
               pq("pf0", 0, 512))
            cp(zb_s[:], pf0[0:64, 0:512], pq("pf0", 0, 512), ["zb"])
            tt(ctxf[64:128, pp_, :], pf2[0:64, 512:1024], zb_s[:], ALU.mult,
               pq("pf2", 512, 1024) + ["zb", "ctxf%d" % pp_], ["ctxf%d" % pp_])
        for ofc in range(8):
            par = ofc % 2
            po = (pf0 if par == 0 else pf1)[:, 0:512]
            ptn = "pf0" if par == 0 else "pf1"
            toks = pq(ptn, 0, 512)
            for pp_ in range(NP):
                mm(po, wfob[:, pp_, 128 * ofc:128 * (ofc + 1)], ctxf[:, pp_, :],
                   ["wfob", "ctxf%d" % pp_], toks,
                   start=(pp_ == 0), stop=(pp_ == 7))
            abv = aob[:] if ofc % 2 == 0 else est[:, 1, 0:512]
            abtok = "aob%d" % (ofc % 2)
            cp(abv, po, toks + ["adma%d" % (ofc % 2)] + ([] if ofc % 2 == 0 else ["est1"]),
               [abtok] + ([] if ofc % 2 == 0 else ["est1"]))
            dma(ao_s[128 * ofc:128 * (ofc + 1), q0:q0 + 512], abv, [abtok],
                ["adma%d" % (ofc % 2), "aod"])

    S.fence(col_s[:, 7:8])
    for qt in range(NQT):
        q0 = qt * 512
        dma(xti[:], xt.rearrange("(c p) t -> p c t", p=128)[:, :, q0:q0 + 512],
            ["xtir", "kbw0", "kbw1", "qbw", "est0", "est1"] if qt == 0 else ["xtir"],
            ["xtiw"])
        dma(aol[:], ao_s.rearrange("(c p) t -> p c t", p=128)[:, :, q0:q0 + 512],
            ["aod", "aolr"], ["aolw"], q="gp")
        for c in range(8):
            stt(ybf[:, c, :], aol[:, c, :], bias_s[:, 2, c:c + 1], xti[:, c, :],
                ALU.add, ALU.add, ["aolw", "xtiw", "bias"], ["ybf", "xtir", "aolr"])
        for c in range(8):
            tt(ysq[:], ybf[:, c, :], ybf[:, c, :], ALU.mult, ["ybf"], ["ysq"])
            mm(pf2[0:1, 0:512], onecol_s[:], ybf[:, c, :], ["onecol", "ybf"],
               pq("pf2", 0, 512), start=(c == 0), stop=(c == 7))
            mm(pf2[0:1, 512:1024], onecol_s[:], ysq[:], ["onecol", "ysq"],
               pq("pf2", 512, 1024), start=(c == 0), stop=(c == 7))
        ts(rows[0:1, 0, :], pf2[0:1, 0:512], inv_d, ALU.mult, pq("pf2", 0, 512), ["r0"])
        ts(rows[0:1, 1, :], pf2[0:1, 512:1024], inv_d, ALU.mult, pq("pf2", 512, 1024), ["r1"])
        tt(rows[0:1, 2, :], rows[0:1, 0, :], rows[0:1, 0, :], ALU.mult, ["r0"], ["r2"])
        tt(rows[0:1, 3, :], rows[0:1, 1, :], rows[0:1, 2, :], ALU.subtract, ["r1", "r2"], ["r3"])
        act(rows[0:1, 4, :], rows[0:1, 3, :], AF.Ln, ["r3"], ["r4"], bias=1e-5)
        act(rows[0:1, 5, :], rows[0:1, 4, :], AF.Exp, ["r4"], ["r5"], scale=-0.5)
        mm(pf1[:, 0:512], onerow_s[:], rows[0:1, 0, :], ["onerow", "r0"], pq("pf1", 0, 512))
        mm(pf1[:, 512:1024], onerow_s[:], rows[0:1, 5, :], ["onerow", "r5"],
           pq("pf1", 512, 1024))
        for c in range(8):
            tt(ybf[:, c, :], ybf[:, c, :], pf1[:, 0:512], ALU.subtract,
               ["ybf"] + pq("pf1", 0, 512), ["ybf"])
            tt(ybf[:, c, :], ybf[:, c, :], pf1[:, 512:1024], ALU.mult,
               ["ybf"] + pq("pf1", 512, 1024), ["ybf"])
            ts2(x1[:, c, q0:q0 + 512], ybf[:, c, :], bias_s[:, 3, c:c + 1],
                bias_s[:, 4, c:c + 1], ALU.mult, ALU.add, ["ybf", "bias", "xn"], ["x1"])
        for sub in range(4):
            st0 = q0 + 128 * sub
            tile = qt * 4 + sub
            for ifc in range(8):
                mm(pf0[0:128, 0:8], x1[:, ifc, st0:st0 + 128], gwt_s[:, ifc, :],
                   ["x1", "gwt"], pq("pf0", 0, 256), start=(ifc == 0), stop=False)
            mm(pf0[0:128, 0:8], onerowb_s[0:1, :], gbrow_s[:], ["onerowb", "gbrow"],
               pq("pf0", 0, 256), start=False, stop=True)
            rmax(col_s[:, 0:1], pf0[0:128, 0:8], pq("pf0", 0, 256), ["nmx"], negate=True)
            act(g8_s[:, 0, :], pf0[0:128, 0:8], AF.Exp, pq("pf0", 0, 256) + ["nmx", "g80"],
                ["g80"], bias=col_s[:, 0:1])
            rsum(col_s[:, 1:2], g8_s[:, 0, :], ["g80"], ["smx"])
            recip(col_s[:, 2:3], col_s[:, 1:2], ["smx"], ["rsx"])
            ts(g8_s[:, 1, :], g8_s[:, 0, :], col_s[:, 2:3], ALU.mult, ["g80", "rsx"], ["g81"])
            rmax(col_s[:, 3:4], g8_s[:, 1, :], ["g81"], ["m1x"])
            ts(g8_s[:, 2, :], g8_s[:, 1, :], col_s[:, 3:4], ALU.is_ge, ["g81", "m1x"], ["g82"])
            stt(g8_s[:, 3, :], g8_s[:, 2, :], -2.0, g8_s[:, 1, :], ALU.mult, ALU.add,
                ["g82", "g81"], ["g83"])
            rmax(col_s[:, 4:5], g8_s[:, 3, :], ["g83"], ["m2x"])
            ts(g8_s[:, 2, :], g8_s[:, 1, :], col_s[:, 4:5], ALU.is_ge, ["g81", "m2x"], ["g82"])
            tt(wbuf_s[:, tile, :], g8_s[:, 1, :], g8_s[:, 2, :], ALU.mult,
               ["g81", "g82"], ["wbuf"])

    if debug:
        dx1 = nc.dram_tensor("dx1", [D, T], BF16, kind="ExternalOutput").ap()
        dma(dx1.rearrange("(c p) t -> p c t", p=128), x1[:], ["x1"], ["dx1"])
        dwb = nc.dram_tensor("dwb", [128, 32, 8], F32, kind="ExternalOutput").ap()
        dma(dwb, wbuf_s[:], ["wbuf"], ["dwb"])
    S.fence(col_s[:, 7:8])
    mset(macc[:].rearrange("p a b -> p (a b)")[:, 0:16384], 0.0,
         ["xtir", "aolr", "ybf", "ysq"], ["macc"])
    mset(macc[:].rearrange("p a b -> p (a b)")[:, 16384:32768], 0.0, ["macc"], ["macc"])
    for e in range(NE):
        if True:
            wpar = e % 2
            dma(webuf[:, wpar, :, :],
                moew[e].rearrange("(a p) o -> p a o", p=128),
                ["webr%d" % wpar] + (["vhat", "mwbf_r"] if e < 2 else []),
                ["webw%d" % wpar], q="gp")
            for ti in range(32):
                tile = ti
                tt0 = tile * 128
                par = ti % 2
                pe_ = pf0 if par == 0 else pf1
                ptn = "pf0" if par == 0 else "pf1"
                toks = pq(ptn, 0, 1024)
                for ifc in range(8):
                    for hf in range(2):
                        mm(pe_[:, 512 * hf:512 * (hf + 1)], x1[:, ifc, tt0:tt0 + 128],
                           webuf[:, wpar, ifc, 512 * hf:512 * (hf + 1)],
                           ["webw%d" % wpar, "x1"], pq(ptn, 512 * hf, 512 * (hf + 1)),
                           start=(ifc == 0), stop=(ifc == 7))
                act(mfm3[:, 2 * (ti % 2):2 * (ti % 2) + 2, :].rearrange("p a b -> p (a b)"),
                    pe_[:, 0:1024], AF.Copy, toks + ["wbuf", "eow%d" % (ti % 2)],
                    ["eow%d" % (ti % 2)], scale=wbuf_s[:, tile, e:e + 1])
                tt(macc[:, tile, :], macc[:, tile, :],
                   mfm3[:, 2 * (ti % 2):2 * (ti % 2) + 2, :].rearrange("p a b -> p (a b)"),
                   ALU.add, ["macc", "eow%d" % (ti % 2)], ["macc", "webr%d" % wpar])
    for tile in range(32):
        tr(pf2[0:8, 0:128], wbuf_s[:, tile, :], ["wbuf"] + pq("pf2", 0, 256),
           pq("pf2", 0, 256), identf_s[:])
        cp(wts_s[:], pf2[0:8, 0:128], pq("pf2", 0, 256), ["wts"])
        for hf in range(2):
            mm(pf1[:, 512 * hf:512 * (hf + 1)], wts_s[:], expb_s[0:8, 512 * hf:512 * (hf + 1)],
               ["wts", "expb"], pq("pf1", 512 * hf, 1024 * (hf + 1) - 512 * hf) if False else
               pq("pf1", 512 * hf, 512 * (hf + 1)))
        tt(macc[:, tile, :], macc[:, tile, :], pf1[:, 0:1024], ALU.add,
           ["macc"] + pq("pf1", 0, 1024), ["macc"])

    if debug:
        dmacc = nc.dram_tensor("dmacc", [128, 32, KB], BF16, kind="ExternalOutput").ap()
        dma(dmacc, macc[:], ["macc"], ["dmacc"])
    S.fence(col_s[:, 7:8])
    for tb in range(8):
        t0 = tb * 512
        for c in range(8):
            for i4 in range(4):
                tile = tb * 4 + i4
                tr(pb0[:, 128 * i4:128 * (i4 + 1)], macc[:, tile, 128 * c:128 * (c + 1)],
                   ["macc"] + pq("pb0", 128 * i4, 128 * (i4 + 1)),
                   pq("pb0", 128 * i4, 128 * (i4 + 1)), identb_s[:])
            cp(mfm3[:, c, :], pb0[:, 0:512], pq("pb0", 0, 512), ["mfm", "webw0", "webw1"])
            tt(y2b[:, c, :], x1[:, c, t0:t0 + 512], mfm3[:, c, :], ALU.add,
               ["x1", "mfm"], ["y2b"])
        for c in range(8):
            tt(y2sq[:], y2b[:, c, :], y2b[:, c, :], ALU.mult, ["y2b"], ["y2sq"])
            mm(pf2[0:1, 0:512], onecol_s[:], y2b[:, c, :], ["onecol", "y2b"],
               pq("pf2", 0, 512), start=(c == 0), stop=(c == 7))
            mm(pf2[0:1, 512:1024], onecol_s[:], y2sq[:], ["onecol", "y2sq"],
               pq("pf2", 512, 1024), start=(c == 0), stop=(c == 7))
        ts(rows[0:1, 0, :], pf2[0:1, 0:512], inv_d, ALU.mult, pq("pf2", 0, 512), ["r0"])
        ts(rows[0:1, 1, :], pf2[0:1, 512:1024], inv_d, ALU.mult, pq("pf2", 512, 1024), ["r1"])
        tt(rows[0:1, 2, :], rows[0:1, 0, :], rows[0:1, 0, :], ALU.mult, ["r0"], ["r2"])
        tt(rows[0:1, 3, :], rows[0:1, 1, :], rows[0:1, 2, :], ALU.subtract, ["r1", "r2"], ["r3"])
        act(rows[0:1, 4, :], rows[0:1, 3, :], AF.Ln, ["r3"], ["r4"], bias=1e-5)
        act(rows[0:1, 5, :], rows[0:1, 4, :], AF.Exp, ["r4"], ["r5"], scale=-0.5)
        mm(pf1[:, 0:512], onerow_s[:], rows[0:1, 0, :], ["onerow", "r0"], pq("pf1", 0, 512))
        mm(pf1[:, 512:1024], onerow_s[:], rows[0:1, 5, :], ["onerow", "r5"],
           pq("pf1", 512, 1024))
        for c in range(8):
            tt(y2b[:, c, :], y2b[:, c, :], pf1[:, 0:512], ALU.subtract,
               ["y2b"] + pq("pf1", 0, 512), ["y2b"])
            tt(y2b[:, c, :], y2b[:, c, :], pf1[:, 512:1024], ALU.mult,
               ["y2b"] + pq("pf1", 512, 1024), ["y2b"])
            act(gbo[:, c, :], y2b[:, c, :], AF.Silu, ["y2b", "bias"], ["gbo"],
                bias=bias_s[:, 6, c:c + 1], scale=bias_s[:, 5, c:c + 1])
        dma(g_sd.rearrange("(c p) t -> p c t", p=128)[:, :, t0:t0 + 512], gbo[:],
            ["gbo"], ["g_sdw"], q="gp")

    S.fence(col_s[:, 7:8])
    for tb in range(8):
        t0 = tb * 512
        dma(gld[:], g_sd.rearrange("(c p) t -> p c t", p=128)[:, :, t0:t0 + 512],
            ["g_sdw", "gldr", "macc", "mfm"], ["gldw"], q="gp")
        dma(xf5[:], xt.rearrange("(c p) t -> p c t", p=128)[:, :, t0:t0 + 512],
            ["xf5r", "y2b", "y2sq", "gbo"], ["xf5w"])
        dma(sf5[:], s_s.rearrange("(c p) t -> p c t", p=128)[:, :, t0:t0 + 512],
            ["s_sd", "sf5r"], ["sf5w"], q="gp")
        for c in range(8):
            tt(ob5[:, c, :], sf5[:, c, :], gld[:, c, :], ALU.mult,
               ["sf5w", "gldw", "ob5r"], ["ob5", "gldr", "sf5r"])
            tt(ob5[:, c, :], ob5[:, c, :], xf5[:, c, :], ALU.add,
               ["ob5", "xf5w"], ["ob5", "xf5r"])
        dma(outT.rearrange("(c p) t -> p c t", p=128)[:, :, t0:t0 + 512], ob5[:],
            ["ob5"], ["ob5r"])

    S.replay()
    return nc


_NC_CACHE = {}


def kernel(**inputs):
    bf16 = ml_dtypes.bfloat16
    f32 = np.float32
    x = np.asarray(inputs["L0_attn_hidden_states"], dtype=f32)

    def t_(a):
        return np.ascontiguousarray(np.asarray(a, dtype=f32).T)

    def fmv(v):
        return np.ascontiguousarray(np.asarray(v, dtype=f32).reshape(8, 128).T)

    ipw = np.asarray(inputs["L0_ffn_in_proj_w"], dtype=f32)
    ipb = np.asarray(inputs["L0_ffn_in_proj_b"], dtype=f32)
    biasm = np.zeros((128, 16, 8), dtype=f32)
    biasm[:, 0] = fmv(ipb[0:1024])
    biasm[:, 1] = fmv(ipb[1024:2048])
    biasm[:, 2] = fmv(inputs["L0_ffn_out_proj_b"])
    biasm[:, 3] = fmv(inputs["L0_ffn_ln1_w"])
    biasm[:, 4] = fmv(inputs["L0_ffn_ln1_b"])
    biasm[:, 5] = fmv(inputs["L0_ffn_ln2_w"])
    biasm[:, 6] = fmv(inputs["L0_ffn_ln2_b"])
    biasm[:, 7] = fmv(inputs["L0_attn_g_w"])

    tgrid = np.arange(256)[None, :]
    pgrid = np.arange(128)[:, None]
    ul = np.zeros((128, 2, 256), dtype=f32)
    ul[:, 0, :] = (tgrid >= pgrid).astype(f32)
    ul[:, 1, :] = (tgrid >= pgrid + 128).astype(f32)
    tri = np.triu(np.ones((128, 128), dtype=f32))
    j_of_p = (np.arange(128) % 64) // 2
    invf = 10000.0 ** (-2.0 * j_of_p / 64.0)
    tpos = np.arange(T, dtype=np.float64)[None, :]
    ang = tpos * invf[:, None]
    cosi = np.cos(ang).astype(f32)
    sini = np.sin(ang).astype(f32)
    psw = np.zeros((128, 128), dtype=f32)
    for p in range(0, 128, 2):
        psw[p + 1, p] = -1.0
        psw[p, p + 1] = 1.0

    common = {
        "g_wq": t_(inputs["L0_attn_Wq"]), "g_wk": t_(inputs["L0_attn_Wk"]),
        "g_wv": t_(inputs["L0_attn_Wv"]), "g_wf": t_(inputs["L0_attn_Wf"]),
        "g_wo": t_(inputs["L0_attn_Wo"]),
        "f_wq": t_(ipw[0:1024]), "f_wk": t_(ipw[1024:2048]), "f_wv": t_(ipw[2048:3072]),
        "f_wo": t_(inputs["L0_ffn_out_proj_w"]),
        "exp_w": np.ascontiguousarray(
            np.transpose(np.asarray(inputs["L0_ffn_expert_w"], dtype=f32), (0, 2, 1))),
        "biasm": biasm,
        "f_bv": np.asarray(ipb[2048:3072], dtype=f32).reshape(1, D),
        "gate_wt": t_(inputs["L0_ffn_gate_w"]),
        "gate_b": np.asarray(inputs["L0_ffn_gate_b"], dtype=f32).reshape(1, 8),
        "exp_b": np.asarray(inputs["L0_ffn_expert_b"], dtype=f32),
        "identf": np.eye(128, dtype=f32),
        "onecol_b": np.ones((128, 1), dtype=bf16),
        "onerow_f": np.ones((1, 128), dtype=f32),
        "onerow_b": np.ones((1, 128), dtype=bf16),
        "tri_b": tri.astype(bf16),
        "ones_b": np.ones((128, 128), dtype=bf16),
        "ul_b": ul.astype(bf16),
        "ul_f": ul,
        "cosi": cosi.astype(bf16),
        "sini": sini.astype(bf16),
        "pswap": psw.astype(bf16),
        "cvals": np.concatenate([np.full((128, 1), 1e-5, f32),
                                 np.full((128, 1), EXPB, f32)], axis=1),
    }
    in_maps = []
    for b in range(8):
        m = dict(common)
        m["xt"] = np.ascontiguousarray(x[b].T)
        in_maps.append(m)

    if "nc" not in _NC_CACHE:
        _NC_CACHE["nc"] = build_nc()
    nc = _NC_CACHE["nc"]
    res = run_bass_kernel_spmd(nc, in_maps, core_ids=list(range(8)))
    out = np.stack([np.asarray(r["outT"], dtype=f32).T for r in res.results], axis=0)
    return out


if __name__ == "__main__":
    nc = build_nc()
    print("built ok; instructions:", nc.n_instructions() if hasattr(nc, "n_instructions") else "?")
